# version 04
# baseline (speedup 1.0000x reference)
import numpy as np
import ml_dtypes
import concourse.bass as bass
import concourse.mybir as mybir
from concourse.bass_utils import run_bass_kernel_spmd

F32 = mybir.dt.float32
BF16 = mybir.dt.bfloat16
AF = mybir.ActivationFunctionType
ALU = mybir.AluOpType
AX = mybir.AxisListType
bf16 = ml_dtypes.bfloat16

T = 2048
D = 1024
C = 8
TC = 16
NPT = 4

ENGS = ("tensor", "scalar", "vector", "sync", "gpsimd")


class Sched:
    def __init__(self, sems):
        self.ops = {e: [] for e in ENGS}
        self.cnt = {n: 0 for n in sems}
        self.sems = sems

    def op(self, eng, fn, inc=None, incval=1):
        if inc is not None:
            self.cnt[inc] += incval
            self.ops[eng].append(("op", fn, self.sems[inc], incval))
            return self.cnt[inc]
        self.ops[eng].append(("op", fn, None, 0))
        return None

    def dma(self, eng, fn, inc):
        return self.op(eng, fn, inc=inc, incval=16)

    def wait(self, eng, sem_name, val):
        if val is None or val <= 0:
            return
        self.ops[eng].append(("wait", self.sems[sem_name], val))

    def cp(self, sem_name):
        return self.cnt[sem_name]

    def run(self, eng_name):
        def body(e):
            for item in self.ops[eng_name]:
                if item[0] == "wait":
                    e.wait_ge(item[1], item[2])
                else:
                    _, fn, sem, incval = item
                    r = fn(e)
                    if sem is not None:
                        r.then_inc(sem, incval)
        return body


def _finish_debug(nc, s, sems, taps):
    outs = {}
    for name, t in taps.items():
        d = nc.dram_tensor("tap_" + name, [t.shape[0], int(np.prod(t.shape[1:]))],
                           t.dtype, kind="ExternalOutput").ap()
        outs[name] = d
    for sem_name in s.cnt:
        if sem_name != "OD":
            s.wait("sync", sem_name, s.cp(sem_name))
    last = None
    for name, t in taps.items():
        last = s.dma("sync", lambda e, t=t, d=outs[name]: e.dma_start(
            out=d, in_=t[:, :]), inc="OD")
    s.wait("sync", "OD", last)
    s.wait("gpsimd", "OD", last)
    with nc.allow_low_precision("debug"), nc.Block() as blk:
        blk.tensor(s.run("tensor"))
        blk.scalar(s.run("scalar"))
        blk.vector(s.run("vector"))
        blk.sync(s.run("sync"))
        blk.gpsimd(s.run("gpsimd"))
    return nc


def build_program(upto=None):
    nc = bass.Bass("TRN2", target_bir_lowering=False)

    x_h = nc.dram_tensor("x", [T, D], F32, kind="ExternalInput").ap()
    wqkv_h = nc.dram_tensor("wqkv", [D, 3 * D], F32, kind="ExternalInput").ap()
    bqk_h = nc.dram_tensor("bqk", [128, 16], F32, kind="ExternalInput").ap()
    bv_h = nc.dram_tensor("bv", [1, D], F32, kind="ExternalInput").ap()
    wrt_h = nc.dram_tensor("wrt", [D, D], F32, kind="ExternalInput").ap()
    wkt_h = nc.dram_tensor("wkt", [D, D], F32, kind="ExternalInput").ap()
    wvt_h = nc.dram_tensor("wvt", [D, D], F32, kind="ExternalInput").ap()
    mu_h = nc.dram_tensor("mu", [128, 16], F32, kind="ExternalInput").ap()
    out_h = nc.dram_tensor("out", [T, D], F32, kind="ExternalOutput").ap()

    SB = nc.sbuf_top
    cur = [nc.sbuf_base]

    def alloc(name, shape, dtype, at=None):
        nb = int(np.prod(shape[1:])) * mybir.dt.size(dtype)
        if at is None:
            at = (cur[0] + 31) // 32 * 32
            cur[0] = at + nb
        return nc.alloc_sbuf_tensor_at(name, shape, dtype, offset=at)

    def addr(t):
        return nc.lookup_mloc(t).addr

    xmain = alloc("xmain", [128, C * T], BF16)
    attnb = alloc("attnb", [128, C * T], BF16)
    vones = alloc("vones", [128, 16 * 1040], BF16)
    scr = {}
    for nm in ("q0", "q1", "k0", "k1"):
        scr[nm] = [alloc(f"scr_{nm}_{i}", [128, T], BF16) for i in range(2)]
    pT = [alloc(f"pT{i}", [128, 1024], BF16) for i in range(NPT)]
    wring = [alloc(f"wring{i}", [128, 1024], BF16) for i in range(4)]
    ub = [alloc(f"ub{i}", [128, 1024], BF16) for i in range(2)]
    ttst = alloc("ttst", [128, 1024], BF16)
    rrow = alloc("rrow", [128, 1024], BF16)
    lnA = alloc("lnA", [128, T], BF16)
    lnB = alloc("lnB", [128, T], BF16)
    lnBM = alloc("lnBM", [128, T], BF16)
    stdb = alloc("stdb", [128, T], BF16)
    meanb = alloc("meanb", [128, T], BF16)
    st_a = alloc("st_a", [128, 512], F32)
    st_b = alloc("st_b", [128, 512], F32)
    st_c = alloc("st_c", [128, 512], F32)
    sAr = alloc("sAr", [128, 512], BF16)
    sBr = alloc("sBr", [128, 512], BF16)
    onesb = alloc("onesb", [128, 128], BF16)
    identf = alloc("identf", [128, 128], F32)
    bqk_s = alloc("bqk_s", [128, 16], F32)
    mu_s = alloc("mu_s", [128, 16], F32)
    mu1m = alloc("mu1m", [128, 16], F32)
    bvrow = alloc("bvrow", [1, D], BF16)
    neg30 = alloc("neg30", [128, 1], F32)
    epsc = alloc("epsc", [128, 1], F32)
    statc = alloc("statc", [128, 64], F32)
    stT = alloc("stT", [64, 128], F32)
    xpc = alloc("xpc", [128, T], BF16)
    tmpc = alloc("tmpc", [128, T], BF16)
    tmpc2 = alloc("tmpc2", [128, T], BF16)
    ostage = alloc("ostage", [128, T], F32)

    a_at = addr(attnb)
    xstage = [nc.alloc_sbuf_tensor_at(f"xstage{i}", [128, 1024], F32,
                                      offset=a_at + i * 4096) for i in range(2)]
    sqstage = [nc.alloc_sbuf_tensor_at(f"sqstage{i}", [128, 1024], F32,
                                       offset=a_at + 8192 + i * 4096)
               for i in range(2)]
    wvfull = nc.alloc_sbuf_tensor_at("wvfull", [128, 8 * 1024], BF16,
                                     offset=a_at + 16384)
    xr_t = nc.alloc_sbuf_tensor_at("xr_t", [128, C * T], BF16,
                                   offset=addr(scr["q0"][0]))
    rowsb = nc.alloc_sbuf_tensor_at("rowsb", [1, 4 * T], BF16,
                                    offset=addr(scr["k0"][0]))
    rsig_t = nc.alloc_sbuf_tensor_at("rsig_t", [128, C * T], BF16,
                                     offset=addr(vones))
    otstage = nc.alloc_sbuf_tensor_at("otstage", [128, T], F32,
                                      offset=addr(vones))
    wslotA = nc.alloc_sbuf_tensor_at("wslotA", [128, 8 * 1024], BF16,
                                     offset=addr(pT[0]))
    wslotB = nc.alloc_sbuf_tensor_at("wslotB", [128, 8 * 1024], BF16,
                                     offset=addr(stdb))
    assert cur[0] <= SB, (cur[0], SB)

    nc.psum_base = 0
    PS = nc.alloc_psum_tensor("PS", [128, 4096], F32)
    psT = [PS[:, 0:1024], PS[:, 1024:2048]]
    psBC = PS[:, 2048:4096]
    slab = [PS[:, 0:1024], PS[:, 1024:2048]]
    pspv = PS[:, 2048:3072]
    psmisc = PS[:, 3072:4096]
    psA = PS[:, 0:2048]
    psB = PS[:, 2048:4096]

    sems = {n: nc.alloc_semaphore(n) for n in
            ("LD0", "LD1", "LDC", "LGC", "RD", "LG0", "LG1", "LG2", "LG3",
             "DPa", "DPb", "TD", "XD", "OD", "W1", "W2",
             "TM", "SC", "VE", "GP")}
    s = Sched(sems)
    ck = {}

    idm = s.op("gpsimd", lambda e: e.memset(identf[:], 0.0), inc="GP")
    s.op("gpsimd", lambda e: e.memset(onesb[:], 1.0))
    s.op("gpsimd", lambda e: e.memset(neg30[:], -30.0))
    s.wait("gpsimd", "GP", idm)
    s.op("gpsimd", lambda e: e.affine_select(
        out=identf[:], in_=identf[:], compare_op=ALU.not_equal, fill=1.0,
        base=0, pattern=[[-1, 128]], channel_multiplier=1))
    for jc in range(16):
        ap = bass.AP(vones, jc * 1040 + 64, [[vones.shape[1], 128], [65, 16]])
        s.op("gpsimd", lambda e, ap=ap: e.memset(ap, 1.0))
    ck["g_consts"] = s.op("gpsimd", lambda e: e.memset(epsc[:], 1e-5), inc="GP")

    s.dma("sync", lambda e: e.dma_start(out=bqk_s[:], in_=bqk_h), inc="LDC")
    ck["ld_consts"] = s.dma("sync", lambda e: e.dma_start(out=mu_s[:], in_=mu_h),
                            inc="LDC")
    s.dma("gpsimd", lambda e: e.dma_start(out=bvrow[:], in_=bv_h), inc="LGC")
    for kc in range(C):
        inap = bass.AP(wqkv_h.tensor, kc * 128 * 3072 + 2048,
                       [[3072, 128], [1, 1024]])
        s.dma("gpsimd", lambda e, i=inap, kc=kc: e.dma_start(
            out=wvfull[:, kc * 1024:(kc + 1) * 1024], in_=i), inc="LGC")
    ck["lg_wv"] = s.cp("LGC")

    s.wait("vector", "LDC", ck["ld_consts"])
    ck["ve_mu"] = s.op("vector", lambda e: e.tensor_scalar(
        mu1m[:], mu_s[:], -1.0, 1.0, ALU.mult, ALU.add), inc="VE")

    xld, trd, evd, sqd = [], [], [], []
    for tc in range(TC):
        b = tc % 2
        lsem = "LD0" if b == 0 else "LD1"
        if tc >= 2:
            s.wait("sync", "TM", trd[tc - 2])
            s.wait("sync", "VE", sqd[tc - 2])
        xld.append(s.dma("sync", lambda e, tc=tc, b=b: e.dma_start(
            out=xstage[b][:], in_=x_h[tc * 128:(tc + 1) * 128, :]), inc=lsem))
        s.wait("vector", lsem, xld[tc])
        sqw = s.op("vector", lambda e, b=b: e.tensor_mul(
            sqstage[b][:], xstage[b][:], xstage[b][:]), inc="VE")
        s.op("vector", lambda e, tc=tc, b=b: e.reduce_sum(
            statc[:, tc:tc + 1], xstage[b][:], axis=AX.X))
        s.wait("vector", "VE", sqw)
        sqd.append(s.op("vector", lambda e, tc=tc, b=b: e.reduce_sum(
            statc[:, 32 + tc:33 + tc], sqstage[b][:], axis=AX.X), inc="VE"))
        s.wait("tensor", lsem, xld[tc])
        if tc == 0:
            s.wait("tensor", "GP", ck["g_consts"])
        if tc >= 2:
            s.wait("tensor", "VE", evd[tc - 2])
        for c in range(C):
            trc = s.op("tensor", lambda e, b=b, c=c: e.transpose(
                psT[b][:, c * 128:(c + 1) * 128],
                xstage[b][:, c * 128:(c + 1) * 128], identf[:]),
                inc="TM" if c == C - 1 else None)
        trd.append(trc)
        s.wait("vector", "TM", trd[tc])
        oap = xmain[:, :].rearrange("p (c t) -> p c t", t=T)[:, :, tc * 128:
                                                            (tc + 1) * 128]
        iap = psT[b][:, :].rearrange("p (c f) -> p c f", f=128)
        evd.append(s.op("vector", lambda e, o=oap, i=iap: e.tensor_copy(o, i),
                        inc="VE"))
    ck["x_in"] = s.cp("VE")
    if upto == "stats_raw":
        return _finish_debug(nc, s, sems, dict(statc=statc))

    s.wait("vector", "VE", ck["x_in"])
    c1 = s.op("vector", lambda e: e.tensor_scalar_mul(
        statc[:, 0:16], statc[:, 0:16], 1.0 / D), inc="VE")
    c2 = s.op("vector", lambda e: e.tensor_scalar_mul(
        statc[:, 32:48], statc[:, 32:48], 1.0 / D), inc="VE")
    s.wait("scalar", "VE", c1)
    m2cp = s.op("scalar", lambda e: e.activation(st_b[:, 0:16], statc[:, 0:16],
                                                 AF.Square), inc="SC")
    s.wait("vector", "SC", m2cp)
    s.wait("vector", "VE", c2)
    ck["ln1_var"] = s.op("vector", lambda e: e.tensor_sub(
        statc[:, 32:48], statc[:, 32:48], st_b[:, 0:16]), inc="VE")
    s.wait("scalar", "GP", ck["g_consts"])
    s.wait("scalar", "VE", ck["ln1_var"])
    lncp = s.op("scalar", lambda e: e.activation(statc[:, 32:48], statc[:, 32:48],
                                                 AF.Ln, bias=epsc[:]), inc="SC")
    s.wait("scalar", "SC", lncp)
    rcp1 = s.op("scalar", lambda e: e.activation(statc[:, 16:32], statc[:, 32:48],
                                                 AF.Exp, scale=-0.5), inc="SC")
    s.wait("scalar", "SC", rcp1)
    ck["ln1_se"] = s.op("scalar", lambda e: e.activation(
        statc[:, 32:48], statc[:, 32:48], AF.Exp, scale=0.5), inc="SC")
    s.wait("vector", "SC", ck["ln1_se"])
    ck["ln1_cols"] = s.op("vector", lambda e: e.scalar_tensor_tensor(
        statc[:, 48:64], statc[:, 0:16], -1.0, statc[:, 16:32],
        ALU.mult, ALU.mult), inc="VE")

    s.wait("tensor", "VE", ck["ln1_cols"])
    ck["ln1_tr"] = s.op("tensor", lambda e: e.transpose(
        psT[0][0:64, 0:128], statc[:, :], identf[:]), inc="TM")
    s.wait("vector", "TM", ck["ln1_tr"])
    ck["ln1_trev"] = s.op("vector", lambda e: e.tensor_copy(
        stT[:], psT[0][0:64, 0:128]), inc="VE")
    s.wait("gpsimd", "VE", ck["ln1_trev"])
    for src_col, ri in ((16, 0), (48, 1), (32, 2), (0, 3)):
        outap = bass.AP(rowsb, ri * T, [[rowsb.shape[1], 1], [1, T]])
        s.dma("gpsimd", lambda e, sc_=src_col, o=outap: e.dma_start(
            out=o, in_=stT[sc_:sc_ + 16, :]), inc="RD")
    ck["ln1_rows"] = s.cp("RD")

    prev_bc_ev = [None]

    def bcast_row(row_off, dst):
        if prev_bc_ev[0] is not None:
            s.wait("tensor", "VE", prev_bc_ev[0])
        for i in range(4):
            cpt = s.op("tensor", lambda e, i=i: e.matmul(
                psBC[:, i * 512:(i + 1) * 512], onesb[0:1, :],
                rowsb[0:1, row_off + i * 512:row_off + (i + 1) * 512],
                start=True, stop=True), inc="TM" if i == 3 else None)
        s.wait("vector", "TM", cpt)
        prev_bc_ev[0] = s.op("vector", lambda e, dst=dst: e.tensor_copy(
            dst[:], psBC[:]), inc="VE")
        return prev_bc_ev[0]

    s.wait("tensor", "RD", ck["ln1_rows"])
    bcast_row(0 * T, lnA)
    bcast_row(1 * T, lnB)
    bcast_row(2 * T, stdb)
    ck["ln1_bcast"] = bcast_row(3 * T, meanb)

    s.wait("vector", "VE", ck["ln1_bcast"])
    for c in range(C):
        sl = slice(c * T, (c + 1) * T)
        mpc = s.op("vector", lambda e, sl=sl: e.tensor_mul(
            xmain[:, sl], xmain[:, sl], lnA[:]), inc="VE")
    s.wait("vector", "VE", mpc)
    for c in range(C):
        sl = slice(c * T, (c + 1) * T)
        apl = s.op("vector", lambda e, sl=sl: e.tensor_add(
            xmain[:, sl], xmain[:, sl], lnB[:]), inc="VE")
    ck["ln1_apply"] = apl
    if upto == "ln1":
        return _finish_debug(nc, s, sems, dict(xmain=xmain, lnA=lnA, lnB=lnB,
                                               stdb=stdb, meanb=meanb,
                                               statc=statc, rowsb=rowsb))

    wload = {}

    def load_w_chunk(m, slot):
        inap = bass.AP(wqkv_h.tensor, m * 128,
                       [[3072, 128], [3072 * 128, 8], [1, 128]])
        wload[m] = (s.dma("gpsimd", lambda e, i=inap, slot=slot: e.dma_start(
            out=wring[slot][:], in_=i), inc=f"LG{slot}"), f"LG{slot}")

    qk_evac = {}
    pair_last_qkt = {}
    prev_misc_user = [None]

    def emit_qk_half(pair, which, half):
        m = pair if which == "q" else 8 + pair
        slot = (2 * pair + (0 if which == "q" else 1)) % 4
        s.wait("tensor", wload[m][1], wload[m][0])
        if pair == 0 and which == "q" and half == 0:
            s.wait("tensor", "VE", ck["ln1_apply"])
        if prev_misc_user[0] is not None:
            s.wait("tensor", "VE", prev_misc_user[0])
        for nt in range(2):
            for kc in range(C):
                cpt = s.op("tensor", lambda e, nt=nt, kc=kc, slot=slot, half=half:
                           e.matmul(psmisc[:, nt * 512:(nt + 1) * 512],
                                    wring[slot][:, kc * 128:(kc + 1) * 128],
                                    xmain[:, kc * T + half * 1024 + nt * 512:
                                          kc * T + half * 1024 + (nt + 1) * 512],
                                    start=(kc == 0), stop=(kc == C - 1)),
                           inc="TM" if (nt == 1 and kc == C - 1) else None)
        s.wait("vector", "TM", cpt)
        if (pair - 2) in pair_last_qkt:
            s.wait("vector", "TM", pair_last_qkt[pair - 2])
        hsl = slice(half * 1024, (half + 1) * 1024)
        e0 = scr[which + "0"][pair % 2]
        e1 = scr[which + "1"][pair % 2]
        s.op("vector", lambda e, e0=e0, hsl=hsl, m=m: e.tensor_scalar(
            e0[0:64, hsl], psmisc[0:64, :], bqk_s[0:64, m:m + 1], None, ALU.add))
        vcp = s.op("vector", lambda e, e1=e1, hsl=hsl, m=m: e.tensor_scalar(
            e1[64:128, hsl], psmisc[64:128, :], bqk_s[64:128, m:m + 1], None,
            ALU.add), inc="VE")
        prev_misc_user[0] = vcp
        if half == 1:
            qk_evac[(pair, which)] = vcp

    dup_cp = {}

    def emit_dups(pair):
        sm = "DPa" if pair % 2 == 0 else "DPb"
        s.wait("gpsimd", "VE", max(qk_evac[(pair, "q")], qk_evac[(pair, "k")]))
        for which in ("q", "k"):
            t0 = scr[which + "0"][pair % 2]
            t1 = scr[which + "1"][pair % 2]
            s.dma("gpsimd", lambda e, t=t0: e.dma_start(out=t[64:128, :],
                                                        in_=t[0:64, :]), inc=sm)
            cpg = s.dma("gpsimd", lambda e, t=t1: e.dma_start(out=t[0:64, :],
                                                              in_=t[64:128, :]),
                        inc=sm)
        dup_cp[pair] = (cpg, sm)

    load_w_chunk(0, 0)
    load_w_chunk(8, 1)
    for which in ("q", "k"):
        for half in range(2):
            emit_qk_half(0, which, half)
    emit_dups(0)
    if upto == "qk0":
        return _finish_debug(nc, s, sems, dict(
            q0=scr["q0"][0], q1=scr["q1"][0], k0=scr["k0"][0], k1=scr["k1"][0]))

    vevac = []
    for jc in range(TC):
        ps = slab[jc % 2]
        if jc == 0:
            s.wait("tensor", "LGC", s.cp("LGC"))
        if jc >= 2:
            s.wait("tensor", "VE", vevac[jc - 2])
        for nt in range(2):
            for kc in range(C):
                s.op("tensor", lambda e, ps=ps, nt=nt, kc=kc, jc=jc: e.matmul(
                    ps[:, nt * 512:(nt + 1) * 512],
                    xmain[:, kc * T + jc * 128:kc * T + (jc + 1) * 128],
                    wvfull[:, kc * 1024 + nt * 512:kc * 1024 + (nt + 1) * 512],
                    start=(kc == 0), stop=False))
            cpt = s.op("tensor", lambda e, ps=ps, nt=nt: e.matmul(
                ps[:, nt * 512:(nt + 1) * 512], onesb[0:1, :],
                bvrow[0:1, nt * 512:(nt + 1) * 512], start=False, stop=True),
                inc="TM" if nt == 1 else None)
        s.wait("vector", "TM", cpt)
        oap = vones[:, jc * 1040:jc * 1040 + 1040].rearrange(
            "p (h f) -> p h f", f=65)[:, :, 0:64]
        iap = ps[:, :].rearrange("p (h f) -> p h f", f=64)
        vevac.append(s.op("vector", lambda e, o=oap, i=iap: e.tensor_copy(o, i),
                          inc="VE"))
    ck["v_done"] = s.cp("VE")
    if upto == "v":
        return _finish_debug(nc, s, sems, dict(vones=vones))

    bg = []
    for pair in range(1, 8):
        bg.append(("loadw", pair, "q"))
        bg.append(("qk", pair, "q", 0))
        bg.append(("qk", pair, "q", 1))
        bg.append(("loadw", pair, "k"))
        bg.append(("qk", pair, "k", 0))
        bg.append(("qk", pair, "k", 1))
        bg.append(("dups", pair))
    bg_i = [0]

    def run_bg(n):
        for _ in range(n):
            if bg_i[0] >= len(bg):
                return
            item = bg[bg_i[0]]
            bg_i[0] += 1
            if item[0] == "loadw":
                _, pair, which = item
                load_w_chunk(pair if which == "q" else 8 + pair,
                             (2 * pair + (0 if which == "q" else 1)) % 4)
            elif item[0] == "qk":
                emit_qk_half(item[1], item[2], item[3])
            else:
                emit_dups(item[1])

    exp_cp, qkt_cp, pv_cp, tt_cp = {}, {}, {}, {}
    stg_dma_cp = {}
    last_ttdma = [0]
    nslab = 0
    dbg_single = upto in ("attn1", "attn1b")
    npair = 1 if dbg_single else 8
    neip = 1 if dbg_single else 2
    nih = 1 if upto == "attn1" else 2
    for pair in range(npair):
        for eip in range(neip):
            h = 2 * pair + eip
            sq0 = scr["q" + str(eip)][pair % 2]
            sk0 = scr["k" + str(eip)][pair % 2]
            if not dbg_single:
                run_bg(0 if h == 0 else (4 if h % 2 == 1 else 3))
            for ihalf in range(nih):
                g = h * 2 + ihalf
                for jc in range(TC):
                    n = nslab
                    nslab += 1
                    sb = n % 2
                    pt = n % NPT
                    if jc == 0 and ihalf == 0:
                        s.wait("tensor", dup_cp[pair][1], dup_cp[pair][0])
                        if h == 0:
                            s.wait("tensor", "VE", ck["v_done"])
                    if n >= 2:
                        s.wait("tensor", "SC", exp_cp[n - 2])
                    s.op("tensor", lambda e, sb=sb, sk0=sk0, sq0=sq0, jc=jc,
                         ihalf=ihalf: e.matmul(
                             slab[sb][:, 0:512],
                             sk0[0:64, jc * 128:(jc + 1) * 128],
                             sq0[0:64, ihalf * 1024:ihalf * 1024 + 512],
                             start=True, stop=True, tile_position=(0, 0)))
                    qkt_cp[n] = s.op("tensor", lambda e, sb=sb, sk0=sk0, sq0=sq0,
                                     jc=jc, ihalf=ihalf: e.matmul(
                                         slab[sb][:, 512:1024],
                                         sk0[64:128, jc * 128:(jc + 1) * 128],
                                         sq0[64:128, ihalf * 1024 + 512:
                                             ihalf * 1024 + 1024],
                                         start=True, stop=True,
                                         tile_position=(64, 0)), inc="TM")
                    s.wait("scalar", "TM", qkt_cp[n])
                    if n >= NPT:
                        s.wait("scalar", "TM", pv_cp[n - NPT])
                    exp_cp[n] = s.op("scalar", lambda e, pt=pt, sb=sb:
                                     e.activation(pT[pt][:], slab[sb][:], AF.Exp,
                                                  bias=neg30[:]), inc="SC")
                    s.wait("tensor", "SC", exp_cp[n])
                    if jc == 0 and g > 0:
                        s.wait("tensor", "VE", tt_cp[g - 1])
                    for it in range(2):
                        pvc = s.op("tensor", lambda e, it=it, jc=jc, h=h, pt=pt:
                                   e.matmul(
                                       pspv[0:65, it * 512:(it + 1) * 512],
                                       vones[:, jc * 1040 + h * 65:
                                             jc * 1040 + h * 65 + 65],
                                       pT[pt][:, it * 512:(it + 1) * 512],
                                       start=(jc == 0), stop=(jc == TC - 1)),
                                   inc="TM" if it == 1 else None)
                    pv_cp[n] = pvc
                s.wait("vector", "TM", pv_cp[nslab - 1])
                ecp = s.op("vector", lambda e, g=g: e.tensor_copy(
                    ub[g % 2][0:65, :], pspv[0:65, :]), inc="VE")
                rcp = s.op("vector", lambda e: e.reciprocal(
                    rrow[64:65, :], pspv[64:65, :]), inc="VE")
                s.wait("tensor", "VE", rcp)
                for it in range(2):
                    bcp = s.op("tensor", lambda e, it=it: e.matmul(
                        pspv[0:64, it * 512:(it + 1) * 512], onesb[64:65, 0:64],
                        rrow[64:65, it * 512:(it + 1) * 512],
                        start=True, stop=True, tile_position=(64, 0)),
                        inc="TM" if it == 1 else None)
                s.wait("vector", "TM", bcp)
                s.wait("vector", "VE", ecp)
                stg = ttst if g % 2 == 0 else rrow
                stsem = "TD" if g % 2 == 0 else "XD"
                if g >= 2:
                    s.wait("vector", stsem, stg_dma_cp.get(g - 2, 0))
                stg_out = bass.AP(stg, 0,
                                  [[1024, 64], [1, 64], [64, 16]])
                tt_cp[g] = s.op("vector", lambda e, o=stg_out, g=g:
                                e.tensor_mul(o, ub[g % 2][0:64, :],
                                             pspv[0:64, :]), inc="VE")
                s.wait("gpsimd", "VE", tt_cp[g])
                band = slice(h * 128 + ihalf * 64, h * 128 + ihalf * 64 + 64)
                for par in range(2):
                    sap = stg[0:64, :].rearrange(
                        "p (qh pr il) -> p qh pr il", pr=2, il=64)[:, :, par, :]
                    dap = attnb[par * 64:(par + 1) * 64, :].rearrange(
                        "p (cc t) -> p cc t", t=T)[:, :, band]
                    dcp = s.dma("gpsimd", lambda e, sap=sap, dap=dap:
                                e.dma_start(out=dap, in_=sap), inc=stsem)
                stg_dma_cp[g] = dcp
            if eip == 1:
                pair_last_qkt[pair] = qkt_cp[nslab - 1]
    if not dbg_single:
        run_bg(100)
    ck["attn_ve"] = s.cp("VE")
    ck["attn_td"] = s.cp("TD")
    ck["attn_xd"] = s.cp("XD")
    ck["attn_sc"] = s.cp("SC")
    ck["attn_tm"] = s.cp("TM")
    if dbg_single or upto == "attn":
        return _finish_debug(nc, s, sems, dict(attnb=attnb, ub0=ub[0]))

    def load_wffn(src_h, dst, sem, wait_tm=None, wait_ve=None, wait_sc=None):
        if wait_tm:
            s.wait("gpsimd", "TM", wait_tm)
        if wait_ve:
            s.wait("gpsimd", "VE", wait_ve)
        if wait_sc:
            s.wait("gpsimd", "SC", wait_sc)
        for kc in range(C):
            inap = bass.AP(src_h.tensor, kc * 128 * D, [[D, 128], [1, 1024]])
            cpl = s.dma("gpsimd", lambda e, i=inap, kc=kc: e.dma_start(
                out=dst[:, kc * 1024:(kc + 1) * 1024], in_=i), inc=sem)
        return (cpl, sem)

    ck["wrt"] = load_wffn(wrt_h, wslotA, "W1", wait_tm=ck["attn_tm"],
                          wait_ve=ck["attn_ve"], wait_sc=ck["attn_sc"])

    def ln_feature_major(src_tensor, extra_ve_wait=None):
        s2cp = None
        for c in range(C):
            sl = slice(c * T, (c + 1) * T)
            buf = tmpc if c % 2 == 0 else xpc
            if extra_ve_wait and c == 0:
                s.wait("vector", "VE", extra_ve_wait)
            if c >= 2:
                s.wait("vector", "TM", s2cp)
            sqv = s.op("vector", lambda e, sl=sl, buf=buf: e.tensor_mul(
                buf[:], src_tensor[:, sl], src_tensor[:, sl]), inc="VE")
            s.wait("tensor", "VE", sqv)
            for i in range(4):
                s.op("tensor", lambda e, c=c, i=i: e.matmul(
                    psA[32 * i:32 * i + 1, 0:512], onesb[:, 0:1],
                    src_tensor[:, c * T + i * 512:c * T + (i + 1) * 512],
                    start=(c == 0), stop=(c == C - 1), tile_position=(0, 32 * i)))
            for i in range(4):
                s2cp = s.op("tensor", lambda e, i=i, buf=buf, c=c: e.matmul(
                    psA[32 * i:32 * i + 1, 512:1024], onesb[:, 0:1],
                    buf[:, i * 512:(i + 1) * 512],
                    start=(c == 0), stop=(c == C - 1), tile_position=(0, 32 * i)),
                    inc="TM" if i == 3 else None)
        s.wait("vector", "TM", s2cp)
        a1 = s.op("vector", lambda e: e.tensor_scalar_mul(st_a[:], psA[:, 0:512],
                                                          1.0 / D), inc="VE")
        a2 = s.op("vector", lambda e: e.tensor_scalar_mul(st_b[:],
                                                          psA[:, 512:1024],
                                                          1.0 / D), inc="VE")
        s.wait("vector", "VE", a1)
        a3 = s.op("vector", lambda e: e.tensor_mul(st_c[:], st_a[:], st_a[:]),
                  inc="VE")
        s.wait("vector", "VE", a3)
        vcp = s.op("vector", lambda e: e.tensor_sub(st_b[:], st_b[:], st_c[:]),
                   inc="VE")
        s.wait("scalar", "VE", vcp)
        for r in (0, 32, 64, 96):
            lcp = s.op("scalar", lambda e, r=r: e.activation(
                st_b[r:r + 1, :], st_b[r:r + 1, :], AF.Ln, bias=epsc[r:r + 1]),
                inc="SC" if r == 96 else None)
        s.wait("scalar", "SC", lcp)
        scp = s.op("scalar", lambda e: e.activation(sAr[:], st_b[:], AF.Exp,
                                                    scale=-0.5), inc="SC")
        s.wait("vector", "SC", scp)
        vcp2 = s.op("vector", lambda e: e.scalar_tensor_tensor(
            sBr[:], st_a[:], -1.0, sAr[:], ALU.mult, ALU.mult), inc="VE")
        s.wait("tensor", "VE", vcp2)
        for i in range(4):
            acp = s.op("tensor", lambda e, i=i: e.matmul(
                psB[:, i * 512:(i + 1) * 512], onesb[32 * i:32 * i + 1, :],
                sAr[32 * i:32 * i + 1, :], start=True, stop=True,
                tile_position=(32 * i, 0)), inc="TM" if i == 3 else None)
        s.wait("vector", "TM", acp)
        va = s.op("vector", lambda e: e.tensor_copy(lnA[:], psB[:]), inc="VE")
        s.wait("tensor", "VE", va)
        for i in range(4):
            bcp = s.op("tensor", lambda e, i=i: e.matmul(
                psB[:, i * 512:(i + 1) * 512], onesb[32 * i:32 * i + 1, :],
                sBr[32 * i:32 * i + 1, :], start=True, stop=True,
                tile_position=(32 * i, 0)), inc="TM" if i == 3 else None)
        s.wait("vector", "TM", bcp)
        return s.op("vector", lambda e: e.tensor_copy(lnB[:], psB[:]), inc="VE")

    s.wait("vector", "VE", ck["attn_ve"])
    s.wait("vector", "TD", ck["attn_td"])
    s.wait("vector", "XD", ck["attn_xd"])
    s.wait("tensor", "SC", ck["attn_sc"])
    ln2 = ln_feature_major(attnb)
    s.wait("vector", "VE", ln2)
    bmcp = s.op("vector", lambda e: e.tensor_add(lnBM[:], lnB[:], meanb[:]),
                inc="VE")
    s.wait("vector", "VE", bmcp)
    for c in range(C):
        sl = slice(c * T, (c + 1) * T)
        p1 = s.op("vector", lambda e, sl=sl: e.tensor_mul(
            attnb[:, sl], attnb[:, sl], lnA[:]), inc="VE")
    s.wait("vector", "VE", p1)
    for c in range(C):
        sl = slice(c * T, (c + 1) * T)
        p2 = s.op("vector", lambda e, sl=sl: e.tensor_add(
            attnb[:, sl], attnb[:, sl], lnBM[:]), inc="VE")
    for c in range(C):
        sl = slice(c * T, (c + 1) * T)
        p3 = s.op("vector", lambda e, sl=sl: e.tensor_mul(
            xmain[:, sl], xmain[:, sl], stdb[:]), inc="VE")
    s.wait("vector", "VE", p3)
    for c in range(C):
        sl = slice(c * T, (c + 1) * T)
        x2cp = s.op("vector", lambda e, sl=sl: e.tensor_add(
            xmain[:, sl], xmain[:, sl], attnb[:, sl]), inc="VE")
    ck["x2"] = x2cp
    if upto == "x2":
        return _finish_debug(nc, s, sems, dict(xmain=xmain))

    ln3 = ln_feature_major(xmain, extra_ve_wait=ck["x2"])
    s.wait("vector", "VE", ln3)
    for c in range(C):
        sl = slice(c * T, (c + 1) * T)
        l3p = s.op("vector", lambda e, sl=sl: e.tensor_mul(
            attnb[:, sl], xmain[:, sl], lnA[:]), inc="VE")
    s.wait("vector", "VE", l3p)
    for c in range(C):
        sl = slice(c * T, (c + 1) * T)
        xn3cp = s.op("vector", lambda e, sl=sl: e.tensor_add(
            attnb[:, sl], attnb[:, sl], lnB[:]), inc="VE")
    ck["xn3"] = xn3cp
    if upto == "xn3":
        return _finish_debug(nc, s, sems, dict(attnb=attnb))

    ck["wkt"] = load_wffn(wkt_h, wslotB, "W2", wait_ve=ck["xn3"])

    def emit_mix(base, dst, wait_ve_first):
        mcps = []
        for c in range(C):
            sl = slice(c * T, (c + 1) * T)
            mc = base + c
            s.wait("gpsimd", "VE", wait_ve_first if c == 0 else mcps[-1])
            gz = s.op("gpsimd", lambda e: e.memset(xpc[:, 0:1], 0.0), inc="GP")
            gcp = s.dma("gpsimd", lambda e, c=c: e.dma_start(
                out=xpc[:, 1:T], in_=attnb[:, c * T:c * T + T - 1]), inc="XD")
            s.wait("vector", "XD", gcp)
            s.wait("vector", "GP", gz)
            tcp = s.op("vector", lambda e, mc=mc: e.tensor_scalar(
                tmpc2[:], xpc[:], mu1m[:, mc:mc + 1], None, ALU.mult), inc="VE")
            s.wait("vector", "VE", tcp)
            mcps.append(s.op("vector", lambda e, sl=sl, mc=mc, dst=dst:
                             e.scalar_tensor_tensor(
                                 dst[:, sl], attnb[:, sl], mu_s[:, mc:mc + 1],
                                 tmpc2[:], ALU.mult, ALU.add), inc="VE"))
        return mcps[-1]

    ck["xr"] = emit_mix(0, xr_t, ck["xn3"])

    def ffn_mm(wslot, src, dc, wait_lg):
        ps = psA if dc % 2 == 0 else psB
        s.wait("tensor", wait_lg[1], wait_lg[0])
        for nt in range(4):
            for kc in range(C):
                cpt = s.op("tensor", lambda e, ps=ps, nt=nt, kc=kc, dc=dc,
                           wslot=wslot, src=src: e.matmul(
                               ps[:, nt * 512:(nt + 1) * 512],
                               wslot[:, kc * 1024 + dc * 128:
                                     kc * 1024 + (dc + 1) * 128],
                               src[:, kc * T + nt * 512:kc * T + (nt + 1) * 512],
                               start=(kc == 0), stop=(kc == C - 1)),
                           inc="TM" if (nt == 3 and kc == C - 1) else None)
        return cpt, ps

    rexp_cp, rmm_cp = {}, {}
    rec_cp = None
    for dc in range(C):
        if dc == 0:
            s.wait("tensor", "VE", ck["xr"])
        if dc >= 2:
            s.wait("tensor", "SC", rexp_cp[dc - 2])
        rmm_cp[dc], ps = ffn_mm(wslotA, xr_t, dc, ck["wrt"])
        s.wait("scalar", "TM", rmm_cp[dc])
        if dc >= 1:
            s.wait("scalar", "VE", rec_cp)
        rexp_cp[dc] = s.op("scalar", lambda e, ps=ps: e.activation(
            ostage[:], ps[:], AF.Exp, scale=-1.0), inc="SC")
        s.wait("vector", "SC", rexp_cp[dc])
        adc = s.op("vector", lambda e: e.tensor_scalar_add(ostage[:], ostage[:],
                                                           1.0), inc="VE")
        s.wait("vector", "VE", adc)
        rec_cp = s.op("vector", lambda e, dc=dc: e.reciprocal(
            rsig_t[:, dc * T:(dc + 1) * T], ostage[:]), inc="VE")
    ck["rsig"] = rec_cp
    ck["rmm_last"] = rmm_cp[C - 1]
    if upto == "rsig":
        return _finish_debug(nc, s, sems, dict(rsig=rsig_t, xr=xr_t))

    ck["wvt"] = load_wffn(wvt_h, wslotA, "W1", wait_tm=ck["rmm_last"])

    s.wait("gpsimd", "TM", ck["rmm_last"])
    ck["xk"] = emit_mix(8, xr_t, ck["xn3"])

    kact_cp = {}
    for dc in range(C):
        if dc == 0:
            s.wait("tensor", "VE", ck["xk"])
            s.wait("tensor", "SC", rexp_cp[C - 1])
        if dc >= 2:
            s.wait("tensor", "SC", kact_cp[dc - 2])
        kcp, ps = ffn_mm(wslotB, xr_t, dc, ck["wkt"])
        s.wait("scalar", "TM", kcp)
        if dc >= 1:
            s.wait("scalar", "SC", kact_cp[dc - 1])
        rl = s.op("scalar", lambda e, ps=ps: e.activation(ostage[:], ps[:],
                                                          AF.Relu), inc="SC")
        s.wait("scalar", "SC", rl)
        kact_cp[dc] = s.op("scalar", lambda e, dc=dc: e.activation(
            attnb[:, dc * T:(dc + 1) * T], ostage[:], AF.Square), inc="SC")
    ck["k2"] = kact_cp[C - 1]
    if upto == "k2":
        return _finish_debug(nc, s, sems, dict(k2=attnb))

    fout_cp = {}
    for dc in range(C):
        if dc == 0:
            s.wait("tensor", "SC", ck["k2"])
            s.wait("tensor", "VE", ck["rsig"])
        if dc >= 2:
            s.wait("tensor", "VE", fout_cp[dc - 2])
        vcp, ps = ffn_mm(wslotA, attnb, dc, ck["wvt"])
        s.wait("vector", "TM", vcp)
        fout_cp[dc] = s.op("vector", lambda e, ps=ps, dc=dc: e.tensor_mul(
            xr_t[:, dc * T:(dc + 1) * T], ps[:],
            rsig_t[:, dc * T:(dc + 1) * T]), inc="VE")
    ck["ffnout"] = fout_cp[C - 1]
    if upto == "ffnout":
        return _finish_debug(nc, s, sems, dict(ffnout=xr_t))

    ln4 = ln_feature_major(xr_t, extra_ve_wait=ck["ffnout"])
    s.wait("vector", "VE", ln4)
    for c in range(C):
        sl = slice(c * T, (c + 1) * T)
        l4p = s.op("vector", lambda e, sl=sl: e.tensor_mul(
            xr_t[:, sl], xr_t[:, sl], lnA[:]), inc="VE")
    s.wait("vector", "VE", l4p)
    for c in range(C):
        sl = slice(c * T, (c + 1) * T)
        lncp4 = s.op("vector", lambda e, sl=sl: e.tensor_add(
            xr_t[:, sl], xr_t[:, sl], lnB[:]), inc="VE")
    outst = []
    stcp = None
    for c in range(C):
        sl = slice(c * T, (c + 1) * T)
        ps = psA if c % 2 == 0 else psB
        if c == 0:
            s.wait("vector", "VE", lncp4)
        else:
            s.wait("vector", "TM", prev_trcp)
        ocp = s.op("vector", lambda e, sl=sl: e.tensor_add(
            ostage[:], xmain[:, sl], xr_t[:, sl]), inc="VE")
        s.wait("tensor", "VE", ocp)
        if c >= 2:
            s.wait("tensor", "VE", outst[c - 2])
        for t in range(TC):
            trcp = s.op("tensor", lambda e, ps=ps, t=t: e.transpose(
                ps[:, t * 128:(t + 1) * 128], ostage[:, t * 128:(t + 1) * 128],
                identf[:]), inc="TM" if t == TC - 1 else None)
        prev_trcp = trcp
        s.wait("vector", "TM", trcp)
        vst = s.op("vector", lambda e, ps=ps: e.tensor_copy(otstage[:], ps[:]),
                   inc="VE")
        outst.append(vst)
        s.wait("sync", "VE", vst)
        outap = bass.AP(out_h.tensor, c * 128,
                        [[D, 128], [128 * D, 16], [1, 128]])
        stcp = s.dma("sync", lambda e, o=outap: e.dma_start(
            out=o, in_=otstage[:]), inc="OD")
        s.wait("vector", "OD", stcp)
    s.wait("gpsimd", "OD", stcp)
    s.wait("sync", "OD", stcp)

    with nc.allow_low_precision("bf16 intermediates; fp32 psum accumulation"), \
         nc.Block() as blk:
        blk.tensor(s.run("tensor"))
        blk.scalar(s.run("scalar"))
        blk.vector(s.run("vector"))
        blk.sync(s.run("sync"))
        blk.gpsimd(s.run("gpsimd"))
    return nc


_NC_CACHE = {}


def _get_program():
    if "nc" not in _NC_CACHE:
        _NC_CACHE["nc"] = build_program()
    return _NC_CACHE["nc"]


def kernel(L0_attn_x, L0_attn_W_qkv, L0_attn_b_qkv, L0_ffn_W_r, L0_ffn_W_k,
           L0_ffn_W_v, L0_ffn_mu_r, L0_ffn_mu_k):
    x = np.asarray(L0_attn_x, dtype=np.float32)
    wqkv = np.ascontiguousarray(np.asarray(L0_attn_W_qkv, dtype=np.float32))
    bqkv = np.asarray(L0_attn_b_qkv, dtype=np.float32)
    bqk = np.ascontiguousarray(bqkv[:2048].reshape(16, 128).T)
    bv = np.ascontiguousarray(bqkv[2048:].reshape(1, 1024))
    wrt = np.ascontiguousarray(np.asarray(L0_ffn_W_r, dtype=np.float32).T)
    wkt = np.ascontiguousarray(np.asarray(L0_ffn_W_k, dtype=np.float32).T)
    wvt = np.ascontiguousarray(np.asarray(L0_ffn_W_v, dtype=np.float32).T)
    mu_r = np.asarray(L0_ffn_mu_r, dtype=np.float32)
    mu_k = np.asarray(L0_ffn_mu_k, dtype=np.float32)
    mu = np.ascontiguousarray(
        np.concatenate([mu_r.reshape(8, 128).T, mu_k.reshape(8, 128).T], axis=1))

    nc = _get_program()
    in_maps = []
    for b in range(8):
        in_maps.append({
            "x": np.ascontiguousarray(x[b]),
            "wqkv": wqkv, "bqk": bqk, "bv": bv,
            "wrt": wrt, "wkt": wkt, "wvt": wvt, "mu": mu,
        })
    res = run_bass_kernel_spmd(nc, in_maps, core_ids=list(range(8)))
    out = np.stack([res.results[b]["out"] for b in range(8)], axis=0)
    return out.astype(np.float32)


# version 07
# speedup vs baseline: 1.7823x; 1.7823x over previous
import numpy as np
import ml_dtypes
import concourse.bass as bass
import concourse.mybir as mybir
from concourse.bass_utils import run_bass_kernel_spmd

F32 = mybir.dt.float32
BF16 = mybir.dt.bfloat16
AF = mybir.ActivationFunctionType
ALU = mybir.AluOpType
AX = mybir.AxisListType
bf16 = ml_dtypes.bfloat16

T = 2048
D = 1024
C = 8
TC = 16
NPT = 4

ENGS = ("tensor", "scalar", "vector", "sync", "gpsimd")


class Sched:
    def __init__(self, sems):
        self.ops = {e: [] for e in ENGS}
        self.cnt = {n: 0 for n in sems}
        self.sems = sems

    def op(self, eng, fn, inc=None, incval=1):
        if inc is not None:
            self.cnt[inc] += incval
            self.ops[eng].append(("op", fn, self.sems[inc], incval))
            return self.cnt[inc]
        self.ops[eng].append(("op", fn, None, 0))
        return None

    def dma(self, eng, fn, inc):
        return self.op(eng, fn, inc=inc, incval=16)

    def wait(self, eng, sem_name, val):
        if val is None or val <= 0:
            return
        self.ops[eng].append(("wait", self.sems[sem_name], val))

    def cp(self, sem_name):
        return self.cnt[sem_name]

    def run(self, eng_name):
        def body(e):
            for item in self.ops[eng_name]:
                if item[0] == "wait":
                    e.wait_ge(item[1], item[2])
                else:
                    _, fn, sem, incval = item
                    r = fn(e)
                    if sem is not None:
                        r.then_inc(sem, incval)
        return body


def _finish_debug(nc, s, sems, taps):
    outs = {}
    for name, t in taps.items():
        d = nc.dram_tensor("tap_" + name, [t.shape[0], int(np.prod(t.shape[1:]))],
                           t.dtype, kind="ExternalOutput").ap()
        outs[name] = d
    for sem_name in s.cnt:
        if sem_name != "OD":
            s.wait("sync", sem_name, s.cp(sem_name))
    last = None
    for name, t in taps.items():
        last = s.dma("sync", lambda e, t=t, d=outs[name]: e.dma_start(
            out=d, in_=t[:, :]), inc="OD")
    s.wait("sync", "OD", last)
    s.wait("gpsimd", "OD", last)
    with nc.allow_low_precision("debug"), nc.Block() as blk:
        blk.tensor(s.run("tensor"))
        blk.scalar(s.run("scalar"))
        blk.vector(s.run("vector"))
        blk.sync(s.run("sync"))
        blk.gpsimd(s.run("gpsimd"))
    return nc


def build_program(upto=None):
    nc = bass.Bass("TRN2", target_bir_lowering=False)

    x_h = nc.dram_tensor("x", [T, D], F32, kind="ExternalInput").ap()
    wqkv_h = nc.dram_tensor("wqkv", [D, 3 * D], F32, kind="ExternalInput").ap()
    bqk_h = nc.dram_tensor("bqk", [128, 16], F32, kind="ExternalInput").ap()
    bv_h = nc.dram_tensor("bv", [1, D], F32, kind="ExternalInput").ap()
    wrt_h = nc.dram_tensor("wrt", [D, D], F32, kind="ExternalInput").ap()
    wkt_h = nc.dram_tensor("wkt", [D, D], F32, kind="ExternalInput").ap()
    wvt_h = nc.dram_tensor("wvt", [D, D], F32, kind="ExternalInput").ap()
    mu_h = nc.dram_tensor("mu", [128, 16], F32, kind="ExternalInput").ap()
    out_h = nc.dram_tensor("out", [T, D], F32, kind="ExternalOutput").ap()

    SB = nc.sbuf_top
    cur = [nc.sbuf_base]

    def alloc(name, shape, dtype, at=None):
        nb = int(np.prod(shape[1:])) * mybir.dt.size(dtype)
        if at is None:
            at = (cur[0] + 31) // 32 * 32
            cur[0] = at + nb
        return nc.alloc_sbuf_tensor_at(name, shape, dtype, offset=at)

    def addr(t):
        return nc.lookup_mloc(t).addr

    xmain = alloc("xmain", [128, C * T], BF16)
    attnb = alloc("attnb", [128, C * T], BF16)
    vones = alloc("vones", [128, 16 * 1040], BF16)
    scr = {}
    for nm in ("q0", "q1", "k0", "k1"):
        scr[nm] = [alloc(f"scr_{nm}_{i}", [128, T], BF16) for i in range(2)]
    pT = [alloc(f"pT{i}", [128, 1024], BF16) for i in range(NPT)]
    wring = [alloc(f"wring{i}", [128, 1024], BF16) for i in range(4)]
    ub = [alloc(f"ub{i}", [128, 1024], BF16) for i in range(2)]
    ttst = alloc("ttst", [128, 1024], BF16)
    rrow = alloc("rrow", [128, 1024], BF16)
    lnA = alloc("lnA", [128, T], BF16)
    lnB = alloc("lnB", [128, T], BF16)
    lnBM = alloc("lnBM", [128, T], BF16)
    stdb = alloc("stdb", [128, T], BF16)
    meanb = alloc("meanb", [128, T], BF16)
    st_a = alloc("st_a", [128, 512], F32)
    st_b = alloc("st_b", [128, 512], F32)
    st_c = alloc("st_c", [128, 512], F32)
    sAr = alloc("sAr", [128, 512], BF16)
    sBr = alloc("sBr", [128, 512], BF16)
    onesb = alloc("onesb", [128, 128], BF16)
    identf = alloc("identf", [128, 128], F32)
    bqk_s = alloc("bqk_s", [128, 16], F32)
    mu_s = alloc("mu_s", [128, 16], F32)
    mu1m = alloc("mu1m", [128, 16], F32)
    bvrow = alloc("bvrow", [1, D], BF16)
    neg30 = alloc("neg30", [128, 1], F32)
    epsc = alloc("epsc", [128, 1], F32)
    statc = alloc("statc", [128, 64], F32)
    stT = alloc("stT", [64, 128], F32)
    xpc = alloc("xpc", [128, T], BF16)
    tmpc = alloc("tmpc", [128, T], BF16)
    tmpc2 = alloc("tmpc2", [128, T], BF16)
    ostage = alloc("ostage", [128, T], F32)

    a_at = addr(attnb)
    xstage = [nc.alloc_sbuf_tensor_at(f"xstage{i}", [128, 1024], F32,
                                      offset=a_at + i * 4096) for i in range(2)]
    sqstage = [nc.alloc_sbuf_tensor_at(f"sqstage{i}", [128, 1024], F32,
                                       offset=a_at + 8192 + i * 4096)
               for i in range(2)]
    wvfull = nc.alloc_sbuf_tensor_at("wvfull", [128, 8 * 1024], BF16,
                                     offset=a_at + 16384)
    xr_t = nc.alloc_sbuf_tensor_at("xr_t", [128, C * T], BF16,
                                   offset=addr(scr["q0"][0]))
    rowsb = nc.alloc_sbuf_tensor_at("rowsb", [1, 4 * T], BF16,
                                    offset=addr(scr["k0"][0]))
    rsig_t = nc.alloc_sbuf_tensor_at("rsig_t", [128, C * T], BF16,
                                     offset=addr(vones))
    otstage = nc.alloc_sbuf_tensor_at("otstage", [128, T], F32,
                                      offset=addr(vones))
    wslotA = nc.alloc_sbuf_tensor_at("wslotA", [128, 8 * 1024], BF16,
                                     offset=addr(pT[0]))
    wslotB = nc.alloc_sbuf_tensor_at("wslotB", [128, 8 * 1024], BF16,
                                     offset=addr(stdb))
    assert cur[0] <= SB, (cur[0], SB)

    nc.psum_base = 0
    PS = nc.alloc_psum_tensor("PS", [128, 4096], F32)
    psT = [PS[:, 0:1024], PS[:, 1024:2048]]
    psBC = PS[:, 2048:4096]
    slab = [PS[:, 0:1024], PS[:, 1024:2048]]
    pspv = PS[:, 2048:3072]
    psmisc = PS[:, 3072:4096]
    psA = PS[:, 0:2048]
    psB = PS[:, 2048:4096]

    sems = {n: nc.alloc_semaphore(n) for n in
            ("LD0", "LD1", "LDC", "LGC", "RD", "LG0", "LG1", "LG2", "LG3",
             "DPa", "DPb", "TD", "XD", "OD", "W1", "W2",
             "TM", "SC", "VE", "GP")}
    s = Sched(sems)
    ck = {}

    idm = s.op("gpsimd", lambda e: e.memset(identf[:], 0.0), inc="GP")
    s.op("gpsimd", lambda e: e.memset(onesb[:], 1.0))
    s.op("gpsimd", lambda e: e.memset(neg30[:], -30.0))
    s.wait("gpsimd", "GP", idm)
    s.op("gpsimd", lambda e: e.affine_select(
        out=identf[:], in_=identf[:], compare_op=ALU.not_equal, fill=1.0,
        base=0, pattern=[[-1, 128]], channel_multiplier=1))
    for jc in range(16):
        ap = bass.AP(vones, jc * 1040 + 64, [[vones.shape[1], 128], [65, 16]])
        s.op("gpsimd", lambda e, ap=ap: e.memset(ap, 1.0))
    ck["g_consts"] = s.op("gpsimd", lambda e: e.memset(epsc[:], 1e-5), inc="GP")

    s.dma("sync", lambda e: e.dma_start(out=bqk_s[:], in_=bqk_h), inc="LDC")
    ck["ld_consts"] = s.dma("sync", lambda e: e.dma_start(out=mu_s[:], in_=mu_h),
                            inc="LDC")
    s.dma("gpsimd", lambda e: e.dma_start(out=bvrow[:], in_=bv_h), inc="LGC")
    for kc in range(C):
        inap = bass.AP(wqkv_h.tensor, kc * 128 * 3072 + 2048,
                       [[3072, 128], [1, 1024]])
        s.dma("gpsimd", lambda e, i=inap, kc=kc: e.dma_start(
            out=wvfull[:, kc * 1024:(kc + 1) * 1024], in_=i), inc="LGC")
    ck["lg_wv"] = s.cp("LGC")

    s.wait("vector", "LDC", ck["ld_consts"])
    ck["ve_mu"] = s.op("vector", lambda e: e.tensor_scalar(
        mu1m[:], mu_s[:], -1.0, 1.0, ALU.mult, ALU.add), inc="VE")

    xld, trd, evd, sqd = [], [], [], []
    for tc in range(TC):
        b = tc % 2
        lsem = "LD0" if b == 0 else "LD1"
        if tc >= 2:
            s.wait("sync", "TM", trd[tc - 2])
            s.wait("sync", "VE", sqd[tc - 2])
        xld.append(s.dma("sync", lambda e, tc=tc, b=b: e.dma_start(
            out=xstage[b][:], in_=x_h[tc * 128:(tc + 1) * 128, :]), inc=lsem))
        s.wait("vector", lsem, xld[tc])
        sqw = s.op("vector", lambda e, b=b: e.tensor_mul(
            sqstage[b][:], xstage[b][:], xstage[b][:]), inc="VE")
        s.op("vector", lambda e, tc=tc, b=b: e.reduce_sum(
            statc[:, tc:tc + 1], xstage[b][:], axis=AX.X))
        s.wait("vector", "VE", sqw)
        sqd.append(s.op("vector", lambda e, tc=tc, b=b: e.reduce_sum(
            statc[:, 32 + tc:33 + tc], sqstage[b][:], axis=AX.X), inc="VE"))
        s.wait("tensor", lsem, xld[tc])
        if tc == 0:
            s.wait("tensor", "GP", ck["g_consts"])
        if tc >= 2:
            s.wait("tensor", "VE", evd[tc - 2])
        for c in range(C):
            trc = s.op("tensor", lambda e, b=b, c=c: e.transpose(
                psT[b][:, c * 128:(c + 1) * 128],
                xstage[b][:, c * 128:(c + 1) * 128], identf[:]),
                inc="TM" if c == C - 1 else None)
        trd.append(trc)
        s.wait("vector", "TM", trd[tc])
        oap = xmain[:, :].rearrange("p (c t) -> p c t", t=T)[:, :, tc * 128:
                                                            (tc + 1) * 128]
        iap = psT[b][:, :].rearrange("p (c f) -> p c f", f=128)
        evd.append(s.op("vector", lambda e, o=oap, i=iap: e.tensor_copy(o, i),
                        inc="VE"))
    ck["x_in"] = s.cp("VE")
    if upto == "stats_raw":
        return _finish_debug(nc, s, sems, dict(statc=statc))

    s.wait("vector", "VE", ck["x_in"])
    c1 = s.op("vector", lambda e: e.tensor_scalar_mul(
        statc[:, 0:16], statc[:, 0:16], 1.0 / D), inc="VE")
    c2 = s.op("vector", lambda e: e.tensor_scalar_mul(
        statc[:, 32:48], statc[:, 32:48], 1.0 / D), inc="VE")
    s.wait("scalar", "VE", c1)
    m2cp = s.op("scalar", lambda e: e.activation(st_b[:, 0:16], statc[:, 0:16],
                                                 AF.Square), inc="SC")
    s.wait("vector", "SC", m2cp)
    s.wait("vector", "VE", c2)
    ck["ln1_var"] = s.op("vector", lambda e: e.tensor_sub(
        statc[:, 32:48], statc[:, 32:48], st_b[:, 0:16]), inc="VE")
    s.wait("scalar", "GP", ck["g_consts"])
    s.wait("scalar", "VE", ck["ln1_var"])
    lncp = s.op("scalar", lambda e: e.activation(statc[:, 32:48], statc[:, 32:48],
                                                 AF.Ln, bias=epsc[:]), inc="SC")
    s.wait("scalar", "SC", lncp)
    rcp1 = s.op("scalar", lambda e: e.activation(statc[:, 16:32], statc[:, 32:48],
                                                 AF.Exp, scale=-0.5), inc="SC")
    s.wait("scalar", "SC", rcp1)
    ck["ln1_se"] = s.op("scalar", lambda e: e.activation(
        statc[:, 32:48], statc[:, 32:48], AF.Exp, scale=0.5), inc="SC")
    s.wait("vector", "SC", ck["ln1_se"])
    ck["ln1_cols"] = s.op("vector", lambda e: e.scalar_tensor_tensor(
        statc[:, 48:64], statc[:, 0:16], -1.0, statc[:, 16:32],
        ALU.mult, ALU.mult), inc="VE")

    s.wait("tensor", "VE", ck["ln1_cols"])
    ck["ln1_tr"] = s.op("tensor", lambda e: e.transpose(
        psT[0][0:64, 0:128], statc[:, :], identf[:]), inc="TM")
    s.wait("vector", "TM", ck["ln1_tr"])
    ck["ln1_trev"] = s.op("vector", lambda e: e.tensor_copy(
        stT[:], psT[0][0:64, 0:128]), inc="VE")
    s.wait("gpsimd", "VE", ck["ln1_trev"])
    for src_col, ri in ((16, 0), (48, 1), (32, 2), (0, 3)):
        outap = bass.AP(rowsb, ri * T, [[rowsb.shape[1], 1], [1, T]])
        s.dma("gpsimd", lambda e, sc_=src_col, o=outap: e.dma_start(
            out=o, in_=stT[sc_:sc_ + 16, :]), inc="RD")
    ck["ln1_rows"] = s.cp("RD")

    prev_bc_ev = [None]

    def bcast_row(row_off, dst):
        if prev_bc_ev[0] is not None:
            s.wait("tensor", "VE", prev_bc_ev[0])
        for i in range(4):
            cpt = s.op("tensor", lambda e, i=i: e.matmul(
                psBC[:, i * 512:(i + 1) * 512], onesb[0:1, :],
                rowsb[0:1, row_off + i * 512:row_off + (i + 1) * 512],
                start=True, stop=True), inc="TM" if i == 3 else None)
        s.wait("vector", "TM", cpt)
        prev_bc_ev[0] = s.op("vector", lambda e, dst=dst: e.tensor_copy(
            dst[:], psBC[:]), inc="VE")
        return prev_bc_ev[0]

    s.wait("tensor", "RD", ck["ln1_rows"])
    bcast_row(0 * T, lnA)
    bcast_row(1 * T, lnB)
    bcast_row(2 * T, stdb)
    ck["ln1_bcast"] = bcast_row(3 * T, meanb)

    s.wait("vector", "VE", ck["ln1_bcast"])
    for c in range(C):
        sl = slice(c * T, (c + 1) * T)
        mpc = s.op("vector", lambda e, sl=sl: e.tensor_mul(
            xmain[:, sl], xmain[:, sl], lnA[:]), inc="VE")
    s.wait("vector", "VE", mpc)
    for c in range(C):
        sl = slice(c * T, (c + 1) * T)
        apl = s.op("vector", lambda e, sl=sl: e.tensor_add(
            xmain[:, sl], xmain[:, sl], lnB[:]), inc="VE")
    ck["ln1_apply"] = apl
    if upto == "ln1":
        return _finish_debug(nc, s, sems, dict(xmain=xmain, lnA=lnA, lnB=lnB,
                                               stdb=stdb, meanb=meanb,
                                               statc=statc, rowsb=rowsb))

    wload = {}

    def load_w_chunk(m, slot):
        pair, which = (m, "q") if m < 8 else (m - 8, "k")
        if pair >= 2:
            s.wait("gpsimd", "VE", qk_evac[(pair - 2, which)])
        inap = bass.AP(wqkv_h.tensor, m * 128,
                       [[3072, 128], [3072 * 128, 8], [1, 128]])
        wload[m] = (s.dma("gpsimd", lambda e, i=inap, slot=slot: e.dma_start(
            out=wring[slot][:], in_=i), inc=f"LG{slot}"), f"LG{slot}")

    qk_evac = {}
    pair_last_qkt = {}
    prev_misc_user = [None]

    def emit_qk_half(pair, which, half):
        m = pair if which == "q" else 8 + pair
        slot = (2 * pair + (0 if which == "q" else 1)) % 4
        s.wait("tensor", wload[m][1], wload[m][0])
        if pair == 0 and which == "q" and half == 0:
            s.wait("tensor", "VE", ck["ln1_apply"])
        if prev_misc_user[0] is not None:
            s.wait("tensor", "VE", prev_misc_user[0])
        for nt in range(2):
            for kc in range(C):
                cpt = s.op("tensor", lambda e, nt=nt, kc=kc, slot=slot, half=half:
                           e.matmul(psmisc[:, nt * 512:(nt + 1) * 512],
                                    wring[slot][:, kc * 128:(kc + 1) * 128],
                                    xmain[:, kc * T + half * 1024 + nt * 512:
                                          kc * T + half * 1024 + (nt + 1) * 512],
                                    start=(kc == 0), stop=(kc == C - 1)),
                           inc="TM" if (nt == 1 and kc == C - 1) else None)
        s.wait("vector", "TM", cpt)
        if (pair - 2) in pair_last_qkt:
            s.wait("vector", "TM", pair_last_qkt[pair - 2])
        hsl = slice(half * 1024, (half + 1) * 1024)
        e0 = scr[which + "0"][pair % 2]
        e1 = scr[which + "1"][pair % 2]
        s.op("vector", lambda e, e0=e0, hsl=hsl, m=m: e.tensor_scalar(
            e0[0:64, hsl], psmisc[0:64, :], bqk_s[0:64, m:m + 1], None, ALU.add))
        vcp = s.op("vector", lambda e, e1=e1, hsl=hsl, m=m: e.tensor_scalar(
            e1[64:128, hsl], psmisc[64:128, :], bqk_s[64:128, m:m + 1], None,
            ALU.add), inc="VE")
        prev_misc_user[0] = vcp
        if half == 1:
            qk_evac[(pair, which)] = vcp

    dup_cp = {}

    def emit_dups(pair):
        sm = "DPa" if pair % 2 == 0 else "DPb"
        s.wait("sync", "VE", max(qk_evac[(pair, "q")], qk_evac[(pair, "k")]))
        for which in ("q", "k"):
            t0 = scr[which + "0"][pair % 2]
            t1 = scr[which + "1"][pair % 2]
            s.dma("sync", lambda e, t=t0: e.dma_start(out=t[64:128, :],
                                                      in_=t[0:64, :]), inc=sm)
            cpg = s.dma("sync", lambda e, t=t1: e.dma_start(out=t[0:64, :],
                                                            in_=t[64:128, :]),
                        inc=sm)
        dup_cp[pair] = (cpg, sm)

    load_w_chunk(0, 0)
    load_w_chunk(8, 1)
    for which in ("q", "k"):
        for half in range(2):
            emit_qk_half(0, which, half)
    emit_dups(0)
    if upto == "qk0":
        return _finish_debug(nc, s, sems, dict(
            q0=scr["q0"][0], q1=scr["q1"][0], k0=scr["k0"][0], k1=scr["k1"][0]))

    vevac = []
    for jc in range(TC):
        ps = slab[jc % 2]
        if jc == 0:
            s.wait("tensor", "LGC", s.cp("LGC"))
        if jc >= 2:
            s.wait("tensor", "VE", vevac[jc - 2])
        for nt in range(2):
            for kc in range(C):
                s.op("tensor", lambda e, ps=ps, nt=nt, kc=kc, jc=jc: e.matmul(
                    ps[:, nt * 512:(nt + 1) * 512],
                    xmain[:, kc * T + jc * 128:kc * T + (jc + 1) * 128],
                    wvfull[:, kc * 1024 + nt * 512:kc * 1024 + (nt + 1) * 512],
                    start=(kc == 0), stop=False))
            cpt = s.op("tensor", lambda e, ps=ps, nt=nt: e.matmul(
                ps[:, nt * 512:(nt + 1) * 512], onesb[0:1, :],
                bvrow[0:1, nt * 512:(nt + 1) * 512], start=False, stop=True),
                inc="TM" if nt == 1 else None)
        s.wait("vector", "TM", cpt)
        oap = vones[:, jc * 1040:jc * 1040 + 1040].rearrange(
            "p (h f) -> p h f", f=65)[:, :, 0:64]
        iap = ps[:, :].rearrange("p (h f) -> p h f", f=64)
        vevac.append(s.op("vector", lambda e, o=oap, i=iap: e.tensor_copy(o, i),
                          inc="VE"))
    ck["v_done"] = s.cp("VE")
    if upto == "v":
        return _finish_debug(nc, s, sems, dict(vones=vones))

    bg = []
    for pair in range(1, 8):
        bg.append(("loadw", pair, "q"))
        bg.append(("qk", pair, "q", 0))
        bg.append(("qk", pair, "q", 1))
        bg.append(("loadw", pair, "k"))
        bg.append(("qk", pair, "k", 0))
        bg.append(("qk", pair, "k", 1))
        bg.append(("dups", pair))
    bg_i = [0]

    def run_bg(n):
        for _ in range(n):
            if bg_i[0] >= len(bg):
                return
            item = bg[bg_i[0]]
            bg_i[0] += 1
            if item[0] == "loadw":
                _, pair, which = item
                load_w_chunk(pair if which == "q" else 8 + pair,
                             (2 * pair + (0 if which == "q" else 1)) % 4)
            elif item[0] == "qk":
                emit_qk_half(item[1], item[2], item[3])
            else:
                emit_dups(item[1])

    exp_cp, qkt_cp, pv_cp, tt_cp = {}, {}, {}, {}
    stg_dma_cp = {}
    last_ttdma = [0]
    nslab = 0
    dbg_single = upto in ("attn1", "attn1b")
    npair = 1 if dbg_single else 8
    neip = 1 if dbg_single else 2
    nih = 1 if upto == "attn1" else 2
    for pair in range(npair):
        for eip in range(neip):
            h = 2 * pair + eip
            sq0 = scr["q" + str(eip)][pair % 2]
            sk0 = scr["k" + str(eip)][pair % 2]
            if not dbg_single:
                run_bg(0 if h == 0 else (4 if h % 2 == 1 else 3))
            for ihalf in range(nih):
                g = h * 2 + ihalf
                g_start = nslab

                def emit_pv(n2, jc2, h=h, g=g):
                    pt2 = n2 % NPT
                    s.wait("tensor", "SC", exp_cp[n2])
                    if jc2 == 0 and g > 0:
                        s.wait("tensor", "VE", tt_cp[g - 1])
                    for it in range(2):
                        pvc = s.op("tensor", lambda e, it=it, jc=jc2, h=h,
                                   pt=pt2: e.matmul(
                                       pspv[0:65, it * 512:(it + 1) * 512],
                                       vones[:, jc * 1040 + h * 65:
                                             jc * 1040 + h * 65 + 65],
                                       pT[pt][:, it * 512:(it + 1) * 512],
                                       start=(jc == 0), stop=(jc == TC - 1)),
                                   inc="TM" if it == 1 else None)
                    pv_cp[n2] = pvc

                LAG = 2
                for jc in range(TC):
                    n = nslab
                    nslab += 1
                    sb = n % 2
                    pt = n % NPT
                    if jc == 0 and ihalf == 0:
                        s.wait("tensor", dup_cp[pair][1], dup_cp[pair][0])
                        if h == 0:
                            s.wait("tensor", "VE", ck["v_done"])
                    if n >= 2:
                        s.wait("tensor", "SC", exp_cp[n - 2])
                    s.op("tensor", lambda e, sb=sb, sk0=sk0, sq0=sq0, jc=jc,
                         ihalf=ihalf: e.matmul(
                             slab[sb][:, 0:512],
                             sk0[0:64, jc * 128:(jc + 1) * 128],
                             sq0[0:64, ihalf * 1024:ihalf * 1024 + 512],
                             start=True, stop=True, tile_position=(0, 0)))
                    qkt_cp[n] = s.op("tensor", lambda e, sb=sb, sk0=sk0, sq0=sq0,
                                     jc=jc, ihalf=ihalf: e.matmul(
                                         slab[sb][:, 512:1024],
                                         sk0[64:128, jc * 128:(jc + 1) * 128],
                                         sq0[64:128, ihalf * 1024 + 512:
                                             ihalf * 1024 + 1024],
                                         start=True, stop=True,
                                         tile_position=(64, 0)), inc="TM")
                    s.wait("scalar", "TM", qkt_cp[n])
                    if n >= NPT:
                        s.wait("scalar", "TM", pv_cp[n - NPT])
                    exp_cp[n] = s.op("scalar", lambda e, pt=pt, sb=sb:
                                     e.activation(pT[pt][:], slab[sb][:], AF.Exp,
                                                  bias=neg30[:]), inc="SC")
                    if jc >= LAG:
                        emit_pv(n - LAG, jc - LAG)
                for jc in range(TC - LAG, TC):
                    emit_pv(g_start + jc, jc)
                s.wait("vector", "TM", pv_cp[nslab - 1])
                ecp = s.op("vector", lambda e, g=g: e.tensor_copy(
                    ub[g % 2][0:65, :], pspv[0:65, :]), inc="VE")
                s.wait("scalar", "VE", ecp)
                lcp = s.op("scalar", lambda e, g=g: e.activation(
                    rrow[64:65, :], ub[g % 2][64:65, :], AF.Ln), inc="SC")
                s.wait("scalar", "SC", lcp)
                rcp = s.op("scalar", lambda e: e.activation(
                    rrow[64:65, :], rrow[64:65, :], AF.Exp, scale=-1.0),
                    inc="SC")
                s.wait("tensor", "SC", rcp)
                s.wait("tensor", "VE", ecp)
                for it in range(2):
                    bcp = s.op("tensor", lambda e, it=it: e.matmul(
                        pspv[0:64, it * 512:(it + 1) * 512], onesb[64:65, 0:64],
                        rrow[64:65, it * 512:(it + 1) * 512],
                        start=True, stop=True, tile_position=(64, 0)),
                        inc="TM" if it == 1 else None)
                s.wait("vector", "TM", bcp)
                s.wait("vector", "VE", ecp)
                for par in range(2):
                    src0 = ub[g % 2][0:64, :].rearrange(
                        "p (tb f) -> p f tb", f=16).rearrange(
                        "p (cc pr) tb -> p pr cc tb", pr=2)[:, par]
                    src1 = pspv[0:64, :].rearrange(
                        "p (tb f) -> p f tb", f=16).rearrange(
                        "p (cc pr) tb -> p pr cc tb", pr=2)[:, par]
                    dst = attnb[par * 64:(par + 1) * 64, :].rearrange(
                        "p (cc t) -> p cc t", t=T)[
                        :, :, h * 128 + ihalf * 64:h * 128 + ihalf * 64 + 64]
                    tt_cp[g] = s.op("vector", lambda e, o=dst, a=src0, b=src1:
                                    e.tensor_mul(o, a, b),
                                    inc="VE" if par == 1 else None)
            if eip == 1:
                pair_last_qkt[pair] = qkt_cp[nslab - 1]
    if not dbg_single:
        run_bg(100)
    ck["attn_ve"] = s.cp("VE")
    ck["attn_td"] = s.cp("TD")
    ck["attn_xd"] = s.cp("XD")
    ck["attn_sc"] = s.cp("SC")
    ck["attn_tm"] = s.cp("TM")
    if dbg_single or upto == "attn":
        return _finish_debug(nc, s, sems, dict(attnb=attnb, ub0=ub[0]))

    def load_wffn(src_h, dst, sem, wait_tm=None, wait_ve=None, wait_sc=None):
        if wait_tm:
            s.wait("gpsimd", "TM", wait_tm)
        if wait_ve:
            s.wait("gpsimd", "VE", wait_ve)
        if wait_sc:
            s.wait("gpsimd", "SC", wait_sc)
        for kc in range(C):
            inap = bass.AP(src_h.tensor, kc * 128 * D, [[D, 128], [1, 1024]])
            cpl = s.dma("gpsimd", lambda e, i=inap, kc=kc: e.dma_start(
                out=dst[:, kc * 1024:(kc + 1) * 1024], in_=i), inc=sem)
        return (cpl, sem)

    ck["wrt"] = load_wffn(wrt_h, wslotA, "W1", wait_tm=ck["attn_tm"],
                          wait_ve=ck["attn_ve"], wait_sc=ck["attn_sc"])

    def ln_feature_major(src_tensor, extra_ve_wait=None):
        s2cp = None
        for c in range(C):
            sl = slice(c * T, (c + 1) * T)
            buf = tmpc if c % 2 == 0 else xpc
            if extra_ve_wait and c == 0:
                s.wait("vector", "VE", extra_ve_wait)
            if c >= 2:
                s.wait("vector", "TM", s2cp)
            sqv = s.op("vector", lambda e, sl=sl, buf=buf: e.tensor_mul(
                buf[:], src_tensor[:, sl], src_tensor[:, sl]), inc="VE")
            s.wait("tensor", "VE", sqv)
            for i in range(4):
                s.op("tensor", lambda e, c=c, i=i: e.matmul(
                    psA[32 * i:32 * i + 1, 0:512], onesb[:, 0:1],
                    src_tensor[:, c * T + i * 512:c * T + (i + 1) * 512],
                    start=(c == 0), stop=(c == C - 1), tile_position=(0, 32 * i)))
            for i in range(4):
                s2cp = s.op("tensor", lambda e, i=i, buf=buf, c=c: e.matmul(
                    psA[32 * i:32 * i + 1, 512:1024], onesb[:, 0:1],
                    buf[:, i * 512:(i + 1) * 512],
                    start=(c == 0), stop=(c == C - 1), tile_position=(0, 32 * i)),
                    inc="TM" if i == 3 else None)
        s.wait("vector", "TM", s2cp)
        a1 = s.op("vector", lambda e: e.tensor_scalar_mul(st_a[:], psA[:, 0:512],
                                                          1.0 / D), inc="VE")
        a2 = s.op("vector", lambda e: e.tensor_scalar_mul(st_b[:],
                                                          psA[:, 512:1024],
                                                          1.0 / D), inc="VE")
        s.wait("vector", "VE", a1)
        a3 = s.op("vector", lambda e: e.tensor_mul(st_c[:], st_a[:], st_a[:]),
                  inc="VE")
        s.wait("vector", "VE", a3)
        vcp = s.op("vector", lambda e: e.tensor_sub(st_b[:], st_b[:], st_c[:]),
                   inc="VE")
        s.wait("scalar", "VE", vcp)
        for r in (0, 32, 64, 96):
            lcp = s.op("scalar", lambda e, r=r: e.activation(
                st_b[r:r + 1, :], st_b[r:r + 1, :], AF.Ln, bias=epsc[r:r + 1]),
                inc="SC" if r == 96 else None)
        s.wait("scalar", "SC", lcp)
        scp = s.op("scalar", lambda e: e.activation(sAr[:], st_b[:], AF.Exp,
                                                    scale=-0.5), inc="SC")
        s.wait("vector", "SC", scp)
        vcp2 = s.op("vector", lambda e: e.scalar_tensor_tensor(
            sBr[:], st_a[:], -1.0, sAr[:], ALU.mult, ALU.mult), inc="VE")
        s.wait("tensor", "VE", vcp2)
        for i in range(4):
            acp = s.op("tensor", lambda e, i=i: e.matmul(
                psB[:, i * 512:(i + 1) * 512], onesb[32 * i:32 * i + 1, :],
                sAr[32 * i:32 * i + 1, :], start=True, stop=True,
                tile_position=(32 * i, 0)), inc="TM" if i == 3 else None)
        s.wait("vector", "TM", acp)
        va = s.op("vector", lambda e: e.tensor_copy(lnA[:], psB[:]), inc="VE")
        s.wait("tensor", "VE", va)
        for i in range(4):
            bcp = s.op("tensor", lambda e, i=i: e.matmul(
                psB[:, i * 512:(i + 1) * 512], onesb[32 * i:32 * i + 1, :],
                sBr[32 * i:32 * i + 1, :], start=True, stop=True,
                tile_position=(32 * i, 0)), inc="TM" if i == 3 else None)
        s.wait("vector", "TM", bcp)
        return s.op("vector", lambda e: e.tensor_copy(lnB[:], psB[:]), inc="VE")

    s.wait("vector", "VE", ck["attn_ve"])
    s.wait("vector", "TD", ck["attn_td"])
    s.wait("vector", "XD", ck["attn_xd"])
    s.wait("tensor", "SC", ck["attn_sc"])
    ln2 = ln_feature_major(attnb)
    s.wait("vector", "VE", ln2)
    bmcp = s.op("vector", lambda e: e.tensor_add(lnBM[:], lnB[:], meanb[:]),
                inc="VE")
    s.wait("vector", "VE", bmcp)
    for c in range(C):
        sl = slice(c * T, (c + 1) * T)
        p1 = s.op("vector", lambda e, sl=sl: e.tensor_mul(
            attnb[:, sl], attnb[:, sl], lnA[:]), inc="VE")
    s.wait("vector", "VE", p1)
    for c in range(C):
        sl = slice(c * T, (c + 1) * T)
        p2 = s.op("vector", lambda e, sl=sl: e.tensor_add(
            attnb[:, sl], attnb[:, sl], lnBM[:]), inc="VE")
    for c in range(C):
        sl = slice(c * T, (c + 1) * T)
        p3 = s.op("vector", lambda e, sl=sl: e.tensor_mul(
            xmain[:, sl], xmain[:, sl], stdb[:]), inc="VE")
    s.wait("vector", "VE", p3)
    for c in range(C):
        sl = slice(c * T, (c + 1) * T)
        x2cp = s.op("vector", lambda e, sl=sl: e.tensor_add(
            xmain[:, sl], xmain[:, sl], attnb[:, sl]), inc="VE")
    ck["x2"] = x2cp
    if upto == "x2":
        return _finish_debug(nc, s, sems, dict(xmain=xmain))

    ln3 = ln_feature_major(xmain, extra_ve_wait=ck["x2"])
    s.wait("vector", "VE", ln3)
    for c in range(C):
        sl = slice(c * T, (c + 1) * T)
        l3p = s.op("vector", lambda e, sl=sl: e.tensor_mul(
            attnb[:, sl], xmain[:, sl], lnA[:]), inc="VE")
    s.wait("vector", "VE", l3p)
    for c in range(C):
        sl = slice(c * T, (c + 1) * T)
        xn3cp = s.op("vector", lambda e, sl=sl: e.tensor_add(
            attnb[:, sl], attnb[:, sl], lnB[:]), inc="VE")
    ck["xn3"] = xn3cp
    if upto == "xn3":
        return _finish_debug(nc, s, sems, dict(attnb=attnb))

    ck["wkt"] = load_wffn(wkt_h, wslotB, "W2", wait_ve=ck["xn3"])

    def emit_mix(base, dst, wait_ve_first):
        mcps = []
        tcps = {}
        for c in range(C):
            sl = slice(c * T, (c + 1) * T)
            mc = base + c
            buf = tmpc2 if c % 2 == 0 else xpc
            if c == 0:
                s.wait("vector", "VE", wait_ve_first)
                s.op("vector", lambda e: e.memset(tmpc2[:, 0:1], 0.0))
                s.op("vector", lambda e: e.memset(xpc[:, 0:1], 0.0))
            if c >= 2:
                s.wait("vector", "VE", mcps[c - 2])
            tcps[c] = s.op("vector", lambda e, mc=mc, c=c, buf=buf:
                           e.tensor_scalar(
                               buf[:, 1:T], attnb[:, c * T:c * T + T - 1],
                               mu1m[:, mc:mc + 1], None, ALU.mult), inc="VE")
            s.wait("vector", "VE", tcps[c])
            mcps.append(s.op("vector", lambda e, sl=sl, mc=mc, dst=dst,
                             buf=buf: e.scalar_tensor_tensor(
                                 dst[:, sl], attnb[:, sl], mu_s[:, mc:mc + 1],
                                 buf[:], ALU.mult, ALU.add), inc="VE"))
        return mcps[-1]

    ck["xr"] = emit_mix(0, xr_t, ck["xn3"])

    def ffn_mm(wslot, src, dc, wait_lg):
        ps = psA if dc % 2 == 0 else psB
        s.wait("tensor", wait_lg[1], wait_lg[0])
        for nt in range(4):
            for kc in range(C):
                cpt = s.op("tensor", lambda e, ps=ps, nt=nt, kc=kc, dc=dc,
                           wslot=wslot, src=src: e.matmul(
                               ps[:, nt * 512:(nt + 1) * 512],
                               wslot[:, kc * 1024 + dc * 128:
                                     kc * 1024 + (dc + 1) * 128],
                               src[:, kc * T + nt * 512:kc * T + (nt + 1) * 512],
                               start=(kc == 0), stop=(kc == C - 1)),
                           inc="TM" if (nt == 3 and kc == C - 1) else None)
        return cpt, ps

    rexp_cp, rmm_cp = {}, {}
    for dc in range(C):
        if dc == 0:
            s.wait("tensor", "VE", ck["xr"])
        if dc >= 2:
            s.wait("tensor", "SC", rexp_cp[dc - 2])
        rmm_cp[dc], ps = ffn_mm(wslotA, xr_t, dc, ck["wrt"])
        s.wait("scalar", "TM", rmm_cp[dc])
        rexp_cp[dc] = s.op("scalar", lambda e, ps=ps, dc=dc: e.activation(
            rsig_t[:, dc * T:(dc + 1) * T], ps[:], AF.Sigmoid), inc="SC")
    ck["rsig"] = rexp_cp[C - 1]
    ck["rmm_last"] = rmm_cp[C - 1]
    if upto == "rsig":
        return _finish_debug(nc, s, sems, dict(rsig=rsig_t, xr=xr_t))

    ck["wvt"] = load_wffn(wvt_h, wslotA, "W1", wait_tm=ck["rmm_last"])

    s.wait("vector", "TM", ck["rmm_last"])
    ck["xk"] = emit_mix(8, xr_t, ck["xn3"])

    kact_cp = {}
    krl_cp = {}
    for dc in range(C):
        if dc == 0:
            s.wait("tensor", "VE", ck["xk"])
            s.wait("tensor", "SC", rexp_cp[C - 1])
        if dc >= 2:
            s.wait("tensor", "VE", kact_cp[dc - 2])
        kcp, ps = ffn_mm(wslotB, xr_t, dc, ck["wkt"])
        kbuf = tmpc if dc % 2 == 0 else tmpc2
        s.wait("scalar", "TM", kcp)
        if dc >= 2:
            s.wait("scalar", "VE", kact_cp[dc - 2])
        krl_cp[dc] = s.op("scalar", lambda e, ps=ps, kbuf=kbuf: e.activation(
            kbuf[:], ps[:], AF.Relu), inc="SC")
        s.wait("vector", "SC", krl_cp[dc])
        kact_cp[dc] = s.op("vector", lambda e, dc=dc, kbuf=kbuf: e.tensor_mul(
            attnb[:, dc * T:(dc + 1) * T], kbuf[:], kbuf[:]), inc="VE")
    ck["k2"] = kact_cp[C - 1]
    if upto == "k2":
        return _finish_debug(nc, s, sems, dict(k2=attnb))

    fout_cp = {}
    for dc in range(C):
        if dc == 0:
            s.wait("tensor", "VE", ck["k2"])
            s.wait("tensor", "SC", ck["rsig"])
        if dc >= 2:
            s.wait("tensor", "VE", fout_cp[dc - 2])
        vcp, ps = ffn_mm(wslotA, attnb, dc, ck["wvt"])
        s.wait("vector", "TM", vcp)
        fout_cp[dc] = s.op("vector", lambda e, ps=ps, dc=dc: e.tensor_mul(
            xr_t[:, dc * T:(dc + 1) * T], ps[:],
            rsig_t[:, dc * T:(dc + 1) * T]), inc="VE")
    ck["ffnout"] = fout_cp[C - 1]
    if upto == "ffnout":
        return _finish_debug(nc, s, sems, dict(ffnout=xr_t))

    ln4 = ln_feature_major(xr_t, extra_ve_wait=ck["ffnout"])
    s.wait("vector", "VE", ln4)
    for c in range(C):
        sl = slice(c * T, (c + 1) * T)
        l4p = s.op("vector", lambda e, sl=sl: e.tensor_mul(
            xr_t[:, sl], xr_t[:, sl], lnA[:]), inc="VE")
    s.wait("vector", "VE", l4p)
    for c in range(C):
        sl = slice(c * T, (c + 1) * T)
        lncp4 = s.op("vector", lambda e, sl=sl: e.tensor_add(
            xr_t[:, sl], xr_t[:, sl], lnB[:]), inc="VE")
    outst = []
    stcp = None
    for c in range(C):
        sl = slice(c * T, (c + 1) * T)
        ps = psA if c % 2 == 0 else psB
        if c == 0:
            s.wait("vector", "VE", lncp4)
        else:
            s.wait("vector", "TM", prev_trcp)
        ocp = s.op("vector", lambda e, sl=sl: e.tensor_add(
            ostage[:], xmain[:, sl], xr_t[:, sl]), inc="VE")
        s.wait("tensor", "VE", ocp)
        if c >= 2:
            s.wait("tensor", "VE", outst[c - 2])
        for t in range(TC):
            trcp = s.op("tensor", lambda e, ps=ps, t=t: e.transpose(
                ps[:, t * 128:(t + 1) * 128], ostage[:, t * 128:(t + 1) * 128],
                identf[:]), inc="TM" if t == TC - 1 else None)
        prev_trcp = trcp
        s.wait("vector", "TM", trcp)
        vst = s.op("vector", lambda e, ps=ps: e.tensor_copy(otstage[:], ps[:]),
                   inc="VE")
        outst.append(vst)
        s.wait("sync", "VE", vst)
        outap = bass.AP(out_h.tensor, c * 128,
                        [[D, 128], [128 * D, 16], [1, 128]])
        stcp = s.dma("sync", lambda e, o=outap: e.dma_start(
            out=o, in_=otstage[:]), inc="OD")
        s.wait("vector", "OD", stcp)
    s.wait("gpsimd", "OD", stcp)
    s.wait("sync", "OD", stcp)

    with nc.allow_low_precision("bf16 intermediates; fp32 psum accumulation"), \
         nc.Block() as blk:
        blk.tensor(s.run("tensor"))
        blk.scalar(s.run("scalar"))
        blk.vector(s.run("vector"))
        blk.sync(s.run("sync"))
        blk.gpsimd(s.run("gpsimd"))
    return nc


_NC_CACHE = {}


def _get_program():
    if "nc" not in _NC_CACHE:
        _NC_CACHE["nc"] = build_program()
    return _NC_CACHE["nc"]


def kernel(L0_attn_x, L0_attn_W_qkv, L0_attn_b_qkv, L0_ffn_W_r, L0_ffn_W_k,
           L0_ffn_W_v, L0_ffn_mu_r, L0_ffn_mu_k):
    x = np.asarray(L0_attn_x, dtype=np.float32)
    wqkv = np.ascontiguousarray(np.asarray(L0_attn_W_qkv, dtype=np.float32))
    bqkv = np.asarray(L0_attn_b_qkv, dtype=np.float32)
    bqk = np.ascontiguousarray(bqkv[:2048].reshape(16, 128).T)
    bv = np.ascontiguousarray(bqkv[2048:].reshape(1, 1024))
    wrt = np.ascontiguousarray(np.asarray(L0_ffn_W_r, dtype=np.float32).T)
    wkt = np.ascontiguousarray(np.asarray(L0_ffn_W_k, dtype=np.float32).T)
    wvt = np.ascontiguousarray(np.asarray(L0_ffn_W_v, dtype=np.float32).T)
    mu_r = np.asarray(L0_ffn_mu_r, dtype=np.float32)
    mu_k = np.asarray(L0_ffn_mu_k, dtype=np.float32)
    mu = np.ascontiguousarray(
        np.concatenate([mu_r.reshape(8, 128).T, mu_k.reshape(8, 128).T], axis=1))

    nc = _get_program()
    in_maps = []
    for b in range(8):
        in_maps.append({
            "x": np.ascontiguousarray(x[b]),
            "wqkv": wqkv, "bqk": bqk, "bv": bv,
            "wrt": wrt, "wkt": wkt, "wvt": wvt, "mu": mu,
        })
    res = run_bass_kernel_spmd(nc, in_maps, core_ids=list(range(8)))
    out = np.stack([res.results[b]["out"] for b in range(8)], axis=0)
    return out.astype(np.float32)


# version 08
# speedup vs baseline: 1.8277x; 1.0255x over previous
import numpy as np
import ml_dtypes
import concourse.bass as bass
import concourse.mybir as mybir
from concourse.bass_utils import run_bass_kernel_spmd

F32 = mybir.dt.float32
BF16 = mybir.dt.bfloat16
AF = mybir.ActivationFunctionType
ALU = mybir.AluOpType
AX = mybir.AxisListType
bf16 = ml_dtypes.bfloat16

T = 2048
D = 1024
C = 8
TC = 16
NPT = 4

ENGS = ("tensor", "scalar", "vector", "sync", "gpsimd")


class Sched:
    def __init__(self, sems):
        self.ops = {e: [] for e in ENGS}
        self.cnt = {n: 0 for n in sems}
        self.sems = sems

    def op(self, eng, fn, inc=None, incval=1):
        if inc is not None:
            self.cnt[inc] += incval
            self.ops[eng].append(("op", fn, self.sems[inc], incval))
            return self.cnt[inc]
        self.ops[eng].append(("op", fn, None, 0))
        return None

    def dma(self, eng, fn, inc):
        return self.op(eng, fn, inc=inc, incval=16)

    def wait(self, eng, sem_name, val):
        if val is None or val <= 0:
            return
        self.ops[eng].append(("wait", self.sems[sem_name], val))

    def cp(self, sem_name):
        return self.cnt[sem_name]

    def run(self, eng_name):
        def body(e):
            for item in self.ops[eng_name]:
                if item[0] == "wait":
                    e.wait_ge(item[1], item[2])
                else:
                    _, fn, sem, incval = item
                    r = fn(e)
                    if sem is not None:
                        r.then_inc(sem, incval)
        return body


def _finish_debug(nc, s, sems, taps):
    outs = {}
    for name, t in taps.items():
        d = nc.dram_tensor("tap_" + name, [t.shape[0], int(np.prod(t.shape[1:]))],
                           t.dtype, kind="ExternalOutput").ap()
        outs[name] = d
    for sem_name in s.cnt:
        if sem_name != "OD":
            s.wait("sync", sem_name, s.cp(sem_name))
    last = None
    for name, t in taps.items():
        last = s.dma("sync", lambda e, t=t, d=outs[name]: e.dma_start(
            out=d, in_=t[:, :]), inc="OD")
    s.wait("sync", "OD", last)
    s.wait("gpsimd", "OD", last)
    with nc.allow_low_precision("debug"), nc.Block() as blk:
        blk.tensor(s.run("tensor"))
        blk.scalar(s.run("scalar"))
        blk.vector(s.run("vector"))
        blk.sync(s.run("sync"))
        blk.gpsimd(s.run("gpsimd"))
    return nc


def build_program(upto=None):
    nc = bass.Bass("TRN2", target_bir_lowering=False)

    x_h = nc.dram_tensor("x", [T, D], F32, kind="ExternalInput").ap()
    wqkv_h = nc.dram_tensor("wqkv", [D, 3 * D], F32, kind="ExternalInput").ap()
    bqk_h = nc.dram_tensor("bqk", [128, 16], F32, kind="ExternalInput").ap()
    bv_h = nc.dram_tensor("bv", [1, D], F32, kind="ExternalInput").ap()
    wrt_h = nc.dram_tensor("wrt", [D, D], F32, kind="ExternalInput").ap()
    wkt_h = nc.dram_tensor("wkt", [D, D], F32, kind="ExternalInput").ap()
    wvt_h = nc.dram_tensor("wvt", [D, D], F32, kind="ExternalInput").ap()
    mu_h = nc.dram_tensor("mu", [128, 16], F32, kind="ExternalInput").ap()
    out_h = nc.dram_tensor("out", [T, D], F32, kind="ExternalOutput").ap()

    SB = nc.sbuf_top
    cur = [nc.sbuf_base]

    def alloc(name, shape, dtype, at=None):
        nb = int(np.prod(shape[1:])) * mybir.dt.size(dtype)
        if at is None:
            at = (cur[0] + 31) // 32 * 32
            cur[0] = at + nb
        return nc.alloc_sbuf_tensor_at(name, shape, dtype, offset=at)

    def addr(t):
        return nc.lookup_mloc(t).addr

    xmain = alloc("xmain", [128, C * T], BF16)
    attnb = alloc("attnb", [128, C * T], BF16)
    vones = alloc("vones", [128, 16 * 1040], BF16)
    scr = {}
    for nm in ("q0", "q1", "k0", "k1"):
        scr[nm] = [alloc(f"scr_{nm}_{i}", [128, T], BF16) for i in range(2)]
    pT = [alloc(f"pT{i}", [128, 1024], BF16) for i in range(NPT)]
    wring = [alloc(f"wring{i}", [128, 1024], BF16) for i in range(4)]
    ub = [alloc(f"ub{i}", [128, 1024], BF16) for i in range(2)]
    ttst = alloc("ttst", [128, 1024], BF16)
    rrow = alloc("rrow", [128, 1024], BF16)
    lnA = alloc("lnA", [128, T], BF16)
    lnB = alloc("lnB", [128, T], BF16)
    lnBM = alloc("lnBM", [128, T], BF16)
    stdb = alloc("stdb", [128, T], BF16)
    meanb = alloc("meanb", [128, T], BF16)
    st_a = alloc("st_a", [128, 512], F32)
    st_b = alloc("st_b", [128, 512], F32)
    st_c = alloc("st_c", [128, 512], F32)
    sAr = alloc("sAr", [128, 512], BF16)
    sBr = alloc("sBr", [128, 512], BF16)
    onesb = alloc("onesb", [128, 128], BF16)
    identf = alloc("identf", [128, 128], F32)
    bqk_s = alloc("bqk_s", [128, 16], F32)
    mu_s = alloc("mu_s", [128, 16], F32)
    mu1m = alloc("mu1m", [128, 16], F32)
    bvrow = alloc("bvrow", [1, D], BF16)
    neg30 = alloc("neg30", [128, 1], F32)
    epsc = alloc("epsc", [128, 1], F32)
    statc = alloc("statc", [128, 64], F32)
    stT = alloc("stT", [64, 128], F32)
    xpc = alloc("xpc", [128, T], BF16)
    tmpc = alloc("tmpc", [128, T], BF16)
    tmpc2 = alloc("tmpc2", [128, T], BF16)
    ostage = alloc("ostage", [128, T], F32)

    a_at = addr(attnb)
    xstage = [nc.alloc_sbuf_tensor_at(f"xstage{i}", [128, 1024], F32,
                                      offset=a_at + i * 4096) for i in range(2)]
    sqstage = [nc.alloc_sbuf_tensor_at(f"sqstage{i}", [128, 1024], F32,
                                       offset=a_at + 8192 + i * 4096)
               for i in range(2)]
    wvfull = nc.alloc_sbuf_tensor_at("wvfull", [128, 8 * 1024], BF16,
                                     offset=a_at + 16384)
    xr_t = nc.alloc_sbuf_tensor_at("xr_t", [128, C * T], BF16,
                                   offset=addr(scr["q0"][0]))
    rowsb = nc.alloc_sbuf_tensor_at("rowsb", [1, 4 * T], BF16,
                                    offset=addr(scr["k0"][0]))
    rsig_t = nc.alloc_sbuf_tensor_at("rsig_t", [128, C * T], BF16,
                                     offset=addr(vones))
    otstage = nc.alloc_sbuf_tensor_at("otstage", [128, T], F32,
                                      offset=addr(vones))
    otstage2 = nc.alloc_sbuf_tensor_at("otstage2", [128, T], F32,
                                       offset=addr(vones) + 8192)
    wslotA = nc.alloc_sbuf_tensor_at("wslotA", [128, 8 * 1024], BF16,
                                     offset=addr(pT[0]))
    wslotB = nc.alloc_sbuf_tensor_at("wslotB", [128, 8 * 1024], BF16,
                                     offset=addr(stdb))
    assert cur[0] <= SB, (cur[0], SB)

    nc.psum_base = 0
    PS = nc.alloc_psum_tensor("PS", [128, 4096], F32)
    psT = [PS[:, 0:1024], PS[:, 1024:2048]]
    psBC = PS[:, 2048:4096]
    slab = [PS[:, 0:1024], PS[:, 1024:2048]]
    pspv = PS[:, 2048:3072]
    psmisc = PS[:, 3072:4096]
    psA = PS[:, 0:2048]
    psB = PS[:, 2048:4096]

    sems = {n: nc.alloc_semaphore(n) for n in
            ("LD0", "LD1", "LDC", "LGC", "RD", "LG0", "LG1", "LG2", "LG3",
             "DPa", "DPb", "TD", "XD", "OD", "OD2", "W1", "W2",
             "TM", "SC", "VE", "GP")}
    s = Sched(sems)
    ck = {}

    idm = s.op("gpsimd", lambda e: e.memset(identf[:], 0.0), inc="GP")
    s.op("gpsimd", lambda e: e.memset(onesb[:], 1.0))
    s.op("gpsimd", lambda e: e.memset(neg30[:], -30.0))
    s.wait("gpsimd", "GP", idm)
    s.op("gpsimd", lambda e: e.affine_select(
        out=identf[:], in_=identf[:], compare_op=ALU.not_equal, fill=1.0,
        base=0, pattern=[[-1, 128]], channel_multiplier=1))
    for jc in range(16):
        ap = bass.AP(vones, jc * 1040 + 64, [[vones.shape[1], 128], [65, 16]])
        s.op("gpsimd", lambda e, ap=ap: e.memset(ap, 1.0))
    ck["g_consts"] = s.op("gpsimd", lambda e: e.memset(epsc[:], 1e-5), inc="GP")

    s.dma("sync", lambda e: e.dma_start(out=bqk_s[:], in_=bqk_h), inc="LDC")
    ck["ld_consts"] = s.dma("sync", lambda e: e.dma_start(out=mu_s[:], in_=mu_h),
                            inc="LDC")
    s.dma("gpsimd", lambda e: e.dma_start(out=bvrow[:], in_=bv_h), inc="LGC")
    for kc in range(C):
        inap = bass.AP(wqkv_h.tensor, kc * 128 * 3072 + 2048,
                       [[3072, 128], [1, 1024]])
        s.dma("gpsimd", lambda e, i=inap, kc=kc: e.dma_start(
            out=wvfull[:, kc * 1024:(kc + 1) * 1024], in_=i), inc="LGC")
    ck["lg_wv"] = s.cp("LGC")

    s.wait("vector", "LDC", ck["ld_consts"])
    ck["ve_mu"] = s.op("vector", lambda e: e.tensor_scalar(
        mu1m[:], mu_s[:], -1.0, 1.0, ALU.mult, ALU.add), inc="VE")

    xld, trd, evd, sqd = [], [], [], []
    for tc in range(TC):
        b = tc % 2
        lsem = "LD0" if b == 0 else "LD1"
        ldeng = "sync" if b == 0 else "scalar"
        if tc >= 2:
            s.wait(ldeng, "TM", trd[tc - 2])
            s.wait(ldeng, "VE", sqd[tc - 2])
        xld.append(s.dma(ldeng, lambda e, tc=tc, b=b: e.dma_start(
            out=xstage[b][:], in_=x_h[tc * 128:(tc + 1) * 128, :]), inc=lsem))
        s.wait("vector", lsem, xld[tc])
        sqw = s.op("vector", lambda e, b=b: e.tensor_mul(
            sqstage[b][:], xstage[b][:], xstage[b][:]), inc="VE")
        s.op("vector", lambda e, tc=tc, b=b: e.reduce_sum(
            statc[:, tc:tc + 1], xstage[b][:], axis=AX.X))
        s.wait("vector", "VE", sqw)
        sqd.append(s.op("vector", lambda e, tc=tc, b=b: e.reduce_sum(
            statc[:, 32 + tc:33 + tc], sqstage[b][:], axis=AX.X), inc="VE"))
        s.wait("tensor", lsem, xld[tc])
        if tc == 0:
            s.wait("tensor", "GP", ck["g_consts"])
        if tc >= 2:
            s.wait("tensor", "VE", evd[tc - 2])
        for c in range(C):
            trc = s.op("tensor", lambda e, b=b, c=c: e.transpose(
                psT[b][:, c * 128:(c + 1) * 128],
                xstage[b][:, c * 128:(c + 1) * 128], identf[:]),
                inc="TM" if c == C - 1 else None)
        trd.append(trc)
        s.wait("vector", "TM", trd[tc])
        oap = xmain[:, :].rearrange("p (c t) -> p c t", t=T)[:, :, tc * 128:
                                                            (tc + 1) * 128]
        iap = psT[b][:, :].rearrange("p (c f) -> p c f", f=128)
        evd.append(s.op("vector", lambda e, o=oap, i=iap: e.tensor_copy(o, i),
                        inc="VE"))
    ck["x_in"] = s.cp("VE")
    if upto == "stats_raw":
        return _finish_debug(nc, s, sems, dict(statc=statc))

    s.wait("vector", "VE", ck["x_in"])
    c1 = s.op("vector", lambda e: e.tensor_scalar_mul(
        statc[:, 0:16], statc[:, 0:16], 1.0 / D), inc="VE")
    c2 = s.op("vector", lambda e: e.tensor_scalar_mul(
        statc[:, 32:48], statc[:, 32:48], 1.0 / D), inc="VE")
    s.wait("scalar", "VE", c1)
    m2cp = s.op("scalar", lambda e: e.activation(st_b[:, 0:16], statc[:, 0:16],
                                                 AF.Square), inc="SC")
    s.wait("vector", "SC", m2cp)
    s.wait("vector", "VE", c2)
    ck["ln1_var"] = s.op("vector", lambda e: e.tensor_sub(
        statc[:, 32:48], statc[:, 32:48], st_b[:, 0:16]), inc="VE")
    s.wait("scalar", "GP", ck["g_consts"])
    s.wait("scalar", "VE", ck["ln1_var"])
    lncp = s.op("scalar", lambda e: e.activation(statc[:, 32:48], statc[:, 32:48],
                                                 AF.Ln, bias=epsc[:]), inc="SC")
    s.wait("scalar", "SC", lncp)
    rcp1 = s.op("scalar", lambda e: e.activation(statc[:, 16:32], statc[:, 32:48],
                                                 AF.Exp, scale=-0.5), inc="SC")
    s.wait("scalar", "SC", rcp1)
    ck["ln1_se"] = s.op("scalar", lambda e: e.activation(
        statc[:, 32:48], statc[:, 32:48], AF.Exp, scale=0.5), inc="SC")
    s.wait("vector", "SC", ck["ln1_se"])
    ck["ln1_cols"] = s.op("vector", lambda e: e.scalar_tensor_tensor(
        statc[:, 48:64], statc[:, 0:16], -1.0, statc[:, 16:32],
        ALU.mult, ALU.mult), inc="VE")

    s.wait("tensor", "VE", ck["ln1_cols"])
    ck["ln1_tr"] = s.op("tensor", lambda e: e.transpose(
        psT[0][0:64, 0:128], statc[:, :], identf[:]), inc="TM")
    s.wait("vector", "TM", ck["ln1_tr"])
    ck["ln1_trev"] = s.op("vector", lambda e: e.tensor_copy(
        stT[:], psT[0][0:64, 0:128]), inc="VE")
    s.wait("gpsimd", "VE", ck["ln1_trev"])
    for src_col, ri in ((16, 0), (48, 1), (32, 2), (0, 3)):
        outap = bass.AP(rowsb, ri * T, [[rowsb.shape[1], 1], [1, T]])
        s.dma("gpsimd", lambda e, sc_=src_col, o=outap: e.dma_start(
            out=o, in_=stT[sc_:sc_ + 16, :]), inc="RD")
    ck["ln1_rows"] = s.cp("RD")

    prev_bc_ev = [None]

    def bcast_row(row_off, dst):
        if prev_bc_ev[0] is not None:
            s.wait("tensor", "VE", prev_bc_ev[0])
        for i in range(4):
            cpt = s.op("tensor", lambda e, i=i: e.matmul(
                psBC[:, i * 512:(i + 1) * 512], onesb[0:1, :],
                rowsb[0:1, row_off + i * 512:row_off + (i + 1) * 512],
                start=True, stop=True), inc="TM" if i == 3 else None)
        s.wait("vector", "TM", cpt)
        prev_bc_ev[0] = s.op("vector", lambda e, dst=dst: e.tensor_copy(
            dst[:], psBC[:]), inc="VE")
        return prev_bc_ev[0]

    s.wait("tensor", "RD", ck["ln1_rows"])
    bcast_row(0 * T, lnA)
    bcast_row(1 * T, lnB)
    bcast_row(2 * T, stdb)
    ck["ln1_bcast"] = bcast_row(3 * T, meanb)

    s.wait("vector", "VE", ck["ln1_bcast"])
    for c in range(C):
        sl = slice(c * T, (c + 1) * T)
        mpc = s.op("vector", lambda e, sl=sl: e.tensor_mul(
            xmain[:, sl], xmain[:, sl], lnA[:]), inc="VE")
    s.wait("vector", "VE", mpc)
    for c in range(C):
        sl = slice(c * T, (c + 1) * T)
        apl = s.op("vector", lambda e, sl=sl: e.tensor_add(
            xmain[:, sl], xmain[:, sl], lnB[:]), inc="VE")
    ck["ln1_apply"] = apl
    if upto == "ln1":
        return _finish_debug(nc, s, sems, dict(xmain=xmain, lnA=lnA, lnB=lnB,
                                               stdb=stdb, meanb=meanb,
                                               statc=statc, rowsb=rowsb))

    wload = {}

    def load_w_chunk(m, slot):
        pair, which = (m, "q") if m < 8 else (m - 8, "k")
        if pair >= 2:
            s.wait("gpsimd", "VE", qk_evac[(pair - 2, which)])
        inap = bass.AP(wqkv_h.tensor, m * 128,
                       [[3072, 128], [3072 * 128, 8], [1, 128]])
        wload[m] = (s.dma("gpsimd", lambda e, i=inap, slot=slot: e.dma_start(
            out=wring[slot][:], in_=i), inc=f"LG{slot}"), f"LG{slot}")

    qk_evac = {}
    pair_last_qkt = {}
    prev_misc_user = [None]

    def emit_qk_half(pair, which, half):
        m = pair if which == "q" else 8 + pair
        slot = (2 * pair + (0 if which == "q" else 1)) % 4
        s.wait("tensor", wload[m][1], wload[m][0])
        if pair == 0 and which == "q" and half == 0:
            s.wait("tensor", "VE", ck["ln1_apply"])
        if prev_misc_user[0] is not None:
            s.wait("tensor", "VE", prev_misc_user[0])
        for nt in range(2):
            for kc in range(C):
                cpt = s.op("tensor", lambda e, nt=nt, kc=kc, slot=slot, half=half:
                           e.matmul(psmisc[:, nt * 512:(nt + 1) * 512],
                                    wring[slot][:, kc * 128:(kc + 1) * 128],
                                    xmain[:, kc * T + half * 1024 + nt * 512:
                                          kc * T + half * 1024 + (nt + 1) * 512],
                                    start=(kc == 0), stop=(kc == C - 1)),
                           inc="TM" if (nt == 1 and kc == C - 1) else None)
        s.wait("vector", "TM", cpt)
        if (pair - 2) in pair_last_qkt:
            s.wait("vector", "TM", pair_last_qkt[pair - 2])
        hsl = slice(half * 1024, (half + 1) * 1024)
        e0 = scr[which + "0"][pair % 2]
        e1 = scr[which + "1"][pair % 2]
        s.op("vector", lambda e, e0=e0, hsl=hsl, m=m: e.tensor_scalar(
            e0[0:64, hsl], psmisc[0:64, :], bqk_s[0:64, m:m + 1], None, ALU.add))
        vcp = s.op("vector", lambda e, e1=e1, hsl=hsl, m=m: e.tensor_scalar(
            e1[64:128, hsl], psmisc[64:128, :], bqk_s[64:128, m:m + 1], None,
            ALU.add), inc="VE")
        prev_misc_user[0] = vcp
        if half == 1:
            qk_evac[(pair, which)] = vcp

    dup_cp = {}

    def emit_dups(pair):
        sm = "DPa" if pair % 2 == 0 else "DPb"
        s.wait("sync", "VE", max(qk_evac[(pair, "q")], qk_evac[(pair, "k")]))
        for which in ("q", "k"):
            t0 = scr[which + "0"][pair % 2]
            t1 = scr[which + "1"][pair % 2]
            s.dma("sync", lambda e, t=t0: e.dma_start(out=t[64:128, :],
                                                      in_=t[0:64, :]), inc=sm)
            cpg = s.dma("sync", lambda e, t=t1: e.dma_start(out=t[0:64, :],
                                                            in_=t[64:128, :]),
                        inc=sm)
        dup_cp[pair] = (cpg, sm)

    load_w_chunk(0, 0)
    load_w_chunk(8, 1)
    for which in ("q", "k"):
        for half in range(2):
            emit_qk_half(0, which, half)
    emit_dups(0)
    if upto == "qk0":
        return _finish_debug(nc, s, sems, dict(
            q0=scr["q0"][0], q1=scr["q1"][0], k0=scr["k0"][0], k1=scr["k1"][0]))

    vevac = []
    for jc in range(TC):
        ps = slab[jc % 2]
        if jc == 0:
            s.wait("tensor", "LGC", s.cp("LGC"))
        if jc >= 2:
            s.wait("tensor", "VE", vevac[jc - 2])
        for nt in range(2):
            for kc in range(C):
                s.op("tensor", lambda e, ps=ps, nt=nt, kc=kc, jc=jc: e.matmul(
                    ps[:, nt * 512:(nt + 1) * 512],
                    xmain[:, kc * T + jc * 128:kc * T + (jc + 1) * 128],
                    wvfull[:, kc * 1024 + nt * 512:kc * 1024 + (nt + 1) * 512],
                    start=(kc == 0), stop=False))
            cpt = s.op("tensor", lambda e, ps=ps, nt=nt: e.matmul(
                ps[:, nt * 512:(nt + 1) * 512], onesb[0:1, :],
                bvrow[0:1, nt * 512:(nt + 1) * 512], start=False, stop=True),
                inc="TM" if nt == 1 else None)
        s.wait("vector", "TM", cpt)
        oap = vones[:, jc * 1040:jc * 1040 + 1040].rearrange(
            "p (h f) -> p h f", f=65)[:, :, 0:64]
        iap = ps[:, :].rearrange("p (h f) -> p h f", f=64)
        vevac.append(s.op("vector", lambda e, o=oap, i=iap: e.tensor_copy(o, i),
                          inc="VE"))
    ck["v_done"] = s.cp("VE")
    if upto == "v":
        return _finish_debug(nc, s, sems, dict(vones=vones))

    bg = []
    for pair in range(1, 8):
        bg.append(("loadw", pair, "q"))
        bg.append(("qk", pair, "q", 0))
        bg.append(("qk", pair, "q", 1))
        bg.append(("loadw", pair, "k"))
        bg.append(("qk", pair, "k", 0))
        bg.append(("qk", pair, "k", 1))
        bg.append(("dups", pair))
    bg_i = [0]

    def run_bg(n):
        for _ in range(n):
            if bg_i[0] >= len(bg):
                return
            item = bg[bg_i[0]]
            bg_i[0] += 1
            if item[0] == "loadw":
                _, pair, which = item
                load_w_chunk(pair if which == "q" else 8 + pair,
                             (2 * pair + (0 if which == "q" else 1)) % 4)
            elif item[0] == "qk":
                emit_qk_half(item[1], item[2], item[3])
            else:
                emit_dups(item[1])

    exp_cp, qkt_cp, pv_cp, tt_cp = {}, {}, {}, {}
    stg_dma_cp = {}
    last_ttdma = [0]
    nslab = 0
    dbg_single = upto in ("attn1", "attn1b")
    npair = 1 if dbg_single else 8
    neip = 1 if dbg_single else 2
    nih = 1 if upto == "attn1" else 2
    for pair in range(npair):
        for eip in range(neip):
            h = 2 * pair + eip
            sq0 = scr["q" + str(eip)][pair % 2]
            sk0 = scr["k" + str(eip)][pair % 2]
            if not dbg_single:
                run_bg(0 if h == 0 else (4 if h % 2 == 1 else 3))
            for ihalf in range(nih):
                g = h * 2 + ihalf
                g_start = nslab

                def emit_pv(n2, jc2, h=h, g=g):
                    pt2 = n2 % NPT
                    s.wait("tensor", "SC", exp_cp[n2])
                    if jc2 == 0 and g > 0:
                        s.wait("tensor", "VE", tt_cp[g - 1])
                    for it in range(2):
                        pvc = s.op("tensor", lambda e, it=it, jc=jc2, h=h,
                                   pt=pt2: e.matmul(
                                       pspv[0:65, it * 512:(it + 1) * 512],
                                       vones[:, jc * 1040 + h * 65:
                                             jc * 1040 + h * 65 + 65],
                                       pT[pt][:, it * 512:(it + 1) * 512],
                                       start=(jc == 0), stop=(jc == TC - 1)),
                                   inc="TM" if it == 1 else None)
                    pv_cp[n2] = pvc

                LAG = 2
                for jc in range(TC):
                    n = nslab
                    nslab += 1
                    sb = n % 2
                    pt = n % NPT
                    if jc == 0 and ihalf == 0:
                        s.wait("tensor", dup_cp[pair][1], dup_cp[pair][0])
                        if h == 0:
                            s.wait("tensor", "VE", ck["v_done"])
                    if n >= 2:
                        s.wait("tensor", "SC", exp_cp[n - 2])
                    s.op("tensor", lambda e, sb=sb, sk0=sk0, sq0=sq0, jc=jc,
                         ihalf=ihalf: e.matmul(
                             slab[sb][:, 0:512],
                             sk0[0:64, jc * 128:(jc + 1) * 128],
                             sq0[0:64, ihalf * 1024:ihalf * 1024 + 512],
                             start=True, stop=True, tile_position=(0, 0)))
                    qkt_cp[n] = s.op("tensor", lambda e, sb=sb, sk0=sk0, sq0=sq0,
                                     jc=jc, ihalf=ihalf: e.matmul(
                                         slab[sb][:, 512:1024],
                                         sk0[64:128, jc * 128:(jc + 1) * 128],
                                         sq0[64:128, ihalf * 1024 + 512:
                                             ihalf * 1024 + 1024],
                                         start=True, stop=True,
                                         tile_position=(64, 0)), inc="TM")
                    s.wait("scalar", "TM", qkt_cp[n])
                    if n >= NPT:
                        s.wait("scalar", "TM", pv_cp[n - NPT])
                    exp_cp[n] = s.op("scalar", lambda e, pt=pt, sb=sb:
                                     e.activation(pT[pt][:], slab[sb][:], AF.Exp,
                                                  bias=neg30[:]), inc="SC")
                    if jc >= LAG:
                        emit_pv(n - LAG, jc - LAG)
                for jc in range(TC - LAG, TC):
                    emit_pv(g_start + jc, jc)
                s.wait("vector", "TM", pv_cp[nslab - 1])
                ecp = s.op("vector", lambda e, g=g: e.tensor_copy(
                    ub[g % 2][0:65, :], pspv[0:65, :]), inc="VE")
                s.wait("scalar", "VE", ecp)
                lcp = s.op("scalar", lambda e, g=g: e.activation(
                    rrow[64:65, :], ub[g % 2][64:65, :], AF.Ln), inc="SC")
                s.wait("scalar", "SC", lcp)
                rcp = s.op("scalar", lambda e: e.activation(
                    rrow[64:65, :], rrow[64:65, :], AF.Exp, scale=-1.0),
                    inc="SC")
                s.wait("tensor", "SC", rcp)
                s.wait("tensor", "VE", ecp)
                for it in range(2):
                    bcp = s.op("tensor", lambda e, it=it: e.matmul(
                        pspv[0:64, it * 512:(it + 1) * 512], onesb[64:65, 0:64],
                        rrow[64:65, it * 512:(it + 1) * 512],
                        start=True, stop=True, tile_position=(64, 0)),
                        inc="TM" if it == 1 else None)
                s.wait("vector", "TM", bcp)
                s.wait("vector", "VE", ecp)
                for par in range(2):
                    src0 = ub[g % 2][0:64, :].rearrange(
                        "p (tb f) -> p f tb", f=16).rearrange(
                        "p (cc pr) tb -> p pr cc tb", pr=2)[:, par]
                    src1 = pspv[0:64, :].rearrange(
                        "p (tb f) -> p f tb", f=16).rearrange(
                        "p (cc pr) tb -> p pr cc tb", pr=2)[:, par]
                    dst = attnb[par * 64:(par + 1) * 64, :].rearrange(
                        "p (cc t) -> p cc t", t=T)[
                        :, :, h * 128 + ihalf * 64:h * 128 + ihalf * 64 + 64]
                    tt_cp[g] = s.op("vector", lambda e, o=dst, a=src0, b=src1:
                                    e.tensor_mul(o, a, b),
                                    inc="VE" if par == 1 else None)
            if eip == 1:
                pair_last_qkt[pair] = qkt_cp[nslab - 1]
    if not dbg_single:
        run_bg(100)
    ck["attn_ve"] = s.cp("VE")
    ck["attn_td"] = s.cp("TD")
    ck["attn_xd"] = s.cp("XD")
    ck["attn_sc"] = s.cp("SC")
    ck["attn_tm"] = s.cp("TM")
    if dbg_single or upto == "attn":
        return _finish_debug(nc, s, sems, dict(attnb=attnb, ub0=ub[0]))

    def load_wffn(src_h, dst, sem, wait_tm=None, wait_ve=None, wait_sc=None):
        if wait_tm:
            s.wait("gpsimd", "TM", wait_tm)
        if wait_ve:
            s.wait("gpsimd", "VE", wait_ve)
        if wait_sc:
            s.wait("gpsimd", "SC", wait_sc)
        for kc in range(C):
            inap = bass.AP(src_h.tensor, kc * 128 * D, [[D, 128], [1, 1024]])
            cpl = s.dma("gpsimd", lambda e, i=inap, kc=kc: e.dma_start(
                out=dst[:, kc * 1024:(kc + 1) * 1024], in_=i), inc=sem)
        return (cpl, sem)

    ck["wrt"] = load_wffn(wrt_h, wslotA, "W1", wait_tm=ck["attn_tm"],
                          wait_ve=ck["attn_ve"], wait_sc=ck["attn_sc"])

    def ln_feature_major(src_tensor, extra_ve_wait=None):
        s2cp = None
        for c in range(C):
            sl = slice(c * T, (c + 1) * T)
            buf = tmpc if c % 2 == 0 else xpc
            if extra_ve_wait and c == 0:
                s.wait("vector", "VE", extra_ve_wait)
            if c >= 2:
                s.wait("vector", "TM", s2cp)
            sqv = s.op("vector", lambda e, sl=sl, buf=buf: e.tensor_mul(
                buf[:], src_tensor[:, sl], src_tensor[:, sl]), inc="VE")
            s.wait("tensor", "VE", sqv)
            for i in range(4):
                s.op("tensor", lambda e, c=c, i=i: e.matmul(
                    psA[32 * i:32 * i + 1, 0:512], onesb[:, 0:1],
                    src_tensor[:, c * T + i * 512:c * T + (i + 1) * 512],
                    start=(c == 0), stop=(c == C - 1), tile_position=(0, 32 * i)))
            for i in range(4):
                s2cp = s.op("tensor", lambda e, i=i, buf=buf, c=c: e.matmul(
                    psA[32 * i:32 * i + 1, 512:1024], onesb[:, 0:1],
                    buf[:, i * 512:(i + 1) * 512],
                    start=(c == 0), stop=(c == C - 1), tile_position=(0, 32 * i)),
                    inc="TM" if i == 3 else None)
        s.wait("vector", "TM", s2cp)
        a1 = s.op("vector", lambda e: e.tensor_scalar_mul(st_a[:], psA[:, 0:512],
                                                          1.0 / D), inc="VE")
        a2 = s.op("vector", lambda e: e.tensor_scalar_mul(st_b[:],
                                                          psA[:, 512:1024],
                                                          1.0 / D), inc="VE")
        s.wait("vector", "VE", a1)
        a3 = s.op("vector", lambda e: e.tensor_mul(st_c[:], st_a[:], st_a[:]),
                  inc="VE")
        s.wait("vector", "VE", a3)
        vcp = s.op("vector", lambda e: e.tensor_sub(st_b[:], st_b[:], st_c[:]),
                   inc="VE")
        s.wait("scalar", "VE", vcp)
        for r in (0, 32, 64, 96):
            lcp = s.op("scalar", lambda e, r=r: e.activation(
                st_b[r:r + 1, :], st_b[r:r + 1, :], AF.Ln, bias=epsc[r:r + 1]),
                inc="SC" if r == 96 else None)
        s.wait("scalar", "SC", lcp)
        scp = s.op("scalar", lambda e: e.activation(sAr[:], st_b[:], AF.Exp,
                                                    scale=-0.5), inc="SC")
        s.wait("vector", "SC", scp)
        vcp2 = s.op("vector", lambda e: e.scalar_tensor_tensor(
            sBr[:], st_a[:], -1.0, sAr[:], ALU.mult, ALU.mult), inc="VE")
        s.wait("tensor", "VE", vcp2)
        for i in range(4):
            acp = s.op("tensor", lambda e, i=i: e.matmul(
                psB[:, i * 512:(i + 1) * 512], onesb[32 * i:32 * i + 1, :],
                sAr[32 * i:32 * i + 1, :], start=True, stop=True,
                tile_position=(32 * i, 0)), inc="TM" if i == 3 else None)
        s.wait("vector", "TM", acp)
        va = s.op("vector", lambda e: e.tensor_copy(lnA[:], psB[:]), inc="VE")
        s.wait("tensor", "VE", va)
        for i in range(4):
            bcp = s.op("tensor", lambda e, i=i: e.matmul(
                psB[:, i * 512:(i + 1) * 512], onesb[32 * i:32 * i + 1, :],
                sBr[32 * i:32 * i + 1, :], start=True, stop=True,
                tile_position=(32 * i, 0)), inc="TM" if i == 3 else None)
        s.wait("vector", "TM", bcp)
        return s.op("vector", lambda e: e.tensor_copy(lnB[:], psB[:]), inc="VE")

    s.wait("vector", "VE", ck["attn_ve"])
    s.wait("vector", "TD", ck["attn_td"])
    s.wait("vector", "XD", ck["attn_xd"])
    s.wait("tensor", "SC", ck["attn_sc"])
    ln2 = ln_feature_major(attnb)
    s.wait("vector", "VE", ln2)
    bmcp = s.op("vector", lambda e: e.tensor_add(lnBM[:], lnB[:], meanb[:]),
                inc="VE")
    s.wait("vector", "VE", bmcp)
    for c in range(C):
        sl = slice(c * T, (c + 1) * T)
        p1 = s.op("vector", lambda e, sl=sl: e.tensor_mul(
            attnb[:, sl], attnb[:, sl], lnA[:]), inc="VE")
    s.wait("vector", "VE", p1)
    for c in range(C):
        sl = slice(c * T, (c + 1) * T)
        p2 = s.op("vector", lambda e, sl=sl: e.tensor_add(
            attnb[:, sl], attnb[:, sl], lnBM[:]), inc="VE")
    for c in range(C):
        sl = slice(c * T, (c + 1) * T)
        p3 = s.op("vector", lambda e, sl=sl: e.tensor_mul(
            xmain[:, sl], xmain[:, sl], stdb[:]), inc="VE")
    s.wait("vector", "VE", p3)
    for c in range(C):
        sl = slice(c * T, (c + 1) * T)
        x2cp = s.op("vector", lambda e, sl=sl: e.tensor_add(
            xmain[:, sl], xmain[:, sl], attnb[:, sl]), inc="VE")
    ck["x2"] = x2cp
    if upto == "x2":
        return _finish_debug(nc, s, sems, dict(xmain=xmain))

    ln3 = ln_feature_major(xmain, extra_ve_wait=ck["x2"])
    s.wait("vector", "VE", ln3)
    for c in range(C):
        sl = slice(c * T, (c + 1) * T)
        l3p = s.op("vector", lambda e, sl=sl: e.tensor_mul(
            attnb[:, sl], xmain[:, sl], lnA[:]), inc="VE")
    s.wait("vector", "VE", l3p)
    for c in range(C):
        sl = slice(c * T, (c + 1) * T)
        xn3cp = s.op("vector", lambda e, sl=sl: e.tensor_add(
            attnb[:, sl], attnb[:, sl], lnB[:]), inc="VE")
    ck["xn3"] = xn3cp
    if upto == "xn3":
        return _finish_debug(nc, s, sems, dict(attnb=attnb))

    ck["wkt"] = load_wffn(wkt_h, wslotB, "W2", wait_ve=ck["xn3"])

    def emit_mix(base, dst, wait_ve_first):
        mcps = []
        tcps = {}
        for c in range(C):
            sl = slice(c * T, (c + 1) * T)
            mc = base + c
            buf = tmpc2 if c % 2 == 0 else xpc
            if c == 0:
                s.wait("vector", "VE", wait_ve_first)
                s.op("vector", lambda e: e.memset(tmpc2[:, 0:1], 0.0))
                s.op("vector", lambda e: e.memset(xpc[:, 0:1], 0.0))
            if c >= 2:
                s.wait("vector", "VE", mcps[c - 2])
            tcps[c] = s.op("vector", lambda e, mc=mc, c=c, buf=buf:
                           e.tensor_scalar(
                               buf[:, 1:T], attnb[:, c * T:c * T + T - 1],
                               mu1m[:, mc:mc + 1], None, ALU.mult), inc="VE")
            s.wait("vector", "VE", tcps[c])
            mcps.append(s.op("vector", lambda e, sl=sl, mc=mc, dst=dst,
                             buf=buf: e.scalar_tensor_tensor(
                                 dst[:, sl], attnb[:, sl], mu_s[:, mc:mc + 1],
                                 buf[:], ALU.mult, ALU.add), inc="VE"))
        return mcps[-1]

    ck["xr"] = emit_mix(0, xr_t, ck["xn3"])

    def ffn_mm(wslot, src, dc, wait_lg):
        ps = psA if dc % 2 == 0 else psB
        s.wait("tensor", wait_lg[1], wait_lg[0])
        for nt in range(4):
            for kc in range(C):
                cpt = s.op("tensor", lambda e, ps=ps, nt=nt, kc=kc, dc=dc,
                           wslot=wslot, src=src: e.matmul(
                               ps[:, nt * 512:(nt + 1) * 512],
                               wslot[:, kc * 1024 + dc * 128:
                                     kc * 1024 + (dc + 1) * 128],
                               src[:, kc * T + nt * 512:kc * T + (nt + 1) * 512],
                               start=(kc == 0), stop=(kc == C - 1)),
                           inc="TM" if (nt == 3 and kc == C - 1) else None)
        return cpt, ps

    rexp_cp, rmm_cp = {}, {}
    for dc in range(C):
        if dc == 0:
            s.wait("tensor", "VE", ck["xr"])
        if dc >= 2:
            s.wait("tensor", "SC", rexp_cp[dc - 2])
        rmm_cp[dc], ps = ffn_mm(wslotA, xr_t, dc, ck["wrt"])
        s.wait("scalar", "TM", rmm_cp[dc])
        rexp_cp[dc] = s.op("scalar", lambda e, ps=ps, dc=dc: e.activation(
            rsig_t[:, dc * T:(dc + 1) * T], ps[:], AF.Sigmoid), inc="SC")
    ck["rsig"] = rexp_cp[C - 1]
    ck["rmm_last"] = rmm_cp[C - 1]
    if upto == "rsig":
        return _finish_debug(nc, s, sems, dict(rsig=rsig_t, xr=xr_t))

    ck["wvt"] = load_wffn(wvt_h, wslotA, "W1", wait_tm=ck["rmm_last"])

    s.wait("vector", "TM", ck["rmm_last"])
    ck["xk"] = emit_mix(8, xr_t, ck["xn3"])

    kact_cp = {}
    krl_cp = {}
    for dc in range(C):
        if dc == 0:
            s.wait("tensor", "VE", ck["xk"])
            s.wait("tensor", "SC", rexp_cp[C - 1])
        if dc >= 2:
            s.wait("tensor", "VE", kact_cp[dc - 2])
        kcp, ps = ffn_mm(wslotB, xr_t, dc, ck["wkt"])
        kbuf = tmpc if dc % 2 == 0 else tmpc2
        s.wait("scalar", "TM", kcp)
        if dc >= 2:
            s.wait("scalar", "VE", kact_cp[dc - 2])
        krl_cp[dc] = s.op("scalar", lambda e, ps=ps, kbuf=kbuf: e.activation(
            kbuf[:], ps[:], AF.Relu), inc="SC")
        s.wait("vector", "SC", krl_cp[dc])
        kact_cp[dc] = s.op("vector", lambda e, dc=dc, kbuf=kbuf: e.tensor_mul(
            attnb[:, dc * T:(dc + 1) * T], kbuf[:], kbuf[:]), inc="VE")
    ck["k2"] = kact_cp[C - 1]
    if upto == "k2":
        return _finish_debug(nc, s, sems, dict(k2=attnb))

    fout_cp = {}
    for dc in range(C):
        if dc == 0:
            s.wait("tensor", "VE", ck["k2"])
            s.wait("tensor", "SC", ck["rsig"])
        if dc >= 2:
            s.wait("tensor", "VE", fout_cp[dc - 2])
        vcp, ps = ffn_mm(wslotA, attnb, dc, ck["wvt"])
        s.wait("vector", "TM", vcp)
        fout_cp[dc] = s.op("vector", lambda e, ps=ps, dc=dc: e.tensor_mul(
            xr_t[:, dc * T:(dc + 1) * T], ps[:],
            rsig_t[:, dc * T:(dc + 1) * T]), inc="VE")
    ck["ffnout"] = fout_cp[C - 1]
    if upto == "ffnout":
        return _finish_debug(nc, s, sems, dict(ffnout=xr_t))

    ln4 = ln_feature_major(xr_t, extra_ve_wait=ck["ffnout"])
    s.wait("vector", "VE", ln4)
    for c in range(C):
        sl = slice(c * T, (c + 1) * T)
        l4p = s.op("vector", lambda e, sl=sl: e.tensor_mul(
            xr_t[:, sl], xr_t[:, sl], lnA[:]), inc="VE")
    s.wait("vector", "VE", l4p)
    for c in range(C):
        sl = slice(c * T, (c + 1) * T)
        lncp4 = s.op("vector", lambda e, sl=sl: e.tensor_add(
            xr_t[:, sl], xr_t[:, sl], lnB[:]), inc="VE")
    outst = []
    outdma = []
    stcp = None
    for c in range(C):
        sl = slice(c * T, (c + 1) * T)
        ps = psA if c % 2 == 0 else psB
        ot = otstage if c % 2 == 0 else otstage2
        odeng = "sync" if c % 2 == 0 else "scalar"
        odsem = "OD" if c % 2 == 0 else "OD2"
        if c == 0:
            s.wait("vector", "VE", lncp4)
        else:
            s.wait("vector", "TM", prev_trcp)
        ocp = s.op("vector", lambda e, sl=sl: e.tensor_add(
            ostage[:], xmain[:, sl], xr_t[:, sl]), inc="VE")
        s.wait("tensor", "VE", ocp)
        if c >= 2:
            s.wait("tensor", "VE", outst[c - 2])
        for t in range(TC):
            trcp = s.op("tensor", lambda e, ps=ps, t=t: e.transpose(
                ps[:, t * 128:(t + 1) * 128], ostage[:, t * 128:(t + 1) * 128],
                identf[:]), inc="TM" if t == TC - 1 else None)
        prev_trcp = trcp
        s.wait("vector", "TM", trcp)
        if c >= 2:
            s.wait("vector", odsem, outdma[c - 2])
        vst = s.op("vector", lambda e, ps=ps, ot=ot: e.tensor_copy(
            ot[:], ps[:]), inc="VE")
        outst.append(vst)
        s.wait(odeng, "VE", vst)
        outap = bass.AP(out_h.tensor, c * 128,
                        [[D, 128], [128 * D, 16], [1, 128]])
        stcp = s.dma(odeng, lambda e, o=outap, ot=ot: e.dma_start(
            out=o, in_=ot[:]), inc=odsem)
        outdma.append(stcp)
    s.wait("gpsimd", "OD", s.cp("OD"))
    s.wait("gpsimd", "OD2", s.cp("OD2"))
    s.wait("sync", "OD", s.cp("OD"))
    s.wait("sync", "OD2", s.cp("OD2"))

    with nc.allow_low_precision("bf16 intermediates; fp32 psum accumulation"), \
         nc.Block() as blk:
        blk.tensor(s.run("tensor"))
        blk.scalar(s.run("scalar"))
        blk.vector(s.run("vector"))
        blk.sync(s.run("sync"))
        blk.gpsimd(s.run("gpsimd"))
    return nc


_NC_CACHE = {}


def _get_program():
    if "nc" not in _NC_CACHE:
        _NC_CACHE["nc"] = build_program()
    return _NC_CACHE["nc"]


def kernel(L0_attn_x, L0_attn_W_qkv, L0_attn_b_qkv, L0_ffn_W_r, L0_ffn_W_k,
           L0_ffn_W_v, L0_ffn_mu_r, L0_ffn_mu_k):
    x = np.asarray(L0_attn_x, dtype=np.float32)
    wqkv = np.ascontiguousarray(np.asarray(L0_attn_W_qkv, dtype=np.float32))
    bqkv = np.asarray(L0_attn_b_qkv, dtype=np.float32)
    bqk = np.ascontiguousarray(bqkv[:2048].reshape(16, 128).T)
    bv = np.ascontiguousarray(bqkv[2048:].reshape(1, 1024))
    wrt = np.ascontiguousarray(np.asarray(L0_ffn_W_r, dtype=np.float32).T)
    wkt = np.ascontiguousarray(np.asarray(L0_ffn_W_k, dtype=np.float32).T)
    wvt = np.ascontiguousarray(np.asarray(L0_ffn_W_v, dtype=np.float32).T)
    mu_r = np.asarray(L0_ffn_mu_r, dtype=np.float32)
    mu_k = np.asarray(L0_ffn_mu_k, dtype=np.float32)
    mu = np.ascontiguousarray(
        np.concatenate([mu_r.reshape(8, 128).T, mu_k.reshape(8, 128).T], axis=1))

    nc = _get_program()
    in_maps = []
    for b in range(8):
        in_maps.append({
            "x": np.ascontiguousarray(x[b]),
            "wqkv": wqkv, "bqk": bqk, "bv": bv,
            "wrt": wrt, "wkt": wkt, "wvt": wvt, "mu": mu,
        })
    res = run_bass_kernel_spmd(nc, in_maps, core_ids=list(range(8)))
    out = np.stack([res.results[b]["out"] for b in range(8)], axis=0)
    return out.astype(np.float32)


# version 09
# speedup vs baseline: 1.8280x; 1.0002x over previous
import numpy as np
import ml_dtypes
import concourse.bass as bass
import concourse.mybir as mybir
from concourse.bass_utils import run_bass_kernel_spmd

F32 = mybir.dt.float32
BF16 = mybir.dt.bfloat16
AF = mybir.ActivationFunctionType
ALU = mybir.AluOpType
AX = mybir.AxisListType
bf16 = ml_dtypes.bfloat16

T = 2048
D = 1024
C = 8
TC = 16
NPT = 5

ENGS = ("tensor", "scalar", "vector", "sync", "gpsimd")


class Sched:
    def __init__(self, sems):
        self.ops = {e: [] for e in ENGS}
        self.cnt = {n: 0 for n in sems}
        self.sems = sems

    def op(self, eng, fn, inc=None, incval=1):
        if inc is not None:
            self.cnt[inc] += incval
            self.ops[eng].append(("op", fn, self.sems[inc], incval))
            return self.cnt[inc]
        self.ops[eng].append(("op", fn, None, 0))
        return None

    def dma(self, eng, fn, inc):
        return self.op(eng, fn, inc=inc, incval=16)

    def wait(self, eng, sem_name, val):
        if val is None or val <= 0:
            return
        self.ops[eng].append(("wait", self.sems[sem_name], val))

    def cp(self, sem_name):
        return self.cnt[sem_name]

    def run(self, eng_name):
        def body(e):
            for item in self.ops[eng_name]:
                if item[0] == "wait":
                    e.wait_ge(item[1], item[2])
                else:
                    _, fn, sem, incval = item
                    r = fn(e)
                    if sem is not None:
                        r.then_inc(sem, incval)
        return body


def _finish_debug(nc, s, sems, taps):
    outs = {}
    for name, t in taps.items():
        d = nc.dram_tensor("tap_" + name, [t.shape[0], int(np.prod(t.shape[1:]))],
                           t.dtype, kind="ExternalOutput").ap()
        outs[name] = d
    for sem_name in s.cnt:
        if sem_name != "OD":
            s.wait("sync", sem_name, s.cp(sem_name))
    last = None
    for name, t in taps.items():
        last = s.dma("sync", lambda e, t=t, d=outs[name]: e.dma_start(
            out=d, in_=t[:, :]), inc="OD")
    s.wait("sync", "OD", last)
    s.wait("gpsimd", "OD", last)
    with nc.allow_low_precision("debug"), nc.Block() as blk:
        blk.tensor(s.run("tensor"))
        blk.scalar(s.run("scalar"))
        blk.vector(s.run("vector"))
        blk.sync(s.run("sync"))
        blk.gpsimd(s.run("gpsimd"))
    return nc


def build_program(upto=None):
    nc = bass.Bass("TRN2", target_bir_lowering=False)

    x_h = nc.dram_tensor("x", [T, D], F32, kind="ExternalInput").ap()
    wqkv_h = nc.dram_tensor("wqkv", [D, 3 * D], F32, kind="ExternalInput").ap()
    bqk_h = nc.dram_tensor("bqk", [128, 16], F32, kind="ExternalInput").ap()
    bv_h = nc.dram_tensor("bv", [1, D], F32, kind="ExternalInput").ap()
    wrt_h = nc.dram_tensor("wrt", [D, D], F32, kind="ExternalInput").ap()
    wkt_h = nc.dram_tensor("wkt", [D, D], F32, kind="ExternalInput").ap()
    wvt_h = nc.dram_tensor("wvt", [D, D], F32, kind="ExternalInput").ap()
    mu_h = nc.dram_tensor("mu", [128, 16], F32, kind="ExternalInput").ap()
    out_h = nc.dram_tensor("out", [T, D], F32, kind="ExternalOutput").ap()

    SB = nc.sbuf_top
    cur = [nc.sbuf_base]

    def alloc(name, shape, dtype, at=None):
        nb = int(np.prod(shape[1:])) * mybir.dt.size(dtype)
        if at is None:
            at = (cur[0] + 31) // 32 * 32
            cur[0] = at + nb
        return nc.alloc_sbuf_tensor_at(name, shape, dtype, offset=at)

    def addr(t):
        return nc.lookup_mloc(t).addr

    xmain = alloc("xmain", [128, C * T], BF16)
    attnb = alloc("attnb", [128, C * T], BF16)
    vones = alloc("vones", [128, 16 * 1040], BF16)
    scr = {}
    for nm in ("q0", "q1", "k0", "k1"):
        scr[nm] = [alloc(f"scr_{nm}_{i}", [128, T], BF16) for i in range(2)]
    pT = [alloc(f"pT{i}", [128, 1024], BF16) for i in range(NPT)]
    wring = [alloc(f"wring{i}", [128, 1024], BF16) for i in range(4)]
    ub = [alloc(f"ub{i}", [128, 1024], BF16) for i in range(2)]
    ttst = alloc("ttst", [128, 1024], BF16)
    pT.append(nc.alloc_sbuf_tensor_at("pT4", [128, 1024], BF16,
                                      offset=addr(ttst)))
    rrow = alloc("rrow", [128, 1024], BF16)
    lnA = alloc("lnA", [128, T], BF16)
    lnB = alloc("lnB", [128, T], BF16)
    lnBM = alloc("lnBM", [128, T], BF16)
    stdb = alloc("stdb", [128, T], BF16)
    meanb = alloc("meanb", [128, T], BF16)
    st_a = alloc("st_a", [128, 512], F32)
    st_b = alloc("st_b", [128, 512], F32)
    st_c = alloc("st_c", [128, 512], F32)
    sAr = alloc("sAr", [128, 512], BF16)
    sBr = alloc("sBr", [128, 512], BF16)
    onesb = alloc("onesb", [128, 128], BF16)
    identf = alloc("identf", [128, 128], F32)
    bqk_s = alloc("bqk_s", [128, 16], F32)
    mu_s = alloc("mu_s", [128, 16], F32)
    mu1m = alloc("mu1m", [128, 16], F32)
    bvrow = alloc("bvrow", [1, D], BF16)
    neg30 = alloc("neg30", [128, 1], F32)
    epsc = alloc("epsc", [128, 1], F32)
    statc = alloc("statc", [128, 64], F32)
    stT = alloc("stT", [64, 128], F32)
    xpc = alloc("xpc", [128, T], BF16)
    tmpc = alloc("tmpc", [128, T], BF16)
    tmpc2 = alloc("tmpc2", [128, T], BF16)
    ostage = alloc("ostage", [128, T], F32)

    a_at = addr(attnb)
    xstage = [nc.alloc_sbuf_tensor_at(f"xstage{i}", [128, 1024], F32,
                                      offset=a_at + i * 4096) for i in range(2)]
    sqstage = [nc.alloc_sbuf_tensor_at(f"sqstage{i}", [128, 1024], F32,
                                       offset=a_at + 8192 + i * 4096)
               for i in range(2)]
    wvfull = nc.alloc_sbuf_tensor_at("wvfull", [128, 8 * 1024], BF16,
                                     offset=a_at + 16384)
    xr_t = nc.alloc_sbuf_tensor_at("xr_t", [128, C * T], BF16,
                                   offset=addr(scr["q0"][0]))
    rowsb = nc.alloc_sbuf_tensor_at("rowsb", [1, 4 * T], BF16,
                                    offset=addr(scr["k0"][0]))
    rsig_t = nc.alloc_sbuf_tensor_at("rsig_t", [128, C * T], BF16,
                                     offset=addr(vones))
    otstage = nc.alloc_sbuf_tensor_at("otstage", [128, T], F32,
                                      offset=addr(vones))
    otstage2 = nc.alloc_sbuf_tensor_at("otstage2", [128, T], F32,
                                       offset=addr(vones) + 8192)
    wslotA = nc.alloc_sbuf_tensor_at("wslotA", [128, 8 * 1024], BF16,
                                     offset=addr(pT[0]))
    wslotB = nc.alloc_sbuf_tensor_at("wslotB", [128, 8 * 1024], BF16,
                                     offset=addr(stdb))
    assert cur[0] <= SB, (cur[0], SB)

    nc.psum_base = 0
    PS = nc.alloc_psum_tensor("PS", [128, 4096], F32)
    psT = [PS[:, 0:1024], PS[:, 1024:2048]]
    psBC = PS[:, 2048:4096]
    slab = [PS[:, 0:1024], PS[:, 1024:2048]]
    pspv = PS[:, 2048:3072]
    psmisc = PS[:, 3072:4096]
    psA = PS[:, 0:2048]
    psB = PS[:, 2048:4096]

    sems = {n: nc.alloc_semaphore(n) for n in
            ("LD0", "LD1", "LDC", "LGC", "RD", "LG0", "LG1", "LG2", "LG3",
             "DPa", "DPb", "TD", "XD", "OD", "OD2", "W1", "W2",
             "TM", "SC", "VE", "GP")}
    s = Sched(sems)
    ck = {}

    idm = s.op("gpsimd", lambda e: e.memset(identf[:], 0.0), inc="GP")
    s.op("gpsimd", lambda e: e.memset(onesb[:], 1.0))
    s.op("gpsimd", lambda e: e.memset(neg30[:], -30.0))
    s.wait("gpsimd", "GP", idm)
    s.op("gpsimd", lambda e: e.affine_select(
        out=identf[:], in_=identf[:], compare_op=ALU.not_equal, fill=1.0,
        base=0, pattern=[[-1, 128]], channel_multiplier=1))
    for jc in range(16):
        ap = bass.AP(vones, jc * 1040 + 64, [[vones.shape[1], 128], [65, 16]])
        s.op("gpsimd", lambda e, ap=ap: e.memset(ap, 1.0))
    ck["g_consts"] = s.op("gpsimd", lambda e: e.memset(epsc[:], 1e-5), inc="GP")

    s.dma("sync", lambda e: e.dma_start(out=bqk_s[:], in_=bqk_h), inc="LDC")
    ck["ld_consts"] = s.dma("sync", lambda e: e.dma_start(out=mu_s[:], in_=mu_h),
                            inc="LDC")
    s.dma("gpsimd", lambda e: e.dma_start(out=bvrow[:], in_=bv_h), inc="LGC")
    for kc in range(C):
        inap = bass.AP(wqkv_h.tensor, kc * 128 * 3072 + 2048,
                       [[3072, 128], [1, 1024]])
        s.dma("gpsimd", lambda e, i=inap, kc=kc: e.dma_start(
            out=wvfull[:, kc * 1024:(kc + 1) * 1024], in_=i), inc="LGC")
    ck["lg_wv"] = s.cp("LGC")

    s.wait("vector", "LDC", ck["ld_consts"])
    ck["ve_mu"] = s.op("vector", lambda e: e.tensor_scalar(
        mu1m[:], mu_s[:], -1.0, 1.0, ALU.mult, ALU.add), inc="VE")

    xld, trd, evd, sqd = [], [], [], []
    for tc in range(TC):
        b = tc % 2
        lsem = "LD0" if b == 0 else "LD1"
        ldeng = "sync" if b == 0 else "scalar"
        if tc >= 2:
            s.wait(ldeng, "TM", trd[tc - 2])
            s.wait(ldeng, "VE", sqd[tc - 2])
        xld.append(s.dma(ldeng, lambda e, tc=tc, b=b: e.dma_start(
            out=xstage[b][:], in_=x_h[tc * 128:(tc + 1) * 128, :]), inc=lsem))
        s.wait("vector", lsem, xld[tc])
        sqw = s.op("vector", lambda e, b=b: e.tensor_mul(
            sqstage[b][:], xstage[b][:], xstage[b][:]), inc="VE")
        s.op("vector", lambda e, tc=tc, b=b: e.reduce_sum(
            statc[:, tc:tc + 1], xstage[b][:], axis=AX.X))
        s.wait("vector", "VE", sqw)
        sqd.append(s.op("vector", lambda e, tc=tc, b=b: e.reduce_sum(
            statc[:, 32 + tc:33 + tc], sqstage[b][:], axis=AX.X), inc="VE"))
        s.wait("tensor", lsem, xld[tc])
        if tc == 0:
            s.wait("tensor", "GP", ck["g_consts"])
        if tc >= 2:
            s.wait("tensor", "VE", evd[tc - 2])
        for c in range(C):
            trc = s.op("tensor", lambda e, b=b, c=c: e.transpose(
                psT[b][:, c * 128:(c + 1) * 128],
                xstage[b][:, c * 128:(c + 1) * 128], identf[:]),
                inc="TM" if c == C - 1 else None)
        trd.append(trc)
        s.wait("vector", "TM", trd[tc])
        oap = xmain[:, :].rearrange("p (c t) -> p c t", t=T)[:, :, tc * 128:
                                                            (tc + 1) * 128]
        iap = psT[b][:, :].rearrange("p (c f) -> p c f", f=128)
        evd.append(s.op("vector", lambda e, o=oap, i=iap: e.tensor_copy(o, i),
                        inc="VE"))
    ck["x_in"] = s.cp("VE")
    if upto == "stats_raw":
        return _finish_debug(nc, s, sems, dict(statc=statc))

    s.wait("vector", "VE", ck["x_in"])
    c1 = s.op("vector", lambda e: e.tensor_scalar_mul(
        statc[:, 0:16], statc[:, 0:16], 1.0 / D), inc="VE")
    c2 = s.op("vector", lambda e: e.tensor_scalar_mul(
        statc[:, 32:48], statc[:, 32:48], 1.0 / D), inc="VE")
    s.wait("scalar", "VE", c1)
    m2cp = s.op("scalar", lambda e: e.activation(st_b[:, 0:16], statc[:, 0:16],
                                                 AF.Square), inc="SC")
    s.wait("vector", "SC", m2cp)
    s.wait("vector", "VE", c2)
    ck["ln1_var"] = s.op("vector", lambda e: e.tensor_sub(
        statc[:, 32:48], statc[:, 32:48], st_b[:, 0:16]), inc="VE")
    s.wait("scalar", "GP", ck["g_consts"])
    s.wait("scalar", "VE", ck["ln1_var"])
    lncp = s.op("scalar", lambda e: e.activation(statc[:, 32:48], statc[:, 32:48],
                                                 AF.Ln, bias=epsc[:]), inc="SC")
    s.wait("scalar", "SC", lncp)
    rcp1 = s.op("scalar", lambda e: e.activation(statc[:, 16:32], statc[:, 32:48],
                                                 AF.Exp, scale=-0.5), inc="SC")
    s.wait("scalar", "SC", rcp1)
    ck["ln1_se"] = s.op("scalar", lambda e: e.activation(
        statc[:, 32:48], statc[:, 32:48], AF.Exp, scale=0.5), inc="SC")
    s.wait("vector", "SC", ck["ln1_se"])
    ck["ln1_cols"] = s.op("vector", lambda e: e.scalar_tensor_tensor(
        statc[:, 48:64], statc[:, 0:16], -1.0, statc[:, 16:32],
        ALU.mult, ALU.mult), inc="VE")

    s.wait("tensor", "VE", ck["ln1_cols"])
    ck["ln1_tr"] = s.op("tensor", lambda e: e.transpose(
        psT[0][0:64, 0:128], statc[:, :], identf[:]), inc="TM")
    s.wait("vector", "TM", ck["ln1_tr"])
    ck["ln1_trev"] = s.op("vector", lambda e: e.tensor_copy(
        stT[:], psT[0][0:64, 0:128]), inc="VE")
    s.wait("gpsimd", "VE", ck["ln1_trev"])
    for src_col, ri in ((16, 0), (48, 1), (32, 2), (0, 3)):
        outap = bass.AP(rowsb, ri * T, [[rowsb.shape[1], 1], [1, T]])
        s.dma("gpsimd", lambda e, sc_=src_col, o=outap: e.dma_start(
            out=o, in_=stT[sc_:sc_ + 16, :]), inc="RD")
    ck["ln1_rows"] = s.cp("RD")

    prev_bc_ev = [None]

    def bcast_row(row_off, dst):
        if prev_bc_ev[0] is not None:
            s.wait("tensor", "VE", prev_bc_ev[0])
        for i in range(4):
            cpt = s.op("tensor", lambda e, i=i: e.matmul(
                psBC[:, i * 512:(i + 1) * 512], onesb[0:1, :],
                rowsb[0:1, row_off + i * 512:row_off + (i + 1) * 512],
                start=True, stop=True), inc="TM" if i == 3 else None)
        s.wait("vector", "TM", cpt)
        prev_bc_ev[0] = s.op("vector", lambda e, dst=dst: e.tensor_copy(
            dst[:], psBC[:]), inc="VE")
        return prev_bc_ev[0]

    s.wait("tensor", "RD", ck["ln1_rows"])
    bcast_row(0 * T, lnA)
    bcast_row(1 * T, lnB)
    bcast_row(2 * T, stdb)
    ck["ln1_bcast"] = bcast_row(3 * T, meanb)

    s.wait("vector", "VE", ck["ln1_bcast"])
    for c in range(C):
        sl = slice(c * T, (c + 1) * T)
        mpc = s.op("vector", lambda e, sl=sl: e.tensor_mul(
            xmain[:, sl], xmain[:, sl], lnA[:]), inc="VE")
    s.wait("vector", "VE", mpc)
    for c in range(C):
        sl = slice(c * T, (c + 1) * T)
        apl = s.op("vector", lambda e, sl=sl: e.tensor_add(
            xmain[:, sl], xmain[:, sl], lnB[:]), inc="VE")
    ck["ln1_apply"] = apl
    if upto == "ln1":
        return _finish_debug(nc, s, sems, dict(xmain=xmain, lnA=lnA, lnB=lnB,
                                               stdb=stdb, meanb=meanb,
                                               statc=statc, rowsb=rowsb))

    wload = {}

    def load_w_chunk(m, slot):
        pair, which = (m, "q") if m < 8 else (m - 8, "k")
        if pair >= 2:
            s.wait("gpsimd", "VE", qk_evac[(pair - 2, which)])
        inap = bass.AP(wqkv_h.tensor, m * 128,
                       [[3072, 128], [3072 * 128, 8], [1, 128]])
        wload[m] = (s.dma("gpsimd", lambda e, i=inap, slot=slot: e.dma_start(
            out=wring[slot][:], in_=i), inc=f"LG{slot}"), f"LG{slot}")

    qk_evac = {}
    pair_last_qkt = {}
    prev_misc_user = [None]

    def emit_qk_half(pair, which, half):
        m = pair if which == "q" else 8 + pair
        slot = (2 * pair + (0 if which == "q" else 1)) % 4
        s.wait("tensor", wload[m][1], wload[m][0])
        if pair == 0 and which == "q" and half == 0:
            s.wait("tensor", "VE", ck["ln1_apply"])
        if prev_misc_user[0] is not None:
            s.wait("tensor", "VE", prev_misc_user[0])
        for nt in range(2):
            for kc in range(C):
                cpt = s.op("tensor", lambda e, nt=nt, kc=kc, slot=slot, half=half:
                           e.matmul(psmisc[:, nt * 512:(nt + 1) * 512],
                                    wring[slot][:, kc * 128:(kc + 1) * 128],
                                    xmain[:, kc * T + half * 1024 + nt * 512:
                                          kc * T + half * 1024 + (nt + 1) * 512],
                                    start=(kc == 0), stop=(kc == C - 1)),
                           inc="TM" if (nt == 1 and kc == C - 1) else None)
        s.wait("vector", "TM", cpt)
        if (pair - 2) in pair_last_qkt:
            s.wait("vector", "TM", pair_last_qkt[pair - 2])
        hsl = slice(half * 1024, (half + 1) * 1024)
        e0 = scr[which + "0"][pair % 2]
        e1 = scr[which + "1"][pair % 2]
        s.op("vector", lambda e, e0=e0, hsl=hsl, m=m: e.tensor_scalar(
            e0[0:64, hsl], psmisc[0:64, :], bqk_s[0:64, m:m + 1], None, ALU.add))
        vcp = s.op("vector", lambda e, e1=e1, hsl=hsl, m=m: e.tensor_scalar(
            e1[64:128, hsl], psmisc[64:128, :], bqk_s[64:128, m:m + 1], None,
            ALU.add), inc="VE")
        prev_misc_user[0] = vcp
        if half == 1:
            qk_evac[(pair, which)] = vcp

    dup_cp = {}

    def emit_dups(pair):
        sm = "DPa" if pair % 2 == 0 else "DPb"
        s.wait("sync", "VE", max(qk_evac[(pair, "q")], qk_evac[(pair, "k")]))
        for which in ("q", "k"):
            t0 = scr[which + "0"][pair % 2]
            t1 = scr[which + "1"][pair % 2]
            s.dma("sync", lambda e, t=t0: e.dma_start(out=t[64:128, :],
                                                      in_=t[0:64, :]), inc=sm)
            cpg = s.dma("sync", lambda e, t=t1: e.dma_start(out=t[0:64, :],
                                                            in_=t[64:128, :]),
                        inc=sm)
        dup_cp[pair] = (cpg, sm)

    load_w_chunk(0, 0)
    load_w_chunk(8, 1)
    for which in ("q", "k"):
        for half in range(2):
            emit_qk_half(0, which, half)
    emit_dups(0)
    if upto == "qk0":
        return _finish_debug(nc, s, sems, dict(
            q0=scr["q0"][0], q1=scr["q1"][0], k0=scr["k0"][0], k1=scr["k1"][0]))

    vevac = []
    for jc in range(TC):
        ps = slab[jc % 2]
        if jc == 0:
            s.wait("tensor", "LGC", s.cp("LGC"))
        if jc >= 2:
            s.wait("tensor", "VE", vevac[jc - 2])
        for nt in range(2):
            for kc in range(C):
                s.op("tensor", lambda e, ps=ps, nt=nt, kc=kc, jc=jc: e.matmul(
                    ps[:, nt * 512:(nt + 1) * 512],
                    xmain[:, kc * T + jc * 128:kc * T + (jc + 1) * 128],
                    wvfull[:, kc * 1024 + nt * 512:kc * 1024 + (nt + 1) * 512],
                    start=(kc == 0), stop=False))
            cpt = s.op("tensor", lambda e, ps=ps, nt=nt: e.matmul(
                ps[:, nt * 512:(nt + 1) * 512], onesb[0:1, :],
                bvrow[0:1, nt * 512:(nt + 1) * 512], start=False, stop=True),
                inc="TM" if nt == 1 else None)
        s.wait("vector", "TM", cpt)
        oap = vones[:, jc * 1040:jc * 1040 + 1040].rearrange(
            "p (h f) -> p h f", f=65)[:, :, 0:64]
        iap = ps[:, :].rearrange("p (h f) -> p h f", f=64)
        vevac.append(s.op("vector", lambda e, o=oap, i=iap: e.tensor_copy(o, i),
                          inc="VE"))
    ck["v_done"] = s.cp("VE")
    if upto == "v":
        return _finish_debug(nc, s, sems, dict(vones=vones))

    bg = []
    for pair in range(1, 8):
        bg.append(("loadw", pair, "q"))
        bg.append(("qk", pair, "q", 0))
        bg.append(("qk", pair, "q", 1))
        bg.append(("loadw", pair, "k"))
        bg.append(("qk", pair, "k", 0))
        bg.append(("qk", pair, "k", 1))
        bg.append(("dups", pair))
    bg_i = [0]

    def run_bg(n):
        for _ in range(n):
            if bg_i[0] >= len(bg):
                return
            item = bg[bg_i[0]]
            bg_i[0] += 1
            if item[0] == "loadw":
                _, pair, which = item
                load_w_chunk(pair if which == "q" else 8 + pair,
                             (2 * pair + (0 if which == "q" else 1)) % 4)
            elif item[0] == "qk":
                emit_qk_half(item[1], item[2], item[3])
            else:
                emit_dups(item[1])

    exp_cp, qkt_cp, pv_cp, tt_cp = {}, {}, {}, {}
    stg_dma_cp = {}
    last_ttdma = [0]
    nslab = 0
    dbg_single = upto in ("attn1", "attn1b")
    npair = 1 if dbg_single else 8
    neip = 1 if dbg_single else 2
    nih = 1 if upto == "attn1" else 2
    for pair in range(npair):
        for eip in range(neip):
            h = 2 * pair + eip
            sq0 = scr["q" + str(eip)][pair % 2]
            sk0 = scr["k" + str(eip)][pair % 2]
            if not dbg_single:
                run_bg(0 if h == 0 else (4 if h % 2 == 1 else 3))
            for ihalf in range(nih):
                g = h * 2 + ihalf
                g_start = nslab

                def emit_pv(n2, jc2, h=h, g=g):
                    pt2 = n2 % NPT
                    s.wait("tensor", "SC", exp_cp[n2])
                    if jc2 == 0 and g > 0:
                        s.wait("tensor", "VE", tt_cp[g - 1])
                    for it in range(2):
                        pvc = s.op("tensor", lambda e, it=it, jc=jc2, h=h,
                                   pt=pt2: e.matmul(
                                       pspv[0:65, it * 512:(it + 1) * 512],
                                       vones[:, jc * 1040 + h * 65:
                                             jc * 1040 + h * 65 + 65],
                                       pT[pt][:, it * 512:(it + 1) * 512],
                                       start=(jc == 0), stop=(jc == TC - 1)),
                                   inc="TM" if it == 1 else None)
                    pv_cp[n2] = pvc

                LAG = 3
                for jc in range(TC):
                    n = nslab
                    nslab += 1
                    sb = n % 2
                    pt = n % NPT
                    if jc == 0 and ihalf == 0:
                        s.wait("tensor", dup_cp[pair][1], dup_cp[pair][0])
                        if h == 0:
                            s.wait("tensor", "VE", ck["v_done"])
                    if n >= 2:
                        s.wait("tensor", "SC", exp_cp[n - 2])
                    s.op("tensor", lambda e, sb=sb, sk0=sk0, sq0=sq0, jc=jc,
                         ihalf=ihalf: e.matmul(
                             slab[sb][:, 0:512],
                             sk0[0:64, jc * 128:(jc + 1) * 128],
                             sq0[0:64, ihalf * 1024:ihalf * 1024 + 512],
                             start=True, stop=True, tile_position=(0, 0)))
                    qkt_cp[n] = s.op("tensor", lambda e, sb=sb, sk0=sk0, sq0=sq0,
                                     jc=jc, ihalf=ihalf: e.matmul(
                                         slab[sb][:, 512:1024],
                                         sk0[64:128, jc * 128:(jc + 1) * 128],
                                         sq0[64:128, ihalf * 1024 + 512:
                                             ihalf * 1024 + 1024],
                                         start=True, stop=True,
                                         tile_position=(64, 0)), inc="TM")
                    s.wait("scalar", "TM", qkt_cp[n])
                    if n >= NPT:
                        s.wait("scalar", "TM", pv_cp[n - NPT])
                    exp_cp[n] = s.op("scalar", lambda e, pt=pt, sb=sb:
                                     e.activation(pT[pt][:], slab[sb][:], AF.Exp,
                                                  bias=neg30[:]), inc="SC")
                    if jc >= LAG:
                        emit_pv(n - LAG, jc - LAG)
                for jc in range(TC - LAG, TC):
                    emit_pv(g_start + jc, jc)
                s.wait("vector", "TM", pv_cp[nslab - 1])
                ecp = s.op("vector", lambda e, g=g: e.tensor_copy(
                    ub[g % 2][0:65, :], pspv[0:65, :]), inc="VE")
                s.wait("scalar", "VE", ecp)
                lcp = s.op("scalar", lambda e, g=g: e.activation(
                    rrow[64:65, :], ub[g % 2][64:65, :], AF.Ln), inc="SC")
                s.wait("scalar", "SC", lcp)
                rcp = s.op("scalar", lambda e: e.activation(
                    rrow[64:65, :], rrow[64:65, :], AF.Exp, scale=-1.0),
                    inc="SC")
                s.wait("tensor", "SC", rcp)
                s.wait("tensor", "VE", ecp)
                for it in range(2):
                    bcp = s.op("tensor", lambda e, it=it: e.matmul(
                        pspv[0:64, it * 512:(it + 1) * 512], onesb[64:65, 0:64],
                        rrow[64:65, it * 512:(it + 1) * 512],
                        start=True, stop=True, tile_position=(64, 0)),
                        inc="TM" if it == 1 else None)
                s.wait("vector", "TM", bcp)
                s.wait("vector", "VE", ecp)
                for par in range(2):
                    src0 = ub[g % 2][0:64, :].rearrange(
                        "p (tb f) -> p f tb", f=16).rearrange(
                        "p (cc pr) tb -> p pr cc tb", pr=2)[:, par]
                    src1 = pspv[0:64, :].rearrange(
                        "p (tb f) -> p f tb", f=16).rearrange(
                        "p (cc pr) tb -> p pr cc tb", pr=2)[:, par]
                    dst = attnb[par * 64:(par + 1) * 64, :].rearrange(
                        "p (cc t) -> p cc t", t=T)[
                        :, :, h * 128 + ihalf * 64:h * 128 + ihalf * 64 + 64]
                    tt_cp[g] = s.op("vector", lambda e, o=dst, a=src0, b=src1:
                                    e.tensor_mul(o, a, b),
                                    inc="VE" if par == 1 else None)
            if eip == 1:
                pair_last_qkt[pair] = qkt_cp[nslab - 1]
    if not dbg_single:
        run_bg(100)
    ck["attn_ve"] = s.cp("VE")
    ck["attn_td"] = s.cp("TD")
    ck["attn_xd"] = s.cp("XD")
    ck["attn_sc"] = s.cp("SC")
    ck["attn_tm"] = s.cp("TM")
    if dbg_single or upto == "attn":
        return _finish_debug(nc, s, sems, dict(attnb=attnb, ub0=ub[0]))

    def load_wffn(src_h, dst, sem, wait_tm=None, wait_ve=None, wait_sc=None):
        if wait_tm:
            s.wait("gpsimd", "TM", wait_tm)
        if wait_ve:
            s.wait("gpsimd", "VE", wait_ve)
        if wait_sc:
            s.wait("gpsimd", "SC", wait_sc)
        for kc in range(C):
            inap = bass.AP(src_h.tensor, kc * 128 * D, [[D, 128], [1, 1024]])
            cpl = s.dma("gpsimd", lambda e, i=inap, kc=kc: e.dma_start(
                out=dst[:, kc * 1024:(kc + 1) * 1024], in_=i), inc=sem)
        return (cpl, sem)

    ck["wrt"] = load_wffn(wrt_h, wslotA, "W1", wait_tm=ck["attn_tm"],
                          wait_ve=ck["attn_ve"], wait_sc=ck["attn_sc"])

    def ln_feature_major(src_tensor, extra_ve_wait=None):
        s2cp = None
        for c in range(C):
            sl = slice(c * T, (c + 1) * T)
            buf = tmpc if c % 2 == 0 else xpc
            if extra_ve_wait and c == 0:
                s.wait("vector", "VE", extra_ve_wait)
            if c >= 2:
                s.wait("vector", "TM", s2cp)
            sqv = s.op("vector", lambda e, sl=sl, buf=buf: e.tensor_mul(
                buf[:], src_tensor[:, sl], src_tensor[:, sl]), inc="VE")
            s.wait("tensor", "VE", sqv)
            for i in range(4):
                s.op("tensor", lambda e, c=c, i=i: e.matmul(
                    psA[32 * i:32 * i + 1, 0:512], onesb[:, 0:1],
                    src_tensor[:, c * T + i * 512:c * T + (i + 1) * 512],
                    start=(c == 0), stop=(c == C - 1), tile_position=(0, 32 * i)))
            for i in range(4):
                s2cp = s.op("tensor", lambda e, i=i, buf=buf, c=c: e.matmul(
                    psA[32 * i:32 * i + 1, 512:1024], onesb[:, 0:1],
                    buf[:, i * 512:(i + 1) * 512],
                    start=(c == 0), stop=(c == C - 1), tile_position=(0, 32 * i)),
                    inc="TM" if i == 3 else None)
        s.wait("vector", "TM", s2cp)
        a1 = s.op("vector", lambda e: e.tensor_scalar_mul(st_a[:], psA[:, 0:512],
                                                          1.0 / D), inc="VE")
        a2 = s.op("vector", lambda e: e.tensor_scalar_mul(st_b[:],
                                                          psA[:, 512:1024],
                                                          1.0 / D), inc="VE")
        s.wait("vector", "VE", a1)
        a3 = s.op("vector", lambda e: e.tensor_mul(st_c[:], st_a[:], st_a[:]),
                  inc="VE")
        s.wait("vector", "VE", a3)
        vcp = s.op("vector", lambda e: e.tensor_sub(st_b[:], st_b[:], st_c[:]),
                   inc="VE")
        s.wait("scalar", "VE", vcp)
        for r in (0, 32, 64, 96):
            lcp = s.op("scalar", lambda e, r=r: e.activation(
                st_b[r:r + 1, :], st_b[r:r + 1, :], AF.Ln, bias=epsc[r:r + 1]),
                inc="SC" if r == 96 else None)
        s.wait("scalar", "SC", lcp)
        scp = s.op("scalar", lambda e: e.activation(sAr[:], st_b[:], AF.Exp,
                                                    scale=-0.5), inc="SC")
        s.wait("vector", "SC", scp)
        vcp2 = s.op("vector", lambda e: e.scalar_tensor_tensor(
            sBr[:], st_a[:], -1.0, sAr[:], ALU.mult, ALU.mult), inc="VE")
        s.wait("tensor", "VE", vcp2)
        for i in range(4):
            acp = s.op("tensor", lambda e, i=i: e.matmul(
                psB[:, i * 512:(i + 1) * 512], onesb[32 * i:32 * i + 1, :],
                sAr[32 * i:32 * i + 1, :], start=True, stop=True,
                tile_position=(32 * i, 0)), inc="TM" if i == 3 else None)
        s.wait("vector", "TM", acp)
        va = s.op("vector", lambda e: e.tensor_copy(lnA[:], psB[:]), inc="VE")
        s.wait("tensor", "VE", va)
        for i in range(4):
            bcp = s.op("tensor", lambda e, i=i: e.matmul(
                psB[:, i * 512:(i + 1) * 512], onesb[32 * i:32 * i + 1, :],
                sBr[32 * i:32 * i + 1, :], start=True, stop=True,
                tile_position=(32 * i, 0)), inc="TM" if i == 3 else None)
        s.wait("vector", "TM", bcp)
        return s.op("vector", lambda e: e.tensor_copy(lnB[:], psB[:]), inc="VE")

    s.wait("vector", "VE", ck["attn_ve"])
    s.wait("vector", "TD", ck["attn_td"])
    s.wait("vector", "XD", ck["attn_xd"])
    s.wait("tensor", "SC", ck["attn_sc"])
    ln2 = ln_feature_major(attnb)
    s.wait("vector", "VE", ln2)
    bmcp = s.op("vector", lambda e: e.tensor_add(lnBM[:], lnB[:], meanb[:]),
                inc="VE")
    s.wait("vector", "VE", bmcp)
    for c in range(C):
        sl = slice(c * T, (c + 1) * T)
        p1 = s.op("vector", lambda e, sl=sl: e.tensor_mul(
            attnb[:, sl], attnb[:, sl], lnA[:]), inc="VE")
    s.wait("vector", "VE", p1)
    for c in range(C):
        sl = slice(c * T, (c + 1) * T)
        p2 = s.op("vector", lambda e, sl=sl: e.tensor_add(
            attnb[:, sl], attnb[:, sl], lnBM[:]), inc="VE")
    for c in range(C):
        sl = slice(c * T, (c + 1) * T)
        p3 = s.op("vector", lambda e, sl=sl: e.tensor_mul(
            xmain[:, sl], xmain[:, sl], stdb[:]), inc="VE")
    s.wait("vector", "VE", p3)
    for c in range(C):
        sl = slice(c * T, (c + 1) * T)
        x2cp = s.op("vector", lambda e, sl=sl: e.tensor_add(
            xmain[:, sl], xmain[:, sl], attnb[:, sl]), inc="VE")
    ck["x2"] = x2cp
    if upto == "x2":
        return _finish_debug(nc, s, sems, dict(xmain=xmain))

    ln3 = ln_feature_major(xmain, extra_ve_wait=ck["x2"])
    s.wait("vector", "VE", ln3)
    for c in range(C):
        sl = slice(c * T, (c + 1) * T)
        l3p = s.op("vector", lambda e, sl=sl: e.tensor_mul(
            attnb[:, sl], xmain[:, sl], lnA[:]), inc="VE")
    s.wait("vector", "VE", l3p)
    for c in range(C):
        sl = slice(c * T, (c + 1) * T)
        xn3cp = s.op("vector", lambda e, sl=sl: e.tensor_add(
            attnb[:, sl], attnb[:, sl], lnB[:]), inc="VE")
    ck["xn3"] = xn3cp
    if upto == "xn3":
        return _finish_debug(nc, s, sems, dict(attnb=attnb))

    ck["wkt"] = load_wffn(wkt_h, wslotB, "W2", wait_ve=ck["xn3"])

    def emit_mix(base, dst, wait_ve_first):
        mcps = []
        tcps = {}
        for c in range(C):
            sl = slice(c * T, (c + 1) * T)
            mc = base + c
            buf = tmpc2 if c % 2 == 0 else xpc
            if c == 0:
                s.wait("vector", "VE", wait_ve_first)
                s.op("vector", lambda e: e.memset(tmpc2[:, 0:1], 0.0))
                s.op("vector", lambda e: e.memset(xpc[:, 0:1], 0.0))
            if c >= 2:
                s.wait("vector", "VE", mcps[c - 2])
            tcps[c] = s.op("vector", lambda e, mc=mc, c=c, buf=buf:
                           e.tensor_scalar(
                               buf[:, 1:T], attnb[:, c * T:c * T + T - 1],
                               mu1m[:, mc:mc + 1], None, ALU.mult), inc="VE")
            s.wait("vector", "VE", tcps[c])
            mcps.append(s.op("vector", lambda e, sl=sl, mc=mc, dst=dst,
                             buf=buf: e.scalar_tensor_tensor(
                                 dst[:, sl], attnb[:, sl], mu_s[:, mc:mc + 1],
                                 buf[:], ALU.mult, ALU.add), inc="VE"))
        return mcps[-1]

    ck["xr"] = emit_mix(0, xr_t, ck["xn3"])

    def ffn_mm(wslot, src, dc, wait_lg):
        ps = psA if dc % 2 == 0 else psB
        s.wait("tensor", wait_lg[1], wait_lg[0])
        for nt in range(4):
            for kc in range(C):
                cpt = s.op("tensor", lambda e, ps=ps, nt=nt, kc=kc, dc=dc,
                           wslot=wslot, src=src: e.matmul(
                               ps[:, nt * 512:(nt + 1) * 512],
                               wslot[:, kc * 1024 + dc * 128:
                                     kc * 1024 + (dc + 1) * 128],
                               src[:, kc * T + nt * 512:kc * T + (nt + 1) * 512],
                               start=(kc == 0), stop=(kc == C - 1)),
                           inc="TM" if (nt == 3 and kc == C - 1) else None)
        return cpt, ps

    rexp_cp, rmm_cp = {}, {}
    for dc in range(C):
        if dc == 0:
            s.wait("tensor", "VE", ck["xr"])
        if dc >= 2:
            s.wait("tensor", "SC", rexp_cp[dc - 2])
        rmm_cp[dc], ps = ffn_mm(wslotA, xr_t, dc, ck["wrt"])
        s.wait("scalar", "TM", rmm_cp[dc])
        rexp_cp[dc] = s.op("scalar", lambda e, ps=ps, dc=dc: e.activation(
            rsig_t[:, dc * T:(dc + 1) * T], ps[:], AF.Sigmoid), inc="SC")
    ck["rsig"] = rexp_cp[C - 1]
    ck["rmm_last"] = rmm_cp[C - 1]
    if upto == "rsig":
        return _finish_debug(nc, s, sems, dict(rsig=rsig_t, xr=xr_t))

    ck["wvt"] = load_wffn(wvt_h, wslotA, "W1", wait_tm=ck["rmm_last"])

    s.wait("vector", "TM", ck["rmm_last"])
    ck["xk"] = emit_mix(8, xr_t, ck["xn3"])

    kact_cp = {}
    krl_cp = {}
    for dc in range(C):
        if dc == 0:
            s.wait("tensor", "VE", ck["xk"])
            s.wait("tensor", "SC", rexp_cp[C - 1])
        if dc >= 2:
            s.wait("tensor", "VE", kact_cp[dc - 2])
        kcp, ps = ffn_mm(wslotB, xr_t, dc, ck["wkt"])
        kbuf = tmpc if dc % 2 == 0 else tmpc2
        s.wait("scalar", "TM", kcp)
        if dc >= 2:
            s.wait("scalar", "VE", kact_cp[dc - 2])
        krl_cp[dc] = s.op("scalar", lambda e, ps=ps, kbuf=kbuf: e.activation(
            kbuf[:], ps[:], AF.Relu), inc="SC")
        s.wait("vector", "SC", krl_cp[dc])
        kact_cp[dc] = s.op("vector", lambda e, dc=dc, kbuf=kbuf: e.tensor_mul(
            attnb[:, dc * T:(dc + 1) * T], kbuf[:], kbuf[:]), inc="VE")
    ck["k2"] = kact_cp[C - 1]
    if upto == "k2":
        return _finish_debug(nc, s, sems, dict(k2=attnb))

    fout_cp = {}
    for dc in range(C):
        if dc == 0:
            s.wait("tensor", "VE", ck["k2"])
            s.wait("tensor", "SC", ck["rsig"])
        if dc >= 2:
            s.wait("tensor", "VE", fout_cp[dc - 2])
        vcp, ps = ffn_mm(wslotA, attnb, dc, ck["wvt"])
        s.wait("vector", "TM", vcp)
        fout_cp[dc] = s.op("vector", lambda e, ps=ps, dc=dc: e.tensor_mul(
            xr_t[:, dc * T:(dc + 1) * T], ps[:],
            rsig_t[:, dc * T:(dc + 1) * T]), inc="VE")
    ck["ffnout"] = fout_cp[C - 1]
    if upto == "ffnout":
        return _finish_debug(nc, s, sems, dict(ffnout=xr_t))

    ln4 = ln_feature_major(xr_t, extra_ve_wait=ck["ffnout"])
    s.wait("vector", "VE", ln4)
    for c in range(C):
        sl = slice(c * T, (c + 1) * T)
        l4p = s.op("vector", lambda e, sl=sl: e.tensor_mul(
            xr_t[:, sl], xr_t[:, sl], lnA[:]), inc="VE")
    s.wait("vector", "VE", l4p)
    for c in range(C):
        sl = slice(c * T, (c + 1) * T)
        lncp4 = s.op("vector", lambda e, sl=sl: e.tensor_add(
            xr_t[:, sl], xr_t[:, sl], lnB[:]), inc="VE")
    outst = []
    outdma = []
    stcp = None
    for c in range(C):
        sl = slice(c * T, (c + 1) * T)
        ps = psA if c % 2 == 0 else psB
        ot = otstage if c % 2 == 0 else otstage2
        odeng = "sync" if c % 2 == 0 else "scalar"
        odsem = "OD" if c % 2 == 0 else "OD2"
        if c == 0:
            s.wait("vector", "VE", lncp4)
        else:
            s.wait("vector", "TM", prev_trcp)
        ocp = s.op("vector", lambda e, sl=sl: e.tensor_add(
            ostage[:], xmain[:, sl], xr_t[:, sl]), inc="VE")
        s.wait("tensor", "VE", ocp)
        if c >= 2:
            s.wait("tensor", "VE", outst[c - 2])
        for t in range(TC):
            trcp = s.op("tensor", lambda e, ps=ps, t=t: e.transpose(
                ps[:, t * 128:(t + 1) * 128], ostage[:, t * 128:(t + 1) * 128],
                identf[:]), inc="TM" if t == TC - 1 else None)
        prev_trcp = trcp
        s.wait("vector", "TM", trcp)
        if c >= 2:
            s.wait("vector", odsem, outdma[c - 2])
        vst = s.op("vector", lambda e, ps=ps, ot=ot: e.tensor_copy(
            ot[:], ps[:]), inc="VE")
        outst.append(vst)
        s.wait(odeng, "VE", vst)
        outap = bass.AP(out_h.tensor, c * 128,
                        [[D, 128], [128 * D, 16], [1, 128]])
        stcp = s.dma(odeng, lambda e, o=outap, ot=ot: e.dma_start(
            out=o, in_=ot[:]), inc=odsem)
        outdma.append(stcp)
    s.wait("gpsimd", "OD", s.cp("OD"))
    s.wait("gpsimd", "OD2", s.cp("OD2"))
    s.wait("sync", "OD", s.cp("OD"))
    s.wait("sync", "OD2", s.cp("OD2"))

    with nc.allow_low_precision("bf16 intermediates; fp32 psum accumulation"), \
         nc.Block() as blk:
        blk.tensor(s.run("tensor"))
        blk.scalar(s.run("scalar"))
        blk.vector(s.run("vector"))
        blk.sync(s.run("sync"))
        blk.gpsimd(s.run("gpsimd"))
    return nc


_NC_CACHE = {}


def _get_program():
    if "nc" not in _NC_CACHE:
        _NC_CACHE["nc"] = build_program()
    return _NC_CACHE["nc"]


def kernel(L0_attn_x, L0_attn_W_qkv, L0_attn_b_qkv, L0_ffn_W_r, L0_ffn_W_k,
           L0_ffn_W_v, L0_ffn_mu_r, L0_ffn_mu_k):
    x = np.asarray(L0_attn_x, dtype=np.float32)
    wqkv = np.ascontiguousarray(np.asarray(L0_attn_W_qkv, dtype=np.float32))
    bqkv = np.asarray(L0_attn_b_qkv, dtype=np.float32)
    bqk = np.ascontiguousarray(bqkv[:2048].reshape(16, 128).T)
    bv = np.ascontiguousarray(bqkv[2048:].reshape(1, 1024))
    wrt = np.ascontiguousarray(np.asarray(L0_ffn_W_r, dtype=np.float32).T)
    wkt = np.ascontiguousarray(np.asarray(L0_ffn_W_k, dtype=np.float32).T)
    wvt = np.ascontiguousarray(np.asarray(L0_ffn_W_v, dtype=np.float32).T)
    mu_r = np.asarray(L0_ffn_mu_r, dtype=np.float32)
    mu_k = np.asarray(L0_ffn_mu_k, dtype=np.float32)
    mu = np.ascontiguousarray(
        np.concatenate([mu_r.reshape(8, 128).T, mu_k.reshape(8, 128).T], axis=1))

    nc = _get_program()
    in_maps = []
    for b in range(8):
        in_maps.append({
            "x": np.ascontiguousarray(x[b]),
            "wqkv": wqkv, "bqk": bqk, "bv": bv,
            "wrt": wrt, "wkt": wkt, "wvt": wvt, "mu": mu,
        })
    res = run_bass_kernel_spmd(nc, in_maps, core_ids=list(range(8)))
    out = np.stack([res.results[b]["out"] for b in range(8)], axis=0)
    return out.astype(np.float32)


# version 10
# speedup vs baseline: 1.8839x; 1.0306x over previous
import numpy as np
import ml_dtypes
import concourse.bass as bass
import concourse.mybir as mybir
from concourse.bass_utils import run_bass_kernel_spmd

F32 = mybir.dt.float32
BF16 = mybir.dt.bfloat16
AF = mybir.ActivationFunctionType
ALU = mybir.AluOpType
AX = mybir.AxisListType
bf16 = ml_dtypes.bfloat16

T = 2048
D = 1024
C = 8
TC = 16
NPT = 5

ENGS = ("tensor", "scalar", "vector", "sync", "gpsimd")


class Sched:
    def __init__(self, sems):
        self.ops = {e: [] for e in ENGS}
        self.cnt = {n: 0 for n in sems}
        self.sems = sems

    def op(self, eng, fn, inc=None, incval=1):
        if inc is not None:
            self.cnt[inc] += incval
            self.ops[eng].append(("op", fn, self.sems[inc], incval))
            return self.cnt[inc]
        self.ops[eng].append(("op", fn, None, 0))
        return None

    def dma(self, eng, fn, inc):
        return self.op(eng, fn, inc=inc, incval=16)

    def wait(self, eng, sem_name, val):
        if val is None or val <= 0:
            return
        self.ops[eng].append(("wait", self.sems[sem_name], val))

    def cp(self, sem_name):
        return self.cnt[sem_name]

    def run(self, eng_name):
        def body(e):
            for item in self.ops[eng_name]:
                if item[0] == "wait":
                    e.wait_ge(item[1], item[2])
                else:
                    _, fn, sem, incval = item
                    r = fn(e)
                    if sem is not None:
                        r.then_inc(sem, incval)
        return body


def _finish_debug(nc, s, sems, taps):
    outs = {}
    for name, t in taps.items():
        d = nc.dram_tensor("tap_" + name, [t.shape[0], int(np.prod(t.shape[1:]))],
                           t.dtype, kind="ExternalOutput").ap()
        outs[name] = d
    for sem_name in s.cnt:
        if sem_name != "OD":
            s.wait("sync", sem_name, s.cp(sem_name))
    last = None
    for name, t in taps.items():
        last = s.dma("sync", lambda e, t=t, d=outs[name]: e.dma_start(
            out=d, in_=t[:, :]), inc="OD")
    s.wait("sync", "OD", last)
    s.wait("gpsimd", "OD", last)
    with nc.allow_low_precision("debug"), nc.Block() as blk:
        blk.tensor(s.run("tensor"))
        blk.scalar(s.run("scalar"))
        blk.vector(s.run("vector"))
        blk.sync(s.run("sync"))
        blk.gpsimd(s.run("gpsimd"))
    return nc


def build_program(upto=None):
    nc = bass.Bass("TRN2", target_bir_lowering=False)

    x_h = nc.dram_tensor("x", [T, D], F32, kind="ExternalInput").ap()
    wqkv_h = nc.dram_tensor("wqkv", [D, 3 * D], F32, kind="ExternalInput").ap()
    bqk_h = nc.dram_tensor("bqk", [128, 16], F32, kind="ExternalInput").ap()
    bv_h = nc.dram_tensor("bv", [1, D], F32, kind="ExternalInput").ap()
    wrt_h = nc.dram_tensor("wrt", [D, D], F32, kind="ExternalInput").ap()
    wkt_h = nc.dram_tensor("wkt", [D, D], F32, kind="ExternalInput").ap()
    wvt_h = nc.dram_tensor("wvt", [D, D], F32, kind="ExternalInput").ap()
    mu_h = nc.dram_tensor("mu", [128, 16], F32, kind="ExternalInput").ap()
    out_h = nc.dram_tensor("out", [T, D], F32, kind="ExternalOutput").ap()

    SB = nc.sbuf_top
    cur = [nc.sbuf_base]

    def alloc(name, shape, dtype, at=None):
        nb = int(np.prod(shape[1:])) * mybir.dt.size(dtype)
        if at is None:
            at = (cur[0] + 31) // 32 * 32
            cur[0] = at + nb
        return nc.alloc_sbuf_tensor_at(name, shape, dtype, offset=at)

    def addr(t):
        return nc.lookup_mloc(t).addr

    xmain = alloc("xmain", [128, C * T], BF16)
    attnb = alloc("attnb", [128, C * T], BF16)
    vones = alloc("vones", [128, 16 * 1040], BF16)
    scr = {}
    for nm in ("q0", "q1", "k0", "k1"):
        scr[nm] = [alloc(f"scr_{nm}_{i}", [128, T], BF16) for i in range(2)]
    pT = [alloc(f"pT{i}", [128, 1024], BF16) for i in range(NPT)]
    wring = [alloc(f"wring{i}", [128, 1024], BF16) for i in range(4)]
    ub = [alloc(f"ub{i}", [128, 1024], BF16) for i in range(2)]
    ttst = alloc("ttst", [128, 1024], BF16)
    pT.append(nc.alloc_sbuf_tensor_at("pT4", [128, 1024], BF16,
                                      offset=addr(ttst)))
    rrow = alloc("rrow", [128, 1024], BF16)
    lnA = alloc("lnA", [128, T], BF16)
    lnB = alloc("lnB", [128, T], BF16)
    lnBM = alloc("lnBM", [128, T], BF16)
    stdb = alloc("stdb", [128, T], BF16)
    meanb = alloc("meanb", [128, T], BF16)
    st_a = alloc("st_a", [128, 512], F32)
    st_b = alloc("st_b", [128, 512], F32)
    st_c = alloc("st_c", [128, 512], F32)
    sAr = alloc("sAr", [128, 512], BF16)
    sBr = alloc("sBr", [128, 512], BF16)
    onesb = alloc("onesb", [128, 128], BF16)
    identf = alloc("identf", [128, 128], F32)
    bqk_s = alloc("bqk_s", [128, 16], F32)
    mu_s = alloc("mu_s", [128, 16], F32)
    mu1m = alloc("mu1m", [128, 16], F32)
    bvrow = alloc("bvrow", [1, D], BF16)
    neg30 = alloc("neg30", [128, 1], F32)
    epsc = alloc("epsc", [128, 1], F32)
    statc = alloc("statc", [128, 64], F32)
    stT = alloc("stT", [64, 128], F32)
    xpc = alloc("xpc", [128, T], BF16)
    tmpc = alloc("tmpc", [128, T], BF16)
    tmpc2 = alloc("tmpc2", [128, T], BF16)
    ostage = alloc("ostage", [128, T], F32)

    a_at = addr(attnb)
    xstage = [nc.alloc_sbuf_tensor_at(f"xstage{i}", [128, 1024], F32,
                                      offset=a_at + i * 4096) for i in range(2)]
    sqstage = [nc.alloc_sbuf_tensor_at(f"sqstage{i}", [128, 1024], F32,
                                       offset=a_at + 8192 + i * 4096)
               for i in range(2)]
    wvfull = nc.alloc_sbuf_tensor_at("wvfull", [128, 8 * 1024], BF16,
                                     offset=a_at + 16384)
    xr_t = nc.alloc_sbuf_tensor_at("xr_t", [128, C * T], BF16,
                                   offset=addr(scr["q0"][0]))
    rowsb = nc.alloc_sbuf_tensor_at("rowsb", [1, 4 * T], BF16,
                                    offset=addr(scr["k0"][0]))
    rsig_t = nc.alloc_sbuf_tensor_at("rsig_t", [128, C * T], BF16,
                                     offset=addr(vones))
    otstage = nc.alloc_sbuf_tensor_at("otstage", [128, T], F32,
                                      offset=addr(vones))
    otstage2 = nc.alloc_sbuf_tensor_at("otstage2", [128, T], F32,
                                       offset=addr(vones) + 8192)
    wslotA = nc.alloc_sbuf_tensor_at("wslotA", [128, 8 * 1024], BF16,
                                     offset=addr(pT[0]))
    wslotB = nc.alloc_sbuf_tensor_at("wslotB", [128, 8 * 1024], BF16,
                                     offset=addr(stdb))
    assert cur[0] <= SB, (cur[0], SB)

    nc.psum_base = 0
    PS = nc.alloc_psum_tensor("PS", [128, 4096], F32)
    psT = [PS[:, 0:1024], PS[:, 1024:2048]]
    psBC = PS[:, 2048:4096]
    slab = [PS[:, 0:1024], PS[:, 1024:2048]]
    pspv = PS[:, 2048:3072]
    psmisc = PS[:, 3072:4096]
    psA = PS[:, 0:2048]
    psB = PS[:, 2048:4096]

    sems = {n: nc.alloc_semaphore(n) for n in
            ("LD0", "LD1", "LDC", "LGC", "RD", "LG0", "LG1", "LG2", "LG3",
             "DPa", "DPb", "DQa", "DQb", "TD", "XD", "OD", "OD2", "W1", "W2",
             "TM", "SC", "VE", "GP")}
    s = Sched(sems)
    ck = {}

    idm = s.op("gpsimd", lambda e: e.memset(identf[:], 0.0), inc="GP")
    s.op("gpsimd", lambda e: e.memset(onesb[:], 1.0))
    s.op("gpsimd", lambda e: e.memset(neg30[:], -30.0))
    s.wait("gpsimd", "GP", idm)
    s.op("gpsimd", lambda e: e.affine_select(
        out=identf[:], in_=identf[:], compare_op=ALU.not_equal, fill=1.0,
        base=0, pattern=[[-1, 128]], channel_multiplier=1))
    for jc in range(16):
        ap = bass.AP(vones, jc * 1040 + 64, [[vones.shape[1], 128], [65, 16]])
        s.op("gpsimd", lambda e, ap=ap: e.memset(ap, 1.0))
    ck["g_consts"] = s.op("gpsimd", lambda e: e.memset(epsc[:], 1e-5), inc="GP")

    s.dma("sync", lambda e: e.dma_start(out=bqk_s[:], in_=bqk_h), inc="LDC")
    ck["ld_consts"] = s.dma("sync", lambda e: e.dma_start(out=mu_s[:], in_=mu_h),
                            inc="LDC")
    s.dma("gpsimd", lambda e: e.dma_start(out=bvrow[:], in_=bv_h), inc="LGC")
    for kc in range(C):
        inap = bass.AP(wqkv_h.tensor, kc * 128 * 3072 + 2048,
                       [[3072, 128], [1, 1024]])
        s.dma("gpsimd", lambda e, i=inap, kc=kc: e.dma_start(
            out=wvfull[:, kc * 1024:(kc + 1) * 1024], in_=i), inc="LGC")
    ck["lg_wv"] = s.cp("LGC")

    s.wait("vector", "LDC", ck["ld_consts"])
    ck["ve_mu"] = s.op("vector", lambda e: e.tensor_scalar(
        mu1m[:], mu_s[:], -1.0, 1.0, ALU.mult, ALU.add), inc="VE")

    xld, trd, evd, sqd = [], [], [], []
    for tc in range(TC):
        b = tc % 2
        lsem = "LD0" if b == 0 else "LD1"
        ldeng = "sync" if b == 0 else "scalar"
        if tc >= 2:
            s.wait(ldeng, "TM", trd[tc - 2])
            s.wait(ldeng, "VE", sqd[tc - 2])
        xld.append(s.dma(ldeng, lambda e, tc=tc, b=b: e.dma_start(
            out=xstage[b][:], in_=x_h[tc * 128:(tc + 1) * 128, :]), inc=lsem))
        s.wait("vector", lsem, xld[tc])
        sqw = s.op("vector", lambda e, b=b: e.tensor_mul(
            sqstage[b][:], xstage[b][:], xstage[b][:]), inc="VE")
        s.op("vector", lambda e, tc=tc, b=b: e.reduce_sum(
            statc[:, tc:tc + 1], xstage[b][:], axis=AX.X))
        s.wait("vector", "VE", sqw)
        sqd.append(s.op("vector", lambda e, tc=tc, b=b: e.reduce_sum(
            statc[:, 32 + tc:33 + tc], sqstage[b][:], axis=AX.X), inc="VE"))
        s.wait("tensor", lsem, xld[tc])
        if tc == 0:
            s.wait("tensor", "GP", ck["g_consts"])
        if tc >= 2:
            s.wait("tensor", "VE", evd[tc - 2])
        for c in range(C):
            trc = s.op("tensor", lambda e, b=b, c=c: e.transpose(
                psT[b][:, c * 128:(c + 1) * 128],
                xstage[b][:, c * 128:(c + 1) * 128], identf[:]),
                inc="TM" if c == C - 1 else None)
        trd.append(trc)
        s.wait("vector", "TM", trd[tc])
        oap = xmain[:, :].rearrange("p (c t) -> p c t", t=T)[:, :, tc * 128:
                                                            (tc + 1) * 128]
        iap = psT[b][:, :].rearrange("p (c f) -> p c f", f=128)
        evd.append(s.op("vector", lambda e, o=oap, i=iap: e.tensor_copy(o, i),
                        inc="VE"))
    ck["x_in"] = s.cp("VE")
    if upto == "stats_raw":
        return _finish_debug(nc, s, sems, dict(statc=statc))

    s.wait("vector", "VE", ck["x_in"])
    c1 = s.op("vector", lambda e: e.tensor_scalar_mul(
        statc[:, 0:16], statc[:, 0:16], 1.0 / D), inc="VE")
    c2 = s.op("vector", lambda e: e.tensor_scalar_mul(
        statc[:, 32:48], statc[:, 32:48], 1.0 / D), inc="VE")
    s.wait("scalar", "VE", c1)
    m2cp = s.op("scalar", lambda e: e.activation(st_b[:, 0:16], statc[:, 0:16],
                                                 AF.Square), inc="SC")
    s.wait("vector", "SC", m2cp)
    s.wait("vector", "VE", c2)
    ck["ln1_var"] = s.op("vector", lambda e: e.tensor_sub(
        statc[:, 32:48], statc[:, 32:48], st_b[:, 0:16]), inc="VE")
    s.wait("scalar", "GP", ck["g_consts"])
    s.wait("scalar", "VE", ck["ln1_var"])
    lncp = s.op("scalar", lambda e: e.activation(statc[:, 32:48], statc[:, 32:48],
                                                 AF.Ln, bias=epsc[:]), inc="SC")
    s.wait("scalar", "SC", lncp)
    rcp1 = s.op("scalar", lambda e: e.activation(statc[:, 16:32], statc[:, 32:48],
                                                 AF.Exp, scale=-0.5), inc="SC")
    s.wait("scalar", "SC", rcp1)
    ck["ln1_se"] = s.op("scalar", lambda e: e.activation(
        statc[:, 32:48], statc[:, 32:48], AF.Exp, scale=0.5), inc="SC")
    s.wait("vector", "SC", ck["ln1_se"])
    ck["ln1_cols"] = s.op("vector", lambda e: e.scalar_tensor_tensor(
        statc[:, 48:64], statc[:, 0:16], -1.0, statc[:, 16:32],
        ALU.mult, ALU.mult), inc="VE")

    s.wait("tensor", "VE", ck["ln1_cols"])
    ck["ln1_tr"] = s.op("tensor", lambda e: e.transpose(
        psT[0][0:64, 0:128], statc[:, :], identf[:]), inc="TM")
    s.wait("vector", "TM", ck["ln1_tr"])
    ck["ln1_trev"] = s.op("vector", lambda e: e.tensor_copy(
        stT[:], psT[0][0:64, 0:128]), inc="VE")
    s.wait("gpsimd", "VE", ck["ln1_trev"])
    for src_col, ri in ((16, 0), (48, 1), (32, 2), (0, 3)):
        outap = bass.AP(rowsb, ri * T, [[rowsb.shape[1], 1], [1, T]])
        s.dma("gpsimd", lambda e, sc_=src_col, o=outap: e.dma_start(
            out=o, in_=stT[sc_:sc_ + 16, :]), inc="RD")
    ck["ln1_rows"] = s.cp("RD")

    prev_bc_ev = [None]

    def bcast_row(row_off, dst):
        if prev_bc_ev[0] is not None:
            s.wait("tensor", "VE", prev_bc_ev[0])
        for i in range(4):
            cpt = s.op("tensor", lambda e, i=i: e.matmul(
                psBC[:, i * 512:(i + 1) * 512], onesb[0:1, :],
                rowsb[0:1, row_off + i * 512:row_off + (i + 1) * 512],
                start=True, stop=True), inc="TM" if i == 3 else None)
        s.wait("vector", "TM", cpt)
        prev_bc_ev[0] = s.op("vector", lambda e, dst=dst: e.tensor_copy(
            dst[:], psBC[:]), inc="VE")
        return prev_bc_ev[0]

    s.wait("tensor", "RD", ck["ln1_rows"])
    bcast_row(0 * T, lnA)
    bcast_row(1 * T, lnB)
    bcast_row(2 * T, stdb)
    ck["ln1_bcast"] = bcast_row(3 * T, meanb)

    s.wait("vector", "VE", ck["ln1_bcast"])
    for c in range(C):
        sl = slice(c * T, (c + 1) * T)
        mpc = s.op("vector", lambda e, sl=sl: e.tensor_mul(
            xmain[:, sl], xmain[:, sl], lnA[:]), inc="VE")
    s.wait("vector", "VE", mpc)
    for c in range(C):
        sl = slice(c * T, (c + 1) * T)
        apl = s.op("vector", lambda e, sl=sl: e.tensor_add(
            xmain[:, sl], xmain[:, sl], lnB[:]), inc="VE")
    ck["ln1_apply"] = apl
    if upto == "ln1":
        return _finish_debug(nc, s, sems, dict(xmain=xmain, lnA=lnA, lnB=lnB,
                                               stdb=stdb, meanb=meanb,
                                               statc=statc, rowsb=rowsb))

    wload = {}

    def load_w_chunk(m, slot):
        pair, which = (m, "q") if m < 8 else (m - 8, "k")
        if pair >= 2:
            s.wait("gpsimd", "VE", qk_evac[(pair - 2, which)])
        inap = bass.AP(wqkv_h.tensor, m * 128,
                       [[3072, 128], [3072 * 128, 8], [1, 128]])
        wload[m] = (s.dma("gpsimd", lambda e, i=inap, slot=slot: e.dma_start(
            out=wring[slot][:], in_=i), inc=f"LG{slot}"), f"LG{slot}")

    qk_evac = {}
    pair_last_qkt = {}
    prev_misc_user = [None]

    def emit_qk_half(pair, which, half):
        m = pair if which == "q" else 8 + pair
        slot = (2 * pair + (0 if which == "q" else 1)) % 4
        s.wait("tensor", wload[m][1], wload[m][0])
        if pair == 0 and which == "q" and half == 0:
            s.wait("tensor", "VE", ck["ln1_apply"])
        if prev_misc_user[0] is not None:
            s.wait("tensor", "VE", prev_misc_user[0])
        for nt in range(2):
            for kc in range(C):
                cpt = s.op("tensor", lambda e, nt=nt, kc=kc, slot=slot, half=half:
                           e.matmul(psmisc[:, nt * 512:(nt + 1) * 512],
                                    wring[slot][:, kc * 128:(kc + 1) * 128],
                                    xmain[:, kc * T + half * 1024 + nt * 512:
                                          kc * T + half * 1024 + (nt + 1) * 512],
                                    start=(kc == 0), stop=(kc == C - 1)),
                           inc="TM" if (nt == 1 and kc == C - 1) else None)
        s.wait("vector", "TM", cpt)
        if (pair - 2) in pair_last_qkt:
            s.wait("vector", "TM", pair_last_qkt[pair - 2])
        hsl = slice(half * 1024, (half + 1) * 1024)
        e0 = scr[which + "0"][pair % 2]
        e1 = scr[which + "1"][pair % 2]
        s.op("vector", lambda e, e0=e0, hsl=hsl, m=m: e.tensor_scalar(
            e0[0:64, hsl], psmisc[0:64, :], bqk_s[0:64, m:m + 1], None, ALU.add))
        vcp = s.op("vector", lambda e, e1=e1, hsl=hsl, m=m: e.tensor_scalar(
            e1[64:128, hsl], psmisc[64:128, :], bqk_s[64:128, m:m + 1], None,
            ALU.add), inc="VE")
        prev_misc_user[0] = vcp
        if half == 1:
            qk_evac[(pair, which)] = vcp

    dup_cp = {}

    def emit_dups(pair):
        cps = []
        for which, eng in (("q", "sync"), ("k", "scalar")):
            sm = ("DP" if which == "q" else "DQ") + ("a" if pair % 2 == 0
                                                     else "b")
            s.wait(eng, "VE", max(qk_evac[(pair, "q")], qk_evac[(pair, "k")]))
            t0 = scr[which + "0"][pair % 2]
            t1 = scr[which + "1"][pair % 2]
            s.dma(eng, lambda e, t=t0: e.dma_start(out=t[64:128, :],
                                                   in_=t[0:64, :]), inc=sm)
            cpg = s.dma(eng, lambda e, t=t1: e.dma_start(out=t[0:64, :],
                                                         in_=t[64:128, :]),
                        inc=sm)
            cps.append((cpg, sm))
        dup_cp[pair] = cps

    load_w_chunk(0, 0)
    load_w_chunk(8, 1)
    for which in ("q", "k"):
        for half in range(2):
            emit_qk_half(0, which, half)
    emit_dups(0)
    if upto == "qk0":
        return _finish_debug(nc, s, sems, dict(
            q0=scr["q0"][0], q1=scr["q1"][0], k0=scr["k0"][0], k1=scr["k1"][0]))

    vevac = []
    for jc in range(TC):
        ps = slab[jc % 2]
        if jc == 0:
            s.wait("tensor", "LGC", s.cp("LGC"))
        if jc >= 2:
            s.wait("tensor", "VE", vevac[jc - 2])
        for nt in range(2):
            for kc in range(C):
                s.op("tensor", lambda e, ps=ps, nt=nt, kc=kc, jc=jc: e.matmul(
                    ps[:, nt * 512:(nt + 1) * 512],
                    xmain[:, kc * T + jc * 128:kc * T + (jc + 1) * 128],
                    wvfull[:, kc * 1024 + nt * 512:kc * 1024 + (nt + 1) * 512],
                    start=(kc == 0), stop=False))
            cpt = s.op("tensor", lambda e, ps=ps, nt=nt: e.matmul(
                ps[:, nt * 512:(nt + 1) * 512], onesb[0:1, :],
                bvrow[0:1, nt * 512:(nt + 1) * 512], start=False, stop=True),
                inc="TM" if nt == 1 else None)
        s.wait("vector", "TM", cpt)
        oap = vones[:, jc * 1040:jc * 1040 + 1040].rearrange(
            "p (h f) -> p h f", f=65)[:, :, 0:64]
        iap = ps[:, :].rearrange("p (h f) -> p h f", f=64)
        vevac.append(s.op("vector", lambda e, o=oap, i=iap: e.tensor_copy(o, i),
                          inc="VE"))
    ck["v_done"] = s.cp("VE")
    if upto == "v":
        return _finish_debug(nc, s, sems, dict(vones=vones))

    bg = []
    for pair in range(1, 8):
        bg.append(("loadw", pair, "q"))
        bg.append(("qk", pair, "q", 0))
        bg.append(("qk", pair, "q", 1))
        bg.append(("loadw", pair, "k"))
        bg.append(("qk", pair, "k", 0))
        bg.append(("qk", pair, "k", 1))
        bg.append(("dups", pair))
    bg_i = [0]

    def run_bg(n):
        for _ in range(n):
            if bg_i[0] >= len(bg):
                return
            item = bg[bg_i[0]]
            bg_i[0] += 1
            if item[0] == "loadw":
                _, pair, which = item
                load_w_chunk(pair if which == "q" else 8 + pair,
                             (2 * pair + (0 if which == "q" else 1)) % 4)
            elif item[0] == "qk":
                emit_qk_half(item[1], item[2], item[3])
            else:
                emit_dups(item[1])

    exp_cp, qkt_cp, pv_cp, tt_cp = {}, {}, {}, {}
    stg_dma_cp = {}
    last_ttdma = [0]
    nslab = 0
    dbg_single = upto in ("attn1", "attn1b")
    npair = 1 if dbg_single else 8
    neip = 1 if dbg_single else 2
    nih = 1 if upto == "attn1" else 2
    for pair in range(npair):
        for eip in range(neip):
            h = 2 * pair + eip
            sq0 = scr["q" + str(eip)][pair % 2]
            sk0 = scr["k" + str(eip)][pair % 2]
            if not dbg_single:
                run_bg(3 if h == 0 else (4 if h % 2 == 1 else 3))
            for ihalf in range(nih):
                g = h * 2 + ihalf
                g_start = nslab

                def emit_pv(n2, jc2, h=h, g=g):
                    pt2 = n2 % NPT
                    s.wait("tensor", "SC", exp_cp[n2])
                    if jc2 == 0 and g > 0:
                        s.wait("tensor", "VE", tt_cp[g - 1])
                    for it in range(2):
                        pvc = s.op("tensor", lambda e, it=it, jc=jc2, h=h,
                                   pt=pt2: e.matmul(
                                       pspv[0:65, it * 512:(it + 1) * 512],
                                       vones[:, jc * 1040 + h * 65:
                                             jc * 1040 + h * 65 + 65],
                                       pT[pt][:, it * 512:(it + 1) * 512],
                                       start=(jc == 0), stop=(jc == TC - 1)),
                                   inc="TM" if it == 1 else None)
                    pv_cp[n2] = pvc

                LAG = 3
                for jc in range(TC):
                    n = nslab
                    nslab += 1
                    sb = n % 2
                    pt = n % NPT
                    if jc == 0 and ihalf == 0:
                        for cpg, sm in dup_cp[pair]:
                            s.wait("tensor", sm, cpg)
                        if h == 0:
                            s.wait("tensor", "VE", ck["v_done"])
                    if n >= 2:
                        s.wait("tensor", "SC", exp_cp[n - 2])
                    s.op("tensor", lambda e, sb=sb, sk0=sk0, sq0=sq0, jc=jc,
                         ihalf=ihalf: e.matmul(
                             slab[sb][:, 0:512],
                             sk0[0:64, jc * 128:(jc + 1) * 128],
                             sq0[0:64, ihalf * 1024:ihalf * 1024 + 512],
                             start=True, stop=True, tile_position=(0, 0)))
                    qkt_cp[n] = s.op("tensor", lambda e, sb=sb, sk0=sk0, sq0=sq0,
                                     jc=jc, ihalf=ihalf: e.matmul(
                                         slab[sb][:, 512:1024],
                                         sk0[64:128, jc * 128:(jc + 1) * 128],
                                         sq0[64:128, ihalf * 1024 + 512:
                                             ihalf * 1024 + 1024],
                                         start=True, stop=True,
                                         tile_position=(64, 0)), inc="TM")
                    s.wait("scalar", "TM", qkt_cp[n])
                    if n >= NPT:
                        s.wait("scalar", "TM", pv_cp[n - NPT])
                    exp_cp[n] = s.op("scalar", lambda e, pt=pt, sb=sb:
                                     e.activation(pT[pt][:], slab[sb][:], AF.Exp,
                                                  bias=neg30[:]), inc="SC")
                    if jc >= LAG:
                        emit_pv(n - LAG, jc - LAG)
                for jc in range(TC - LAG, TC):
                    emit_pv(g_start + jc, jc)
                s.wait("vector", "TM", pv_cp[nslab - 1])
                ecp = s.op("vector", lambda e, g=g: e.tensor_copy(
                    ub[g % 2][0:65, :], pspv[0:65, :]), inc="VE")
                s.wait("scalar", "VE", ecp)
                lcp = s.op("scalar", lambda e, g=g: e.activation(
                    rrow[64:65, :], ub[g % 2][64:65, :], AF.Ln), inc="SC")
                s.wait("scalar", "SC", lcp)
                rcp = s.op("scalar", lambda e: e.activation(
                    rrow[64:65, :], rrow[64:65, :], AF.Exp, scale=-1.0),
                    inc="SC")
                s.wait("tensor", "SC", rcp)
                s.wait("tensor", "VE", ecp)
                for it in range(2):
                    bcp = s.op("tensor", lambda e, it=it: e.matmul(
                        pspv[0:64, it * 512:(it + 1) * 512], onesb[64:65, 0:64],
                        rrow[64:65, it * 512:(it + 1) * 512],
                        start=True, stop=True, tile_position=(64, 0)),
                        inc="TM" if it == 1 else None)
                s.wait("vector", "TM", bcp)
                s.wait("vector", "VE", ecp)
                for par in range(2):
                    src0 = ub[g % 2][0:64, :].rearrange(
                        "p (tb f) -> p f tb", f=16).rearrange(
                        "p (cc pr) tb -> p pr cc tb", pr=2)[:, par]
                    src1 = pspv[0:64, :].rearrange(
                        "p (tb f) -> p f tb", f=16).rearrange(
                        "p (cc pr) tb -> p pr cc tb", pr=2)[:, par]
                    dst = attnb[par * 64:(par + 1) * 64, :].rearrange(
                        "p (cc t) -> p cc t", t=T)[
                        :, :, h * 128 + ihalf * 64:h * 128 + ihalf * 64 + 64]
                    tt_cp[g] = s.op("vector", lambda e, o=dst, a=src0, b=src1:
                                    e.tensor_mul(o, a, b),
                                    inc="VE" if par == 1 else None)
            if eip == 1:
                pair_last_qkt[pair] = qkt_cp[nslab - 1]
    if not dbg_single:
        run_bg(100)
    ck["attn_ve"] = s.cp("VE")
    ck["attn_td"] = s.cp("TD")
    ck["attn_xd"] = s.cp("XD")
    ck["attn_sc"] = s.cp("SC")
    ck["attn_tm"] = s.cp("TM")
    if dbg_single or upto == "attn":
        return _finish_debug(nc, s, sems, dict(attnb=attnb, ub0=ub[0]))

    def load_wffn(src_h, dst, sem, wait_tm=None, wait_ve=None, wait_sc=None):
        if wait_tm:
            s.wait("gpsimd", "TM", wait_tm)
        if wait_ve:
            s.wait("gpsimd", "VE", wait_ve)
        if wait_sc:
            s.wait("gpsimd", "SC", wait_sc)
        for kc in range(C):
            inap = bass.AP(src_h.tensor, kc * 128 * D, [[D, 128], [1, 1024]])
            cpl = s.dma("gpsimd", lambda e, i=inap, kc=kc: e.dma_start(
                out=dst[:, kc * 1024:(kc + 1) * 1024], in_=i), inc=sem)
        return (cpl, sem)

    ck["wrt"] = load_wffn(wrt_h, wslotA, "W1", wait_tm=ck["attn_tm"],
                          wait_ve=ck["attn_ve"], wait_sc=ck["attn_sc"])

    def ln_feature_major(src_tensor, extra_ve_wait=None):
        s2cp = None
        for c in range(C):
            sl = slice(c * T, (c + 1) * T)
            buf = tmpc if c % 2 == 0 else xpc
            if extra_ve_wait and c == 0:
                s.wait("vector", "VE", extra_ve_wait)
            if c >= 2:
                s.wait("vector", "TM", s2cp)
            sqv = s.op("vector", lambda e, sl=sl, buf=buf: e.tensor_mul(
                buf[:], src_tensor[:, sl], src_tensor[:, sl]), inc="VE")
            s.wait("tensor", "VE", sqv)
            for i in range(4):
                s.op("tensor", lambda e, c=c, i=i: e.matmul(
                    psA[32 * i:32 * i + 1, 0:512], onesb[:, 0:1],
                    src_tensor[:, c * T + i * 512:c * T + (i + 1) * 512],
                    start=(c == 0), stop=(c == C - 1), tile_position=(0, 32 * i)))
            for i in range(4):
                s2cp = s.op("tensor", lambda e, i=i, buf=buf, c=c: e.matmul(
                    psA[32 * i:32 * i + 1, 512:1024], onesb[:, 0:1],
                    buf[:, i * 512:(i + 1) * 512],
                    start=(c == 0), stop=(c == C - 1), tile_position=(0, 32 * i)),
                    inc="TM" if i == 3 else None)
        s.wait("vector", "TM", s2cp)
        a1 = s.op("vector", lambda e: e.tensor_scalar_mul(st_a[:], psA[:, 0:512],
                                                          1.0 / D), inc="VE")
        a2 = s.op("vector", lambda e: e.tensor_scalar_mul(st_b[:],
                                                          psA[:, 512:1024],
                                                          1.0 / D), inc="VE")
        s.wait("vector", "VE", a1)
        a3 = s.op("vector", lambda e: e.tensor_mul(st_c[:], st_a[:], st_a[:]),
                  inc="VE")
        s.wait("vector", "VE", a3)
        vcp = s.op("vector", lambda e: e.tensor_sub(st_b[:], st_b[:], st_c[:]),
                   inc="VE")
        s.wait("scalar", "VE", vcp)
        for r in (0, 32, 64, 96):
            lcp = s.op("scalar", lambda e, r=r: e.activation(
                st_b[r:r + 1, :], st_b[r:r + 1, :], AF.Ln, bias=epsc[r:r + 1]),
                inc="SC" if r == 96 else None)
        s.wait("scalar", "SC", lcp)
        scp = s.op("scalar", lambda e: e.activation(sAr[:], st_b[:], AF.Exp,
                                                    scale=-0.5), inc="SC")
        s.wait("vector", "SC", scp)
        vcp2 = s.op("vector", lambda e: e.scalar_tensor_tensor(
            sBr[:], st_a[:], -1.0, sAr[:], ALU.mult, ALU.mult), inc="VE")
        s.wait("tensor", "VE", vcp2)
        for i in range(4):
            acp = s.op("tensor", lambda e, i=i: e.matmul(
                psB[:, i * 512:(i + 1) * 512], onesb[32 * i:32 * i + 1, :],
                sAr[32 * i:32 * i + 1, :], start=True, stop=True,
                tile_position=(32 * i, 0)), inc="TM" if i == 3 else None)
        s.wait("vector", "TM", acp)
        va = s.op("vector", lambda e: e.tensor_copy(lnA[:], psB[:]), inc="VE")
        s.wait("tensor", "VE", va)
        for i in range(4):
            bcp = s.op("tensor", lambda e, i=i: e.matmul(
                psB[:, i * 512:(i + 1) * 512], onesb[32 * i:32 * i + 1, :],
                sBr[32 * i:32 * i + 1, :], start=True, stop=True,
                tile_position=(32 * i, 0)), inc="TM" if i == 3 else None)
        s.wait("vector", "TM", bcp)
        return s.op("vector", lambda e: e.tensor_copy(lnB[:], psB[:]), inc="VE")

    s.wait("vector", "VE", ck["attn_ve"])
    s.wait("vector", "TD", ck["attn_td"])
    s.wait("vector", "XD", ck["attn_xd"])
    s.wait("tensor", "SC", ck["attn_sc"])
    ln2 = ln_feature_major(attnb)
    s.wait("vector", "VE", ln2)
    bmcp = s.op("vector", lambda e: e.tensor_add(lnBM[:], lnB[:], meanb[:]),
                inc="VE")
    s.wait("vector", "VE", bmcp)
    for c in range(C):
        sl = slice(c * T, (c + 1) * T)
        p1 = s.op("vector", lambda e, sl=sl: e.tensor_mul(
            attnb[:, sl], attnb[:, sl], lnA[:]), inc="VE")
    s.wait("vector", "VE", p1)
    for c in range(C):
        sl = slice(c * T, (c + 1) * T)
        p2 = s.op("vector", lambda e, sl=sl: e.tensor_add(
            attnb[:, sl], attnb[:, sl], lnBM[:]), inc="VE")
    for c in range(C):
        sl = slice(c * T, (c + 1) * T)
        p3 = s.op("vector", lambda e, sl=sl: e.tensor_mul(
            xmain[:, sl], xmain[:, sl], stdb[:]), inc="VE")
    s.wait("vector", "VE", p3)
    for c in range(C):
        sl = slice(c * T, (c + 1) * T)
        x2cp = s.op("vector", lambda e, sl=sl: e.tensor_add(
            xmain[:, sl], xmain[:, sl], attnb[:, sl]), inc="VE")
    ck["x2"] = x2cp
    if upto == "x2":
        return _finish_debug(nc, s, sems, dict(xmain=xmain))

    ln3 = ln_feature_major(xmain, extra_ve_wait=ck["x2"])
    s.wait("vector", "VE", ln3)
    for c in range(C):
        sl = slice(c * T, (c + 1) * T)
        l3p = s.op("vector", lambda e, sl=sl: e.tensor_mul(
            attnb[:, sl], xmain[:, sl], lnA[:]), inc="VE")
    s.wait("vector", "VE", l3p)
    for c in range(C):
        sl = slice(c * T, (c + 1) * T)
        xn3cp = s.op("vector", lambda e, sl=sl: e.tensor_add(
            attnb[:, sl], attnb[:, sl], lnB[:]), inc="VE")
    ck["xn3"] = xn3cp
    if upto == "xn3":
        return _finish_debug(nc, s, sems, dict(attnb=attnb))

    ck["wkt"] = load_wffn(wkt_h, wslotB, "W2", wait_ve=ck["xn3"])

    def emit_mix(base, dst, wait_ve_first):
        mcps = []
        tcps = {}
        for c in range(C):
            sl = slice(c * T, (c + 1) * T)
            mc = base + c
            buf = tmpc2 if c % 2 == 0 else xpc
            if c == 0:
                s.wait("vector", "VE", wait_ve_first)
                s.op("vector", lambda e: e.memset(tmpc2[:, 0:1], 0.0))
                s.op("vector", lambda e: e.memset(xpc[:, 0:1], 0.0))
            if c >= 2:
                s.wait("vector", "VE", mcps[c - 2])
            tcps[c] = s.op("vector", lambda e, mc=mc, c=c, buf=buf:
                           e.tensor_scalar(
                               buf[:, 1:T], attnb[:, c * T:c * T + T - 1],
                               mu1m[:, mc:mc + 1], None, ALU.mult), inc="VE")
            s.wait("vector", "VE", tcps[c])
            mcps.append(s.op("vector", lambda e, sl=sl, mc=mc, dst=dst,
                             buf=buf: e.scalar_tensor_tensor(
                                 dst[:, sl], attnb[:, sl], mu_s[:, mc:mc + 1],
                                 buf[:], ALU.mult, ALU.add), inc="VE"))
        return mcps[-1]

    ck["xr"] = emit_mix(0, xr_t, ck["xn3"])

    def ffn_mm(wslot, src, dc, wait_lg):
        ps = psA if dc % 2 == 0 else psB
        s.wait("tensor", wait_lg[1], wait_lg[0])
        for nt in range(4):
            for kc in range(C):
                cpt = s.op("tensor", lambda e, ps=ps, nt=nt, kc=kc, dc=dc,
                           wslot=wslot, src=src: e.matmul(
                               ps[:, nt * 512:(nt + 1) * 512],
                               wslot[:, kc * 1024 + dc * 128:
                                     kc * 1024 + (dc + 1) * 128],
                               src[:, kc * T + nt * 512:kc * T + (nt + 1) * 512],
                               start=(kc == 0), stop=(kc == C - 1)),
                           inc="TM" if (nt == 3 and kc == C - 1) else None)
        return cpt, ps

    rexp_cp, rmm_cp = {}, {}
    for dc in range(C):
        if dc == 0:
            s.wait("tensor", "VE", ck["xr"])
        if dc >= 2:
            s.wait("tensor", "SC", rexp_cp[dc - 2])
        rmm_cp[dc], ps = ffn_mm(wslotA, xr_t, dc, ck["wrt"])
        s.wait("scalar", "TM", rmm_cp[dc])
        rexp_cp[dc] = s.op("scalar", lambda e, ps=ps, dc=dc: e.activation(
            rsig_t[:, dc * T:(dc + 1) * T], ps[:], AF.Sigmoid), inc="SC")
    ck["rsig"] = rexp_cp[C - 1]
    ck["rmm_last"] = rmm_cp[C - 1]
    if upto == "rsig":
        return _finish_debug(nc, s, sems, dict(rsig=rsig_t, xr=xr_t))

    ck["wvt"] = load_wffn(wvt_h, wslotA, "W1", wait_tm=ck["rmm_last"])

    s.wait("vector", "TM", ck["rmm_last"])
    ck["xk"] = emit_mix(8, xr_t, ck["xn3"])

    kact_cp = {}
    krl_cp = {}
    for dc in range(C):
        if dc == 0:
            s.wait("tensor", "VE", ck["xk"])
            s.wait("tensor", "SC", rexp_cp[C - 1])
        if dc >= 2:
            s.wait("tensor", "VE", kact_cp[dc - 2])
        kcp, ps = ffn_mm(wslotB, xr_t, dc, ck["wkt"])
        kbuf = tmpc if dc % 2 == 0 else tmpc2
        s.wait("scalar", "TM", kcp)
        if dc >= 2:
            s.wait("scalar", "VE", kact_cp[dc - 2])
        krl_cp[dc] = s.op("scalar", lambda e, ps=ps, kbuf=kbuf: e.activation(
            kbuf[:], ps[:], AF.Relu), inc="SC")
        s.wait("vector", "SC", krl_cp[dc])
        kact_cp[dc] = s.op("vector", lambda e, dc=dc, kbuf=kbuf: e.tensor_mul(
            attnb[:, dc * T:(dc + 1) * T], kbuf[:], kbuf[:]), inc="VE")
    ck["k2"] = kact_cp[C - 1]
    if upto == "k2":
        return _finish_debug(nc, s, sems, dict(k2=attnb))

    fout_cp = {}
    for dc in range(C):
        if dc == 0:
            s.wait("tensor", "VE", ck["k2"])
            s.wait("tensor", "SC", ck["rsig"])
        if dc >= 2:
            s.wait("tensor", "VE", fout_cp[dc - 2])
        vcp, ps = ffn_mm(wslotA, attnb, dc, ck["wvt"])
        s.wait("vector", "TM", vcp)
        fout_cp[dc] = s.op("vector", lambda e, ps=ps, dc=dc: e.tensor_mul(
            xr_t[:, dc * T:(dc + 1) * T], ps[:],
            rsig_t[:, dc * T:(dc + 1) * T]), inc="VE")
    ck["ffnout"] = fout_cp[C - 1]
    if upto == "ffnout":
        return _finish_debug(nc, s, sems, dict(ffnout=xr_t))

    ln4 = ln_feature_major(xr_t, extra_ve_wait=ck["ffnout"])
    s.wait("vector", "VE", ln4)
    for c in range(C):
        sl = slice(c * T, (c + 1) * T)
        l4p = s.op("vector", lambda e, sl=sl: e.tensor_mul(
            xr_t[:, sl], xr_t[:, sl], lnA[:]), inc="VE")
    s.wait("vector", "VE", l4p)
    for c in range(C):
        sl = slice(c * T, (c + 1) * T)
        lncp4 = s.op("vector", lambda e, sl=sl: e.tensor_add(
            xr_t[:, sl], xr_t[:, sl], lnB[:]), inc="VE")
    outst = []
    outdma = []
    stcp = None
    for c in range(C):
        sl = slice(c * T, (c + 1) * T)
        ps = psA if c % 2 == 0 else psB
        ot = otstage if c % 2 == 0 else otstage2
        odeng = "sync" if c % 2 == 0 else "scalar"
        odsem = "OD" if c % 2 == 0 else "OD2"
        if c == 0:
            s.wait("vector", "VE", lncp4)
        else:
            s.wait("vector", "TM", prev_trcp)
        ocp = s.op("vector", lambda e, sl=sl: e.tensor_add(
            ostage[:], xmain[:, sl], xr_t[:, sl]), inc="VE")
        s.wait("tensor", "VE", ocp)
        if c >= 2:
            s.wait("tensor", "VE", outst[c - 2])
        for t in range(TC):
            trcp = s.op("tensor", lambda e, ps=ps, t=t: e.transpose(
                ps[:, t * 128:(t + 1) * 128], ostage[:, t * 128:(t + 1) * 128],
                identf[:]), inc="TM" if t == TC - 1 else None)
        prev_trcp = trcp
        s.wait("vector", "TM", trcp)
        if c >= 2:
            s.wait("vector", odsem, outdma[c - 2])
        vst = s.op("vector", lambda e, ps=ps, ot=ot: e.tensor_copy(
            ot[:], ps[:]), inc="VE")
        outst.append(vst)
        s.wait(odeng, "VE", vst)
        outap = bass.AP(out_h.tensor, c * 128,
                        [[D, 128], [128 * D, 16], [1, 128]])
        stcp = s.dma(odeng, lambda e, o=outap, ot=ot: e.dma_start(
            out=o, in_=ot[:]), inc=odsem)
        outdma.append(stcp)
    s.wait("gpsimd", "OD", s.cp("OD"))
    s.wait("gpsimd", "OD2", s.cp("OD2"))
    s.wait("sync", "OD", s.cp("OD"))
    s.wait("sync", "OD2", s.cp("OD2"))

    with nc.allow_low_precision("bf16 intermediates; fp32 psum accumulation"), \
         nc.Block() as blk:
        blk.tensor(s.run("tensor"))
        blk.scalar(s.run("scalar"))
        blk.vector(s.run("vector"))
        blk.sync(s.run("sync"))
        blk.gpsimd(s.run("gpsimd"))
    return nc


_NC_CACHE = {}


def _get_program():
    if "nc" not in _NC_CACHE:
        _NC_CACHE["nc"] = build_program()
    return _NC_CACHE["nc"]


def kernel(L0_attn_x, L0_attn_W_qkv, L0_attn_b_qkv, L0_ffn_W_r, L0_ffn_W_k,
           L0_ffn_W_v, L0_ffn_mu_r, L0_ffn_mu_k):
    x = np.asarray(L0_attn_x, dtype=np.float32)
    wqkv = np.ascontiguousarray(np.asarray(L0_attn_W_qkv, dtype=np.float32))
    bqkv = np.asarray(L0_attn_b_qkv, dtype=np.float32)
    bqk = np.ascontiguousarray(bqkv[:2048].reshape(16, 128).T)
    bv = np.ascontiguousarray(bqkv[2048:].reshape(1, 1024))
    wrt = np.ascontiguousarray(np.asarray(L0_ffn_W_r, dtype=np.float32).T)
    wkt = np.ascontiguousarray(np.asarray(L0_ffn_W_k, dtype=np.float32).T)
    wvt = np.ascontiguousarray(np.asarray(L0_ffn_W_v, dtype=np.float32).T)
    mu_r = np.asarray(L0_ffn_mu_r, dtype=np.float32)
    mu_k = np.asarray(L0_ffn_mu_k, dtype=np.float32)
    mu = np.ascontiguousarray(
        np.concatenate([mu_r.reshape(8, 128).T, mu_k.reshape(8, 128).T], axis=1))

    nc = _get_program()
    in_maps = []
    for b in range(8):
        in_maps.append({
            "x": np.ascontiguousarray(x[b]),
            "wqkv": wqkv, "bqk": bqk, "bv": bv,
            "wrt": wrt, "wkt": wkt, "wvt": wvt, "mu": mu,
        })
    res = run_bass_kernel_spmd(nc, in_maps, core_ids=list(range(8)))
    out = np.stack([res.results[b]["out"] for b in range(8)], axis=0)
    return out.astype(np.float32)


# version 11 (final)
# speedup vs baseline: 1.9131x; 1.0155x over previous
import numpy as np
import ml_dtypes
import concourse.bass as bass
import concourse.mybir as mybir
from concourse.bass_utils import run_bass_kernel_spmd

F32 = mybir.dt.float32
BF16 = mybir.dt.bfloat16
AF = mybir.ActivationFunctionType
ALU = mybir.AluOpType
AX = mybir.AxisListType
bf16 = ml_dtypes.bfloat16

T = 2048
D = 1024
C = 8
TC = 16
NPT = 5

ENGS = ("tensor", "scalar", "vector", "sync", "gpsimd")


class Sched:
    def __init__(self, sems):
        self.ops = {e: [] for e in ENGS}
        self.cnt = {n: 0 for n in sems}
        self.sems = sems

    def op(self, eng, fn, inc=None, incval=1):
        if inc is not None:
            self.cnt[inc] += incval
            self.ops[eng].append(("op", fn, self.sems[inc], incval))
            return self.cnt[inc]
        self.ops[eng].append(("op", fn, None, 0))
        return None

    def dma(self, eng, fn, inc):
        return self.op(eng, fn, inc=inc, incval=16)

    def wait(self, eng, sem_name, val):
        if val is None or val <= 0:
            return
        self.ops[eng].append(("wait", self.sems[sem_name], val))

    def cp(self, sem_name):
        return self.cnt[sem_name]

    def run(self, eng_name):
        def body(e):
            for item in self.ops[eng_name]:
                if item[0] == "wait":
                    e.wait_ge(item[1], item[2])
                else:
                    _, fn, sem, incval = item
                    r = fn(e)
                    if sem is not None:
                        r.then_inc(sem, incval)
        return body


def _finish_debug(nc, s, sems, taps):
    outs = {}
    for name, t in taps.items():
        d = nc.dram_tensor("tap_" + name, [t.shape[0], int(np.prod(t.shape[1:]))],
                           t.dtype, kind="ExternalOutput").ap()
        outs[name] = d
    for sem_name in s.cnt:
        if sem_name != "OD":
            s.wait("sync", sem_name, s.cp(sem_name))
    last = None
    for name, t in taps.items():
        last = s.dma("sync", lambda e, t=t, d=outs[name]: e.dma_start(
            out=d, in_=t[:, :]), inc="OD")
    s.wait("sync", "OD", last)
    s.wait("gpsimd", "OD", last)
    with nc.allow_low_precision("debug"), nc.Block() as blk:
        blk.tensor(s.run("tensor"))
        blk.scalar(s.run("scalar"))
        blk.vector(s.run("vector"))
        blk.sync(s.run("sync"))
        blk.gpsimd(s.run("gpsimd"))
    return nc


def build_program(upto=None):
    nc = bass.Bass("TRN2", target_bir_lowering=False)

    x_h = nc.dram_tensor("x", [T, D], F32, kind="ExternalInput").ap()
    wqkv_h = nc.dram_tensor("wqkv", [D, 3 * D], F32, kind="ExternalInput").ap()
    bqk_h = nc.dram_tensor("bqk", [128, 16], F32, kind="ExternalInput").ap()
    bv_h = nc.dram_tensor("bv", [1, D], F32, kind="ExternalInput").ap()
    wrt_h = nc.dram_tensor("wrt", [D, D], F32, kind="ExternalInput").ap()
    wkt_h = nc.dram_tensor("wkt", [D, D], F32, kind="ExternalInput").ap()
    wvt_h = nc.dram_tensor("wvt", [D, D], F32, kind="ExternalInput").ap()
    mu_h = nc.dram_tensor("mu", [128, 16], F32, kind="ExternalInput").ap()
    out_h = nc.dram_tensor("out", [T, D], F32, kind="ExternalOutput").ap()

    SB = nc.sbuf_top
    cur = [nc.sbuf_base]

    def alloc(name, shape, dtype, at=None):
        nb = int(np.prod(shape[1:])) * mybir.dt.size(dtype)
        if at is None:
            at = (cur[0] + 31) // 32 * 32
            cur[0] = at + nb
        return nc.alloc_sbuf_tensor_at(name, shape, dtype, offset=at)

    def addr(t):
        return nc.lookup_mloc(t).addr

    xmain = alloc("xmain", [128, C * T], BF16)
    attnb = alloc("attnb", [128, C * T], BF16)
    vones = alloc("vones", [128, 16 * 1040], BF16)
    scr = {}
    for nm in ("q0", "q1", "k0", "k1"):
        scr[nm] = [alloc(f"scr_{nm}_{i}", [128, T], BF16) for i in range(2)]
    pT = [alloc(f"pT{i}", [128, 1024], BF16) for i in range(NPT)]
    wring = [alloc(f"wring{i}", [128, 1024], BF16) for i in range(4)]
    ub = [alloc(f"ub{i}", [128, 1024], BF16) for i in range(2)]
    ttst = alloc("ttst", [128, 1024], BF16)
    pT.append(nc.alloc_sbuf_tensor_at("pT4", [128, 1024], BF16,
                                      offset=addr(ttst)))
    rrow = alloc("rrow", [128, 1024], BF16)
    lnA = alloc("lnA", [128, T], BF16)
    lnB = alloc("lnB", [128, T], BF16)
    lnBM = alloc("lnBM", [128, T], BF16)
    stdb = alloc("stdb", [128, T], BF16)
    meanb = alloc("meanb", [128, T], BF16)
    st_a = alloc("st_a", [128, 512], F32)
    st_b = alloc("st_b", [128, 512], F32)
    st_c = alloc("st_c", [128, 512], F32)
    sAr = alloc("sAr", [128, 512], BF16)
    sBr = alloc("sBr", [128, 512], BF16)
    onesb = alloc("onesb", [128, 128], BF16)
    identf = alloc("identf", [128, 128], F32)
    bqk_s = alloc("bqk_s", [128, 16], F32)
    mu_s = alloc("mu_s", [128, 16], F32)
    mu1m = alloc("mu1m", [128, 16], F32)
    bvrow = alloc("bvrow", [1, D], BF16)
    neg30 = alloc("neg30", [128, 1], F32)
    epsc = alloc("epsc", [128, 1], F32)
    statc = alloc("statc", [128, 64], F32)
    stT = alloc("stT", [64, 128], F32)
    xpc = alloc("xpc", [128, T], BF16)
    tmpc = alloc("tmpc", [128, T], BF16)
    tmpc2 = alloc("tmpc2", [128, T], BF16)
    ostage = alloc("ostage", [128, T], F32)

    a_at = addr(attnb)
    xstage = [nc.alloc_sbuf_tensor_at(f"xstage{i}", [128, 1024], F32,
                                      offset=a_at + i * 4096) for i in range(2)]
    sqstage = [nc.alloc_sbuf_tensor_at(f"sqstage{i}", [128, 1024], F32,
                                       offset=a_at + 8192 + i * 4096)
               for i in range(2)]
    wvfull = nc.alloc_sbuf_tensor_at("wvfull", [128, 8 * 1024], BF16,
                                     offset=a_at + 16384)
    xr_t = nc.alloc_sbuf_tensor_at("xr_t", [128, C * T], BF16,
                                   offset=addr(scr["q0"][0]))
    rowsb = nc.alloc_sbuf_tensor_at("rowsb", [1, 4 * T], BF16,
                                    offset=addr(scr["k0"][0]))
    rsig_t = nc.alloc_sbuf_tensor_at("rsig_t", [128, C * T], BF16,
                                     offset=addr(vones))
    otstage = nc.alloc_sbuf_tensor_at("otstage", [128, T], F32,
                                      offset=addr(vones))
    otstage2 = nc.alloc_sbuf_tensor_at("otstage2", [128, T], F32,
                                       offset=addr(vones) + 8192)
    wslotA = nc.alloc_sbuf_tensor_at("wslotA", [128, 8 * 1024], BF16,
                                     offset=addr(pT[0]))
    wslotB = nc.alloc_sbuf_tensor_at("wslotB", [128, 8 * 1024], BF16,
                                     offset=addr(stdb))
    assert cur[0] <= SB, (cur[0], SB)

    nc.psum_base = 0
    PS = nc.alloc_psum_tensor("PS", [128, 4096], F32)
    psT = [PS[:, 0:1024], PS[:, 1024:2048]]
    psBC = PS[:, 2048:4096]
    slab = [PS[:, 0:1024], PS[:, 1024:2048]]
    pspv = PS[:, 2048:3072]
    psmisc = PS[:, 3072:4096]
    psA = PS[:, 0:2048]
    psB = PS[:, 2048:4096]

    sems = {n: nc.alloc_semaphore(n) for n in
            ("LD0", "LD1", "LDC", "LGC", "RD", "LG0", "LG1", "LG2", "LG3",
             "DPa", "DPb", "DQa", "DQb", "TD", "XD", "OD", "OD2", "W1", "W2",
             "TM", "SC", "VE", "GP")}
    s = Sched(sems)
    ck = {}

    idm = s.op("gpsimd", lambda e: e.memset(identf[:], 0.0), inc="GP")
    s.op("gpsimd", lambda e: e.memset(onesb[:], 1.0))
    s.op("gpsimd", lambda e: e.memset(neg30[:], -30.0))
    s.wait("gpsimd", "GP", idm)
    s.op("gpsimd", lambda e: e.affine_select(
        out=identf[:], in_=identf[:], compare_op=ALU.not_equal, fill=1.0,
        base=0, pattern=[[-1, 128]], channel_multiplier=1))
    for jc in range(16):
        ap = bass.AP(vones, jc * 1040 + 64, [[vones.shape[1], 128], [65, 16]])
        s.op("gpsimd", lambda e, ap=ap: e.memset(ap, 1.0))
    ck["g_consts"] = s.op("gpsimd", lambda e: e.memset(epsc[:], 1e-5), inc="GP")

    s.dma("sync", lambda e: e.dma_start(out=bqk_s[:], in_=bqk_h), inc="LDC")
    ck["ld_consts"] = s.dma("sync", lambda e: e.dma_start(out=mu_s[:], in_=mu_h),
                            inc="LDC")
    s.dma("gpsimd", lambda e: e.dma_start(out=bvrow[:], in_=bv_h), inc="LGC")
    for kc in range(C):
        inap = bass.AP(wqkv_h.tensor, kc * 128 * 3072 + 2048,
                       [[3072, 128], [1, 1024]])
        s.dma("gpsimd", lambda e, i=inap, kc=kc: e.dma_start(
            out=wvfull[:, kc * 1024:(kc + 1) * 1024], in_=i), inc="LGC")
    ck["lg_wv"] = s.cp("LGC")

    s.wait("vector", "LDC", ck["ld_consts"])
    ck["ve_mu"] = s.op("vector", lambda e: e.tensor_scalar(
        mu1m[:], mu_s[:], -1.0, 1.0, ALU.mult, ALU.add), inc="VE")

    xld, trd, evd, sqd = [], [], [], []
    for tc in range(TC):
        b = tc % 2
        lsem = "LD0" if b == 0 else "LD1"
        ldeng = "sync" if b == 0 else "scalar"
        if tc >= 2:
            s.wait(ldeng, "TM", trd[tc - 2])
            s.wait(ldeng, "VE", sqd[tc - 2])
        xld.append(s.dma(ldeng, lambda e, tc=tc, b=b: e.dma_start(
            out=xstage[b][:], in_=x_h[tc * 128:(tc + 1) * 128, :]), inc=lsem))
        s.wait("vector", lsem, xld[tc])
        sqw = s.op("vector", lambda e, b=b: e.tensor_mul(
            sqstage[b][:], xstage[b][:], xstage[b][:]), inc="VE")
        s.op("vector", lambda e, tc=tc, b=b: e.reduce_sum(
            statc[:, tc:tc + 1], xstage[b][:], axis=AX.X))
        s.wait("vector", "VE", sqw)
        sqd.append(s.op("vector", lambda e, tc=tc, b=b: e.reduce_sum(
            statc[:, 32 + tc:33 + tc], sqstage[b][:], axis=AX.X), inc="VE"))
        s.wait("tensor", lsem, xld[tc])
        if tc == 0:
            s.wait("tensor", "GP", ck["g_consts"])
        if tc >= 2:
            s.wait("tensor", "VE", evd[tc - 2])
        for c in range(C):
            trc = s.op("tensor", lambda e, b=b, c=c: e.transpose(
                psT[b][:, c * 128:(c + 1) * 128],
                xstage[b][:, c * 128:(c + 1) * 128], identf[:]),
                inc="TM" if c == C - 1 else None)
        trd.append(trc)
        s.wait("vector", "TM", trd[tc])
        oap = xmain[:, :].rearrange("p (c t) -> p c t", t=T)[:, :, tc * 128:
                                                            (tc + 1) * 128]
        iap = psT[b][:, :].rearrange("p (c f) -> p c f", f=128)
        evd.append(s.op("vector", lambda e, o=oap, i=iap: e.tensor_copy(o, i),
                        inc="VE"))
    ck["x_in"] = s.cp("VE")
    if upto == "stats_raw":
        return _finish_debug(nc, s, sems, dict(statc=statc))

    s.wait("vector", "VE", ck["x_in"])
    c1 = s.op("vector", lambda e: e.tensor_scalar_mul(
        statc[:, 0:16], statc[:, 0:16], 1.0 / D), inc="VE")
    c2 = s.op("vector", lambda e: e.tensor_scalar_mul(
        statc[:, 32:48], statc[:, 32:48], 1.0 / D), inc="VE")
    s.wait("scalar", "VE", c1)
    m2cp = s.op("scalar", lambda e: e.activation(st_b[:, 0:16], statc[:, 0:16],
                                                 AF.Square), inc="SC")
    s.wait("vector", "SC", m2cp)
    s.wait("vector", "VE", c2)
    ck["ln1_var"] = s.op("vector", lambda e: e.tensor_sub(
        statc[:, 32:48], statc[:, 32:48], st_b[:, 0:16]), inc="VE")
    s.wait("scalar", "GP", ck["g_consts"])
    s.wait("scalar", "VE", ck["ln1_var"])
    lncp = s.op("scalar", lambda e: e.activation(statc[:, 32:48], statc[:, 32:48],
                                                 AF.Ln, bias=epsc[:]), inc="SC")
    s.wait("scalar", "SC", lncp)
    rcp1 = s.op("scalar", lambda e: e.activation(statc[:, 16:32], statc[:, 32:48],
                                                 AF.Exp, scale=-0.5), inc="SC")
    s.wait("scalar", "SC", rcp1)
    ck["ln1_se"] = s.op("scalar", lambda e: e.activation(
        statc[:, 32:48], statc[:, 32:48], AF.Exp, scale=0.5), inc="SC")
    s.wait("vector", "SC", ck["ln1_se"])
    ck["ln1_cols"] = s.op("vector", lambda e: e.scalar_tensor_tensor(
        statc[:, 48:64], statc[:, 0:16], -1.0, statc[:, 16:32],
        ALU.mult, ALU.mult), inc="VE")

    s.wait("tensor", "VE", ck["ln1_cols"])
    ck["ln1_tr"] = s.op("tensor", lambda e: e.transpose(
        psT[0][0:64, 0:128], statc[:, :], identf[:]), inc="TM")
    s.wait("vector", "TM", ck["ln1_tr"])
    ck["ln1_trev"] = s.op("vector", lambda e: e.tensor_copy(
        stT[:], psT[0][0:64, 0:128]), inc="VE")
    s.wait("gpsimd", "VE", ck["ln1_trev"])
    for src_col, ri in ((16, 0), (48, 1), (32, 2), (0, 3)):
        outap = bass.AP(rowsb, ri * T, [[rowsb.shape[1], 1], [1, T]])
        s.dma("gpsimd", lambda e, sc_=src_col, o=outap: e.dma_start(
            out=o, in_=stT[sc_:sc_ + 16, :]), inc="RD")
    ck["ln1_rows"] = s.cp("RD")

    prev_bc_ev = [None]

    def bcast_row(row_off, dst):
        if prev_bc_ev[0] is not None:
            s.wait("tensor", "VE", prev_bc_ev[0])
        for i in range(4):
            cpt = s.op("tensor", lambda e, i=i: e.matmul(
                psBC[:, i * 512:(i + 1) * 512], onesb[0:1, :],
                rowsb[0:1, row_off + i * 512:row_off + (i + 1) * 512],
                start=True, stop=True), inc="TM" if i == 3 else None)
        s.wait("vector", "TM", cpt)
        prev_bc_ev[0] = s.op("vector", lambda e, dst=dst: e.tensor_copy(
            dst[:], psBC[:]), inc="VE")
        return prev_bc_ev[0]

    s.wait("tensor", "RD", ck["ln1_rows"])
    bcast_row(0 * T, lnA)
    bcast_row(1 * T, lnB)
    bcast_row(2 * T, stdb)
    ck["ln1_bcast"] = bcast_row(3 * T, meanb)

    s.wait("vector", "VE", ck["ln1_bcast"])
    for c in range(C):
        sl = slice(c * T, (c + 1) * T)
        mpc = s.op("vector", lambda e, sl=sl: e.tensor_mul(
            xmain[:, sl], xmain[:, sl], lnA[:]), inc="VE")
    s.wait("vector", "VE", mpc)
    for c in range(C):
        sl = slice(c * T, (c + 1) * T)
        apl = s.op("vector", lambda e, sl=sl: e.tensor_add(
            xmain[:, sl], xmain[:, sl], lnB[:]), inc="VE")
    ck["ln1_apply"] = apl
    if upto == "ln1":
        return _finish_debug(nc, s, sems, dict(xmain=xmain, lnA=lnA, lnB=lnB,
                                               stdb=stdb, meanb=meanb,
                                               statc=statc, rowsb=rowsb))

    wload = {}

    def load_w_chunk(m, slot):
        pair, which = (m, "q") if m < 8 else (m - 8, "k")
        if pair >= 2:
            s.wait("gpsimd", "VE", qk_evac[(pair - 2, which)])
        inap = bass.AP(wqkv_h.tensor, m * 128,
                       [[3072, 128], [3072 * 128, 8], [1, 128]])
        wload[m] = (s.dma("gpsimd", lambda e, i=inap, slot=slot: e.dma_start(
            out=wring[slot][:], in_=i), inc=f"LG{slot}"), f"LG{slot}")

    qk_evac = {}
    pair_last_qkt = {}
    prev_misc_user = [None]

    def emit_qk_half(pair, which, half):
        m = pair if which == "q" else 8 + pair
        slot = (2 * pair + (0 if which == "q" else 1)) % 4
        s.wait("tensor", wload[m][1], wload[m][0])
        if pair == 0 and which == "q" and half == 0:
            s.wait("tensor", "VE", ck["ln1_apply"])
        if prev_misc_user[0] is not None:
            s.wait("tensor", "VE", prev_misc_user[0])
        for nt in range(2):
            for kc in range(C):
                cpt = s.op("tensor", lambda e, nt=nt, kc=kc, slot=slot, half=half:
                           e.matmul(psmisc[:, nt * 512:(nt + 1) * 512],
                                    wring[slot][:, kc * 128:(kc + 1) * 128],
                                    xmain[:, kc * T + half * 1024 + nt * 512:
                                          kc * T + half * 1024 + (nt + 1) * 512],
                                    start=(kc == 0), stop=(kc == C - 1)),
                           inc="TM" if (nt == 1 and kc == C - 1) else None)
        s.wait("vector", "TM", cpt)
        if (pair - 2) in pair_last_qkt:
            s.wait("vector", "TM", pair_last_qkt[pair - 2])
        hsl = slice(half * 1024, (half + 1) * 1024)
        e0 = scr[which + "0"][pair % 2]
        e1 = scr[which + "1"][pair % 2]
        s.op("vector", lambda e, e0=e0, hsl=hsl, m=m: e.tensor_scalar(
            e0[0:64, hsl], psmisc[0:64, :], bqk_s[0:64, m:m + 1], None, ALU.add))
        vcp = s.op("vector", lambda e, e1=e1, hsl=hsl, m=m: e.tensor_scalar(
            e1[64:128, hsl], psmisc[64:128, :], bqk_s[64:128, m:m + 1], None,
            ALU.add), inc="VE")
        prev_misc_user[0] = vcp
        if half == 1:
            qk_evac[(pair, which)] = vcp

    dup_cp = {}

    def emit_dups(pair):
        cps = []
        for which, eng in (("q", "sync"), ("k", "scalar")):
            sm = ("DP" if which == "q" else "DQ") + ("a" if pair % 2 == 0
                                                     else "b")
            s.wait(eng, "VE", max(qk_evac[(pair, "q")], qk_evac[(pair, "k")]))
            t0 = scr[which + "0"][pair % 2]
            t1 = scr[which + "1"][pair % 2]
            s.dma(eng, lambda e, t=t0: e.dma_start(out=t[64:128, :],
                                                   in_=t[0:64, :]), inc=sm)
            cpg = s.dma(eng, lambda e, t=t1: e.dma_start(out=t[0:64, :],
                                                         in_=t[64:128, :]),
                        inc=sm)
            cps.append((cpg, sm))
        dup_cp[pair] = cps

    load_w_chunk(0, 0)
    load_w_chunk(8, 1)
    for which in ("q", "k"):
        for half in range(2):
            emit_qk_half(0, which, half)
    emit_dups(0)
    if upto == "qk0":
        return _finish_debug(nc, s, sems, dict(
            q0=scr["q0"][0], q1=scr["q1"][0], k0=scr["k0"][0], k1=scr["k1"][0]))

    vevac = []
    for jc in range(TC):
        ps = slab[jc % 2]
        if jc == 0:
            s.wait("tensor", "LGC", s.cp("LGC"))
        if jc >= 2:
            s.wait("tensor", "VE", vevac[jc - 2])
        for nt in range(2):
            for kc in range(C):
                s.op("tensor", lambda e, ps=ps, nt=nt, kc=kc, jc=jc: e.matmul(
                    ps[:, nt * 512:(nt + 1) * 512],
                    xmain[:, kc * T + jc * 128:kc * T + (jc + 1) * 128],
                    wvfull[:, kc * 1024 + nt * 512:kc * 1024 + (nt + 1) * 512],
                    start=(kc == 0), stop=False))
            cpt = s.op("tensor", lambda e, ps=ps, nt=nt: e.matmul(
                ps[:, nt * 512:(nt + 1) * 512], onesb[0:1, :],
                bvrow[0:1, nt * 512:(nt + 1) * 512], start=False, stop=True),
                inc="TM" if nt == 1 else None)
        s.wait("vector", "TM", cpt)
        oap = vones[:, jc * 1040:jc * 1040 + 1040].rearrange(
            "p (h f) -> p h f", f=65)[:, :, 0:64]
        iap = ps[:, :].rearrange("p (h f) -> p h f", f=64)
        vevac.append(s.op("vector", lambda e, o=oap, i=iap: e.tensor_copy(o, i),
                          inc="VE"))
    ck["v_done"] = s.cp("VE")
    if upto == "v":
        return _finish_debug(nc, s, sems, dict(vones=vones))

    bg = []
    for pair in range(1, 8):
        bg.append(("loadw", pair, "q"))
        bg.append(("qk", pair, "q", 0))
        bg.append(("qk", pair, "q", 1))
        bg.append(("loadw", pair, "k"))
        bg.append(("qk", pair, "k", 0))
        bg.append(("qk", pair, "k", 1))
        bg.append(("dups", pair))
    bg_i = [0]

    def run_bg(n):
        for _ in range(n):
            if bg_i[0] >= len(bg):
                return
            item = bg[bg_i[0]]
            bg_i[0] += 1
            if item[0] == "loadw":
                _, pair, which = item
                load_w_chunk(pair if which == "q" else 8 + pair,
                             (2 * pair + (0 if which == "q" else 1)) % 4)
            elif item[0] == "qk":
                emit_qk_half(item[1], item[2], item[3])
            else:
                emit_dups(item[1])

    exp_cp, qkt_cp, pv_cp, tt_cp = {}, {}, {}, {}
    stg_dma_cp = {}
    last_ttdma = [0]
    nslab = 0
    dbg_single = upto in ("attn1", "attn1b")
    npair = 1 if dbg_single else 8
    neip = 1 if dbg_single else 2
    nih = 1 if upto == "attn1" else 2
    for pair in range(npair):
        for eip in range(neip):
            h = 2 * pair + eip
            sq0 = scr["q" + str(eip)][pair % 2]
            sk0 = scr["k" + str(eip)][pair % 2]
            if not dbg_single:
                run_bg(3 if h == 0 else (4 if h % 2 == 1 else 3))
            for ihalf in range(nih):
                g = h * 2 + ihalf
                g_start = nslab

                def emit_pv(n2, jc2, h=h, g=g):
                    pt2 = n2 % NPT
                    s.wait("tensor", "SC", exp_cp[n2])
                    if jc2 == 0 and g > 0:
                        s.wait("tensor", "VE", tt_cp[g - 1])
                    for it in range(2):
                        pvc = s.op("tensor", lambda e, it=it, jc=jc2, h=h,
                                   pt=pt2: e.matmul(
                                       pspv[0:65, it * 512:(it + 1) * 512],
                                       vones[:, jc * 1040 + h * 65:
                                             jc * 1040 + h * 65 + 65],
                                       pT[pt][:, it * 512:(it + 1) * 512],
                                       start=(jc == 0), stop=(jc == TC - 1)),
                                   inc="TM" if it == 1 else None)
                    pv_cp[n2] = pvc

                LAG = 3
                for jc in range(TC):
                    n = nslab
                    nslab += 1
                    sb = n % 2
                    pt = n % NPT
                    if jc == 0 and ihalf == 0:
                        for cpg, sm in dup_cp[pair]:
                            s.wait("tensor", sm, cpg)
                        if h == 0:
                            s.wait("tensor", "VE", ck["v_done"])
                    if n >= 2:
                        s.wait("tensor", "SC", exp_cp[n - 2])
                    s.op("tensor", lambda e, sb=sb, sk0=sk0, sq0=sq0, jc=jc,
                         ihalf=ihalf: e.matmul(
                             slab[sb][:, 0:512],
                             sk0[0:64, jc * 128:(jc + 1) * 128],
                             sq0[0:64, ihalf * 1024:ihalf * 1024 + 512],
                             start=True, stop=True, tile_position=(0, 0)))
                    qkt_cp[n] = s.op("tensor", lambda e, sb=sb, sk0=sk0, sq0=sq0,
                                     jc=jc, ihalf=ihalf: e.matmul(
                                         slab[sb][:, 512:1024],
                                         sk0[64:128, jc * 128:(jc + 1) * 128],
                                         sq0[64:128, ihalf * 1024 + 512:
                                             ihalf * 1024 + 1024],
                                         start=True, stop=True,
                                         tile_position=(64, 0)), inc="TM")
                    s.wait("scalar", "TM", qkt_cp[n])
                    if n >= NPT:
                        s.wait("scalar", "TM", pv_cp[n - NPT])
                    exp_cp[n] = s.op("scalar", lambda e, pt=pt, sb=sb:
                                     e.activation(pT[pt][:], slab[sb][:], AF.Exp,
                                                  bias=neg30[:]), inc="SC")
                    if jc >= LAG:
                        emit_pv(n - LAG, jc - LAG)
                for jc in range(TC - LAG, TC):
                    emit_pv(g_start + jc, jc)
                s.wait("vector", "TM", pv_cp[nslab - 1])
                ecp = s.op("vector", lambda e, g=g: e.tensor_copy(
                    ub[g % 2][0:65, :], pspv[0:65, :]), inc="VE")
                s.wait("scalar", "VE", ecp)
                lcp = s.op("scalar", lambda e, g=g: e.activation(
                    rrow[64:65, :], ub[g % 2][64:65, :], AF.Ln), inc="SC")
                s.wait("scalar", "SC", lcp)
                rcp = s.op("scalar", lambda e: e.activation(
                    rrow[64:65, :], rrow[64:65, :], AF.Exp, scale=-1.0),
                    inc="SC")
                s.wait("tensor", "SC", rcp)
                s.wait("tensor", "VE", ecp)
                for it in range(2):
                    bcp = s.op("tensor", lambda e, it=it: e.matmul(
                        pspv[0:64, it * 512:(it + 1) * 512], onesb[64:65, 0:64],
                        rrow[64:65, it * 512:(it + 1) * 512],
                        start=True, stop=True, tile_position=(64, 0)),
                        inc="TM" if it == 1 else None)
                s.wait("vector", "TM", bcp)
                s.wait("vector", "VE", ecp)
                for par in range(2):
                    src0 = ub[g % 2][0:64, :].rearrange(
                        "p (tb f) -> p f tb", f=16).rearrange(
                        "p (cc pr) tb -> p pr cc tb", pr=2)[:, par]
                    src1 = pspv[0:64, :].rearrange(
                        "p (tb f) -> p f tb", f=16).rearrange(
                        "p (cc pr) tb -> p pr cc tb", pr=2)[:, par]
                    dst = attnb[par * 64:(par + 1) * 64, :].rearrange(
                        "p (cc t) -> p cc t", t=T)[
                        :, :, h * 128 + ihalf * 64:h * 128 + ihalf * 64 + 64]
                    tt_cp[g] = s.op("vector", lambda e, o=dst, a=src0, b=src1:
                                    e.tensor_mul(o, a, b),
                                    inc="VE" if par == 1 else None)
            if eip == 1:
                pair_last_qkt[pair] = qkt_cp[nslab - 1]
    if not dbg_single:
        run_bg(100)
    ck["attn_ve"] = s.cp("VE")
    ck["attn_td"] = s.cp("TD")
    ck["attn_xd"] = s.cp("XD")
    ck["attn_sc"] = s.cp("SC")
    ck["attn_tm"] = s.cp("TM")
    if dbg_single or upto == "attn":
        return _finish_debug(nc, s, sems, dict(attnb=attnb, ub0=ub[0]))

    def load_wffn(src_h, dst, sem, wait_tm=None, wait_ve=None, wait_sc=None):
        if wait_tm:
            s.wait("gpsimd", "TM", wait_tm)
        if wait_ve:
            s.wait("gpsimd", "VE", wait_ve)
        if wait_sc:
            s.wait("gpsimd", "SC", wait_sc)
        for kc in range(C):
            inap = bass.AP(src_h.tensor, kc * 128 * D, [[D, 128], [1, 1024]])
            cpl = s.dma("gpsimd", lambda e, i=inap, kc=kc: e.dma_start(
                out=dst[:, kc * 1024:(kc + 1) * 1024], in_=i), inc=sem)
        return (cpl, sem)

    ck["wrt"] = load_wffn(wrt_h, wslotA, "W1", wait_tm=ck["attn_tm"],
                          wait_ve=ck["attn_ve"], wait_sc=ck["attn_sc"])

    def ln_feature_major(src_tensor, extra_ve_wait=None):
        s2cp = None
        for c in range(C):
            sl = slice(c * T, (c + 1) * T)
            buf = tmpc if c % 2 == 0 else xpc
            if extra_ve_wait and c == 0:
                s.wait("vector", "VE", extra_ve_wait)
            if c >= 2:
                s.wait("vector", "TM", s2cp)
            sqv = s.op("vector", lambda e, sl=sl, buf=buf: e.tensor_mul(
                buf[:], src_tensor[:, sl], src_tensor[:, sl]), inc="VE")
            s.wait("tensor", "VE", sqv)
            for i in range(4):
                s.op("tensor", lambda e, c=c, i=i: e.matmul(
                    psA[32 * i:32 * i + 1, 0:512], onesb[:, 0:1],
                    src_tensor[:, c * T + i * 512:c * T + (i + 1) * 512],
                    start=(c == 0), stop=(c == C - 1), tile_position=(0, 32 * i)))
            for i in range(4):
                s2cp = s.op("tensor", lambda e, i=i, buf=buf, c=c: e.matmul(
                    psA[32 * i:32 * i + 1, 512:1024], onesb[:, 0:1],
                    buf[:, i * 512:(i + 1) * 512],
                    start=(c == 0), stop=(c == C - 1), tile_position=(0, 32 * i)),
                    inc="TM" if i == 3 else None)
        s.wait("vector", "TM", s2cp)
        a1 = s.op("vector", lambda e: e.tensor_scalar_mul(st_a[:], psA[:, 0:512],
                                                          1.0 / D), inc="VE")
        a2 = s.op("vector", lambda e: e.tensor_scalar_mul(st_b[:],
                                                          psA[:, 512:1024],
                                                          1.0 / D), inc="VE")
        s.wait("vector", "VE", a1)
        a3 = s.op("vector", lambda e: e.tensor_mul(st_c[:], st_a[:], st_a[:]),
                  inc="VE")
        s.wait("vector", "VE", a3)
        vcp = s.op("vector", lambda e: e.tensor_sub(st_b[:], st_b[:], st_c[:]),
                   inc="VE")
        s.wait("scalar", "VE", vcp)
        for r in (0, 32, 64, 96):
            lcp = s.op("scalar", lambda e, r=r: e.activation(
                st_b[r:r + 1, :], st_b[r:r + 1, :], AF.Ln, bias=epsc[r:r + 1]),
                inc="SC" if r == 96 else None)
        s.wait("scalar", "SC", lcp)
        scp = s.op("scalar", lambda e: e.activation(sAr[:], st_b[:], AF.Exp,
                                                    scale=-0.5), inc="SC")
        s.wait("vector", "SC", scp)
        vcp2 = s.op("vector", lambda e: e.scalar_tensor_tensor(
            sBr[:], st_a[:], -1.0, sAr[:], ALU.mult, ALU.mult), inc="VE")
        s.wait("tensor", "VE", vcp2)
        for i in range(4):
            acp = s.op("tensor", lambda e, i=i: e.matmul(
                psB[:, i * 512:(i + 1) * 512], onesb[32 * i:32 * i + 1, :],
                sAr[32 * i:32 * i + 1, :], start=True, stop=True,
                tile_position=(32 * i, 0)), inc="TM" if i == 3 else None)
        s.wait("vector", "TM", acp)
        va = s.op("vector", lambda e: e.tensor_copy(lnA[:], psB[:]), inc="VE")
        s.wait("tensor", "VE", va)
        for i in range(4):
            bcp = s.op("tensor", lambda e, i=i: e.matmul(
                psB[:, i * 512:(i + 1) * 512], onesb[32 * i:32 * i + 1, :],
                sBr[32 * i:32 * i + 1, :], start=True, stop=True,
                tile_position=(32 * i, 0)), inc="TM" if i == 3 else None)
        s.wait("vector", "TM", bcp)
        return s.op("vector", lambda e: e.tensor_copy(lnB[:], psB[:]), inc="VE")

    s.wait("vector", "VE", ck["attn_ve"])
    s.wait("vector", "TD", ck["attn_td"])
    s.wait("vector", "XD", ck["attn_xd"])
    s.wait("tensor", "SC", ck["attn_sc"])
    ln2 = ln_feature_major(attnb)
    s.wait("vector", "VE", ln2)
    bmcp = s.op("vector", lambda e: e.tensor_add(lnBM[:], lnB[:], meanb[:]),
                inc="VE")
    s.wait("vector", "VE", bmcp)
    for c in range(C):
        sl = slice(c * T, (c + 1) * T)
        p1 = s.op("vector", lambda e, sl=sl: e.tensor_mul(
            attnb[:, sl], attnb[:, sl], lnA[:]), inc="VE")
    s.wait("vector", "VE", p1)
    for c in range(C):
        sl = slice(c * T, (c + 1) * T)
        p2 = s.op("vector", lambda e, sl=sl: e.tensor_add(
            attnb[:, sl], attnb[:, sl], lnBM[:]), inc="VE")
    for c in range(C):
        sl = slice(c * T, (c + 1) * T)
        p3 = s.op("vector", lambda e, sl=sl: e.tensor_mul(
            xmain[:, sl], xmain[:, sl], stdb[:]), inc="VE")
    s.wait("vector", "VE", p3)
    for c in range(C):
        sl = slice(c * T, (c + 1) * T)
        x2cp = s.op("vector", lambda e, sl=sl: e.tensor_add(
            xmain[:, sl], xmain[:, sl], attnb[:, sl]), inc="VE")
    ck["x2"] = x2cp
    if upto == "x2":
        return _finish_debug(nc, s, sems, dict(xmain=xmain))

    ln3 = ln_feature_major(xmain, extra_ve_wait=ck["x2"])
    s.wait("vector", "VE", ln3)
    for c in range(C):
        sl = slice(c * T, (c + 1) * T)
        l3p = s.op("vector", lambda e, sl=sl: e.tensor_mul(
            attnb[:, sl], xmain[:, sl], lnA[:]), inc="VE")
    s.wait("vector", "VE", l3p)
    for c in range(C):
        sl = slice(c * T, (c + 1) * T)
        xn3cp = s.op("vector", lambda e, sl=sl: e.tensor_add(
            attnb[:, sl], attnb[:, sl], lnB[:]), inc="VE")
    ck["xn3"] = xn3cp
    if upto == "xn3":
        return _finish_debug(nc, s, sems, dict(attnb=attnb))

    ck["wkt"] = load_wffn(wkt_h, wslotB, "W2", wait_ve=ck["xn3"])

    def emit_mix(base, dst, wait_ve_first):
        mcps = []
        tcps = {}
        for c in range(C):
            sl = slice(c * T, (c + 1) * T)
            mc = base + c
            buf = tmpc2 if c % 2 == 0 else xpc
            if c == 0:
                s.wait("vector", "VE", wait_ve_first)
                s.op("vector", lambda e: e.memset(tmpc2[:, 0:1], 0.0))
                s.op("vector", lambda e: e.memset(xpc[:, 0:1], 0.0))
            if c >= 2:
                s.wait("vector", "VE", mcps[c - 2])
            tcps[c] = s.op("vector", lambda e, mc=mc, c=c, buf=buf:
                           e.tensor_scalar(
                               buf[:, 1:T], attnb[:, c * T:c * T + T - 1],
                               mu1m[:, mc:mc + 1], None, ALU.mult), inc="VE")
            s.wait("vector", "VE", tcps[c])
            mcps.append(s.op("vector", lambda e, sl=sl, mc=mc, dst=dst,
                             buf=buf: e.scalar_tensor_tensor(
                                 dst[:, sl], attnb[:, sl], mu_s[:, mc:mc + 1],
                                 buf[:], ALU.mult, ALU.add), inc="VE"))
        return mcps[-1]

    ck["xr"] = emit_mix(0, xr_t, ck["xn3"])

    def ffn_mm(wslot, src, dc, wait_lg):
        ps = psA if dc % 2 == 0 else psB
        s.wait("tensor", wait_lg[1], wait_lg[0])
        for nt in range(4):
            for kc in range(C):
                cpt = s.op("tensor", lambda e, ps=ps, nt=nt, kc=kc, dc=dc,
                           wslot=wslot, src=src: e.matmul(
                               ps[:, nt * 512:(nt + 1) * 512],
                               wslot[:, kc * 1024 + dc * 128:
                                     kc * 1024 + (dc + 1) * 128],
                               src[:, kc * T + nt * 512:kc * T + (nt + 1) * 512],
                               start=(kc == 0), stop=(kc == C - 1)),
                           inc="TM" if (nt == 3 and kc == C - 1) else None)
        return cpt, ps

    rexp_cp, rmm_cp = {}, {}
    for dc in range(C):
        if dc == 0:
            s.wait("tensor", "VE", ck["xr"])
        if dc >= 2:
            s.wait("tensor", "SC", rexp_cp[dc - 2])
        rmm_cp[dc], ps = ffn_mm(wslotA, xr_t, dc, ck["wrt"])
        s.wait("scalar", "TM", rmm_cp[dc])
        rexp_cp[dc] = s.op("scalar", lambda e, ps=ps, dc=dc: e.activation(
            rsig_t[:, dc * T:(dc + 1) * T], ps[:], AF.Sigmoid), inc="SC")
    ck["rsig"] = rexp_cp[C - 1]
    ck["rmm_last"] = rmm_cp[C - 1]
    if upto == "rsig":
        return _finish_debug(nc, s, sems, dict(rsig=rsig_t, xr=xr_t))

    ck["wvt"] = load_wffn(wvt_h, wslotA, "W1", wait_tm=ck["rmm_last"])

    ck["xk"] = emit_mix(8, attnb, ck["xr"])

    kact_cp = {}
    krl_cp = {}
    for dc in range(C):
        if dc == 0:
            s.wait("tensor", "VE", ck["xk"])
            s.wait("tensor", "SC", rexp_cp[C - 1])
        if dc >= 2:
            s.wait("tensor", "VE", kact_cp[dc - 2])
        kcp, ps = ffn_mm(wslotB, attnb, dc, ck["wkt"])
        kbuf = tmpc if dc % 2 == 0 else tmpc2
        s.wait("scalar", "TM", kcp)
        if dc >= 2:
            s.wait("scalar", "VE", kact_cp[dc - 2])
        krl_cp[dc] = s.op("scalar", lambda e, ps=ps, kbuf=kbuf: e.activation(
            kbuf[:], ps[:], AF.Relu), inc="SC")
        s.wait("vector", "SC", krl_cp[dc])
        kact_cp[dc] = s.op("vector", lambda e, dc=dc, kbuf=kbuf: e.tensor_mul(
            xr_t[:, dc * T:(dc + 1) * T], kbuf[:], kbuf[:]), inc="VE")
    ck["k2"] = kact_cp[C - 1]
    if upto == "k2":
        return _finish_debug(nc, s, sems, dict(k2=xr_t))

    fout_cp = {}
    for dc in range(C):
        if dc == 0:
            s.wait("tensor", "VE", ck["k2"])
            s.wait("tensor", "SC", ck["rsig"])
        if dc >= 2:
            s.wait("tensor", "VE", fout_cp[dc - 2])
        vcp, ps = ffn_mm(wslotA, xr_t, dc, ck["wvt"])
        s.wait("vector", "TM", vcp)
        fout_cp[dc] = s.op("vector", lambda e, ps=ps, dc=dc: e.tensor_mul(
            attnb[:, dc * T:(dc + 1) * T], ps[:],
            rsig_t[:, dc * T:(dc + 1) * T]), inc="VE")
    ck["ffnout"] = fout_cp[C - 1]
    if upto == "ffnout":
        return _finish_debug(nc, s, sems, dict(ffnout=attnb))

    ln4 = ln_feature_major(attnb, extra_ve_wait=ck["ffnout"])
    s.wait("vector", "VE", ln4)
    for c in range(C):
        sl = slice(c * T, (c + 1) * T)
        l4p = s.op("vector", lambda e, sl=sl: e.tensor_mul(
            attnb[:, sl], attnb[:, sl], lnA[:]), inc="VE")
    s.wait("vector", "VE", l4p)
    for c in range(C):
        sl = slice(c * T, (c + 1) * T)
        lncp4 = s.op("vector", lambda e, sl=sl: e.tensor_add(
            attnb[:, sl], attnb[:, sl], lnB[:]), inc="VE")
    outst = []
    outdma = []
    stcp = None
    for c in range(C):
        sl = slice(c * T, (c + 1) * T)
        ps = psA if c % 2 == 0 else psB
        ot = otstage if c % 2 == 0 else otstage2
        odeng = "sync" if c % 2 == 0 else "scalar"
        odsem = "OD" if c % 2 == 0 else "OD2"
        if c == 0:
            s.wait("vector", "VE", lncp4)
        else:
            s.wait("vector", "TM", prev_trcp)
        ocp = s.op("vector", lambda e, sl=sl: e.tensor_add(
            ostage[:], xmain[:, sl], attnb[:, sl]), inc="VE")
        s.wait("tensor", "VE", ocp)
        if c >= 2:
            s.wait("tensor", "VE", outst[c - 2])
        for t in range(TC):
            trcp = s.op("tensor", lambda e, ps=ps, t=t: e.transpose(
                ps[:, t * 128:(t + 1) * 128], ostage[:, t * 128:(t + 1) * 128],
                identf[:]), inc="TM" if t == TC - 1 else None)
        prev_trcp = trcp
        s.wait("vector", "TM", trcp)
        if c >= 2:
            s.wait("vector", odsem, outdma[c - 2])
        vst = s.op("vector", lambda e, ps=ps, ot=ot: e.tensor_copy(
            ot[:], ps[:]), inc="VE")
        outst.append(vst)
        s.wait(odeng, "VE", vst)
        outap = bass.AP(out_h.tensor, c * 128,
                        [[D, 128], [128 * D, 16], [1, 128]])
        stcp = s.dma(odeng, lambda e, o=outap, ot=ot: e.dma_start(
            out=o, in_=ot[:]), inc=odsem)
        outdma.append(stcp)
    s.wait("gpsimd", "OD", s.cp("OD"))
    s.wait("gpsimd", "OD2", s.cp("OD2"))
    s.wait("sync", "OD", s.cp("OD"))
    s.wait("sync", "OD2", s.cp("OD2"))

    with nc.allow_low_precision("bf16 intermediates; fp32 psum accumulation"), \
         nc.Block() as blk:
        blk.tensor(s.run("tensor"))
        blk.scalar(s.run("scalar"))
        blk.vector(s.run("vector"))
        blk.sync(s.run("sync"))
        blk.gpsimd(s.run("gpsimd"))
    return nc


_NC_CACHE = {}


def _get_program():
    if "nc" not in _NC_CACHE:
        _NC_CACHE["nc"] = build_program()
    return _NC_CACHE["nc"]


def kernel(L0_attn_x, L0_attn_W_qkv, L0_attn_b_qkv, L0_ffn_W_r, L0_ffn_W_k,
           L0_ffn_W_v, L0_ffn_mu_r, L0_ffn_mu_k):
    x = np.asarray(L0_attn_x, dtype=np.float32)
    wqkv = np.ascontiguousarray(np.asarray(L0_attn_W_qkv, dtype=np.float32))
    bqkv = np.asarray(L0_attn_b_qkv, dtype=np.float32)
    bqk = np.ascontiguousarray(bqkv[:2048].reshape(16, 128).T)
    bv = np.ascontiguousarray(bqkv[2048:].reshape(1, 1024))
    wrt = np.ascontiguousarray(np.asarray(L0_ffn_W_r, dtype=np.float32).T)
    wkt = np.ascontiguousarray(np.asarray(L0_ffn_W_k, dtype=np.float32).T)
    wvt = np.ascontiguousarray(np.asarray(L0_ffn_W_v, dtype=np.float32).T)
    mu_r = np.asarray(L0_ffn_mu_r, dtype=np.float32)
    mu_k = np.asarray(L0_ffn_mu_k, dtype=np.float32)
    mu = np.ascontiguousarray(
        np.concatenate([mu_r.reshape(8, 128).T, mu_k.reshape(8, 128).T], axis=1))

    nc = _get_program()
    in_maps = []
    for b in range(8):
        in_maps.append({
            "x": np.ascontiguousarray(x[b]),
            "wqkv": wqkv, "bqk": bqk, "bv": bv,
            "wrt": wrt, "wkt": wkt, "wvt": wvt, "mu": mu,
        })
    res = run_bass_kernel_spmd(nc, in_maps, core_ids=list(range(8)))
    out = np.stack([res.results[b]["out"] for b in range(8)], axis=0)
    return out.astype(np.float32)
